# version 04
# baseline (speedup 1.0000x reference)
import re
import numpy as np
import ml_dtypes
import concourse.bass as bass
import concourse.mybir as mybir
from concourse.bass_utils import run_bass_kernel_spmd
from contextlib import ExitStack

F32 = mybir.dt.float32
BF16 = mybir.dt.bfloat16
I32 = mybir.dt.int32
U8 = mybir.dt.uint8
AF = mybir.ActivationFunctionType
ALU = mybir.AluOpType
AXX = mybir.AxisListType.X

P = 128
DM = 1024
NCH = DM // P
TOK = 4096
HALO = 128
RFULL = TOK + HALO
EPS = 1e-5

C_RNE = float(1.5 * 2.0 ** 23)
SH_HI = int(0x1FFFFF - 0x3F800000)
MASK_HI = int(np.uint32(0xFFC00000)) - (1 << 32)
ADD_BACK = 0x3F800000
ABS_MASK = 0x7FFFFFFF
SGN_MASK = int(np.uint32(0x80000000)) - (1 << 32)
ONE_BITS = 0x3F800000

L0I = 2816
L1H = 4096

ENGINES = ("sync", "gpsimd", "vector", "scalar", "tensor")


class Chain:

    def __init__(self, nc, ctxstack, tag):
        self.nc = nc
        self.q = {e: [] for e in ENGINES}
        self.sem = {}
        self.cnt = {}
        self.e = ctxstack.enter_context
        for e in ENGINES:
            self.sem[e] = self.e(nc.semaphore(f"{tag}_{e}"))
            self.cnt[e] = 0
        self.tag = tag
        self.dsem = {}
        self.dcnt = {}
        self.region = 0
        self.slotmap = {}
        self.region_nslots = {}

    def _waits(self, eng, deps):
        best = {}
        for tok in deps:
            if tok is None:
                continue
            sem, val = tok
            if val > best.get(sem, -1):
                best[sem] = val
        return list(best.items())

    def step(self, eng, emitfn, deps=()):
        waits = self._waits(eng, deps)
        self.cnt[eng] += 1
        val = self.cnt[eng]
        sem = self.sem[eng]

        def run(e, waits=waits, emitfn=emitfn, sem=sem):
            for s, v in waits:
                e.wait_ge(s, v)
            last = emitfn(self.nc)
            last.then_inc(sem, 1)
        self.q[eng].append(run)
        return (sem, val)

    def canon(self, key):
        rk = (self.region, key)
        if rk not in self.slotmap:
            idx = self.region_nslots.get(self.region, 0)
            self.region_nslots[self.region] = idx + 1
            self.slotmap[rk] = f"s{idx}"
        return self.slotmap[rk]

    def dma(self, eng, out, in_, deps=(), key=None, **kw):
        waits = self._waits(eng, deps)
        key = self.canon(key or out.name)
        if key not in self.dsem:
            self.dsem[key] = self.e(self.nc.semaphore(f"{self.tag}_d_{key}"))
            self.dcnt[key] = 0
        self.dcnt[key] += 16
        val = self.dcnt[key]
        sem = self.dsem[key]

        def run(e, waits=waits, out=out, in_=in_, sem=sem, kw=kw):
            for s, v in waits:
                e.wait_ge(s, v)
            e.dma_start(out=out, in_=in_, **kw).then_inc(sem, 16)
        self.q[eng].append(run)
        return (sem, val)

    def barrier_token(self, eng):
        return (self.sem[eng], self.cnt[eng])

    def wait_only(self, eng, deps):
        waits = self._waits(eng, deps)

        def run(e, waits=waits):
            for s, v in waits:
                e.wait_ge(s, v)
        self.q[eng].append(run)

    def fence(self):
        toks = [(self.sem[e], self.cnt[e]) for e in ENGINES]
        toks += [(self.dsem[k], self.dcnt[k]) for k in self.dsem]
        for e in ENGINES:
            waits = [(sm, v) for sm, v in toks
                     if v > 0 and sm is not self.sem[e]]

            def run(eng, waits=waits):
                for sm, v in waits:
                    eng.wait_ge(sm, v)
            self.q[e].append(run)
        self.region += 1

    def flush(self, block):
        for ename in ENGINES:
            if not self.q[ename]:
                continue
            lst = self.q[ename]

            def body(eng, lst=lst):
                for fn in lst:
                    fn(eng)
            getattr(block, ename)(body)
        self.q = {e: [] for e in ENGINES}


class Bufs:
    def __init__(self, nc, ctxstack):
        self.nc = nc
        self.e = ctxstack.enter_context
        self.map = {}
        self._psum_cms = {}
        self._scopes = []

    def push_scope(self):
        self._scopes.append([])

    def pop_scope(self):
        names = self._scopes.pop()
        for name in reversed(names):
            cm = self._psum_cms.pop(name)
            cm.__exit__(None, None, None)
            self.map.pop(name, None)

    def sbuf(self, name, shape, dtype):
        if name not in self.map:
            if self._scopes:
                cm = self.nc.sbuf_tensor(name, shape, dtype)
                t = cm.__enter__()
                self._psum_cms[name] = cm
                self._scopes[-1].append(name)
                self.map[name] = t
            else:
                self.map[name] = self.e(
                    self.nc.sbuf_tensor(name, shape, dtype))
        return self.map[name]

    def sbuf_tmp(self, name, shape, dtype):
        cm = self.nc.sbuf_tensor(name, shape, dtype)
        t = cm.__enter__()
        self._psum_cms[name] = cm
        if self._scopes:
            self._scopes[-1].append(name)
        self.map[name] = t
        return t

    def psum(self, name, shape, dtype=F32):
        cm = self.nc.psum_tensor(name, shape, dtype)
        t = cm.__enter__()
        self._psum_cms[name] = cm
        if self._scopes:
            self._scopes[-1].append(name)
        self.map[name] = t
        return t

    def release(self, *names):
        for name in names:
            cm = self._psum_cms.pop(name)
            cm.__exit__(None, None, None)
            self.map.pop(name, None)
            for sc in self._scopes:
                if name in sc:
                    sc.remove(name)



def emit_recip(ch, out, in_, tmp, n, deps):
    t = ch.step("vector", lambda nc: nc.vector.tensor_scalar(
        out=out[:, :n].bitcast(I32), in0=in_[:, :n].bitcast(I32),
        scalar1=-1, scalar2=0x7EF311C3, op0=ALU.mult, op1=ALU.add), deps=deps)
    for _ in range(3):
        t = ch.step("vector", lambda nc: nc.vector.tensor_tensor(
            tmp[:, :n], in_[:, :n], out[:, :n], ALU.mult), deps=[t])
        t = ch.step("vector", lambda nc: nc.vector.tensor_scalar(
            out=tmp[:, :n], in0=tmp[:, :n], scalar1=-1.0, scalar2=2.0,
            op0=ALU.mult, op1=ALU.add), deps=[t])
        t = ch.step("vector", lambda nc: nc.vector.tensor_tensor(
            out[:, :n], out[:, :n], tmp[:, :n], ALU.mult), deps=[t])
    return t


def quant_scratch(bufs, tag, F, tmp=False):
    al = bufs.sbuf_tmp if tmp else bufs.sbuf
    return dict(
        F=F,
        amax=al(f"{tag}_amax", [P, F // 16], F32),
        s=al(f"{tag}_s", [P, F // 16], F32),
        xn=al(f"{tag}_xn", [P, F], F32),
        a=al(f"{tag}_a", [P, F], F32),
        ql=al(f"{tag}_ql", [P, F], F32),
        qh=al(f"{tag}_qh", [P, F], F32),
        mask=al(f"{tag}_mask", [P, F], U8),
        sgn=al(f"{tag}_sgn", [P, F], I32),
        qs=al(f"{tag}_qs", [P, F], F32),
    )


def emit_quant(ch, q, xin, out_bf, F, deps):
    B = F // 16
    t1 = ch.step("vector", lambda nc: nc.vector.tensor_reduce(
        q["amax"][:, :B].unsqueeze(2),
        xin[:, :F].rearrange("p (b s) -> p b s", s=16),
        AXX, ALU.max, apply_absolute_value=True), deps=deps)

    ts_ = ch.step("vector", lambda nc: nc.vector.tensor_scalar(
        out=q["s"][:, :B], in0=q["amax"][:, :B], scalar1=1.0 / 6.0,
        scalar2=1e-12, op0=ALU.mult, op1=ALU.max), deps=[t1])
    t2 = emit_recip(ch, q["amax"], q["s"], q["ql"], B, [ts_])
    return ch.step("vector",
                   lambda nc: emit_quant_group(nc, q, xin, out_bf, F),
                   deps=list(deps) + [t2])


def emit_quant_group(nc, q, xin, out_bf, F):
    B = F // 16
    v = nc.vector
    v.tensor_tensor(q["xn"][:, :F].rearrange("p (b s) -> p b s", s=16),
                    xin[:, :F].rearrange("p (b s) -> p b s", s=16),
                    q["amax"][:, :B].unsqueeze(2).broadcast_to([P, B, 16]),
                    ALU.mult)
    v.tensor_single_scalar(q["a"][:, :F].bitcast(I32),
                           q["xn"][:, :F].bitcast(I32),
                           ABS_MASK, ALU.bitwise_and)
    v.tensor_scalar(out=q["ql"][:, :F], in0=q["a"][:, :F],
                    scalar1=2.0, scalar2=C_RNE, op0=ALU.mult, op1=ALU.add)
    v.tensor_scalar(out=q["ql"][:, :F], in0=q["ql"][:, :F],
                    scalar1=C_RNE, scalar2=0.5, op0=ALU.subtract, op1=ALU.mult)
    v.tensor_single_scalar(q["qh"][:, :F].bitcast(I32),
                           q["a"][:, :F].bitcast(I32), SH_HI, ALU.add)
    v.tensor_single_scalar(q["qh"][:, :F].bitcast(I32),
                           q["qh"][:, :F].bitcast(I32), MASK_HI,
                           ALU.bitwise_and)
    v.tensor_single_scalar(q["qh"][:, :F].bitcast(I32),
                           q["qh"][:, :F].bitcast(I32), ADD_BACK, ALU.add)
    v.tensor_single_scalar(q["mask"][:, :F], q["a"][:, :F].bitcast(I32),
                           ONE_BITS, ALU.is_lt)
    v.select(q["qs"][:, :F], q["mask"][:, :F], q["ql"][:, :F], q["qh"][:, :F])
    v.tensor_single_scalar(q["sgn"][:, :F], q["xn"][:, :F].bitcast(I32),
                           SGN_MASK, ALU.bitwise_and)
    v.tensor_tensor(q["qs"][:, :F].bitcast(I32), q["qs"][:, :F].bitcast(I32),
                    q["sgn"][:, :F], ALU.bitwise_or)
    return v.tensor_tensor(
        out_bf[:, :F].rearrange("p (b s) -> p b s", s=16),
        q["qs"][:, :F].rearrange("p (b s) -> p b s", s=16),
        q["s"][:, :B].unsqueeze(2).broadcast_to([P, B, 16]), ALU.mult)



def emit_weight_prep(ch, bufs, src, dst, nr, ncol, tag, quant=False,
                     transpose=True, deps=()):
    nc = ch.nc
    wrow = [bufs.sbuf_tmp(f"{tag}_wrow{i}", [P, ncol], F32) for i in range(2)]
    qbf = bufs.sbuf_tmp(f"{tag}_qbf", [P, ncol], BF16)
    q = quant_scratch(bufs, f"{tag}_q", ncol, tmp=True) if quant else None
    if transpose:
        tev = bufs.sbuf_tmp(f"{tag}_tev", [P, ncol], BF16)
        ident = bufs.sbuf("sb_ident", [P, P], BF16)
        pst = bufs.psum(f"{tag}_pst", [P, P], BF16)
    row_free = [None, None]
    qbf_free = None
    tev_free = None
    pst_free = None
    last_store = None
    for rt in range(nr // P):
        wb = wrow[rt % 2]
        dl = ch.dma("sync", wb[:], src[rt * P:(rt + 1) * P, :],
                    deps=[row_free[rt % 2]] + list(deps))
        if quant:
            tq = emit_quant(ch, q, wb, qbf, ncol, [dl, qbf_free])
        else:
            tq = ch.step("vector", lambda nc, wb=wb: nc.vector.tensor_copy(
                qbf[:], wb[:]), deps=[dl, qbf_free])
        row_free[rt % 2] = tq
        if not transpose:
            qbf_free = ch.dma("gpsimd", dst[rt * P:(rt + 1) * P, :], qbf[:],
                              deps=[tq], key=dst.name)
            last_store = qbf_free
            continue
        evs = []
        for ct in range(ncol // P):
            tt = ch.step("tensor", lambda nc, ct=ct: nc.tensor.transpose(
                pst[:], qbf[:, ct * P:(ct + 1) * P], ident[:]),
                deps=[tq, pst_free, tev_free])
            ev = ch.step("vector", lambda nc, ct=ct: nc.vector.tensor_copy(
                tev[:, ct * P:(ct + 1) * P], pst[:]), deps=[tt])
            pst_free = ev
            evs.append(ev)
        qbf_free = evs[-1]
        tev_free = ch.dma(
            "gpsimd",
            bass.AP(dst.tensor, rt * P, [[nr, P], [P * nr, ncol // P], [1, P]]),
            tev[:].rearrange("p (c j) -> p c j", j=P),
            deps=[evs[-1]], key=dst.name)
        last_store = tev_free
    names = [f"{tag}_wrow0", f"{tag}_wrow1", f"{tag}_qbf"]
    if quant:
        names += [f"{tag}_q_{k}" for k in
                  ("amax", "s", "xn", "a", "ql", "qh", "mask", "sgn", "qs")]
    if transpose:
        names += [f"{tag}_tev", f"{tag}_pst"]
    bufs.release(*reversed(names))
    return last_store



def ln_scratch(bufs, tag):
    return dict(
        mneg=bufs.sbuf(f"{tag}_mneg", [P, 1], F32),
        ss=bufs.sbuf(f"{tag}_ss", [P, 1], F32),
        var=bufs.sbuf(f"{tag}_var", [P, 1], F32),
        sd=bufs.sbuf(f"{tag}_sd", [P, 1], F32),
        rstd=bufs.sbuf(f"{tag}_rstd", [P, 1], F32),
        junk=bufs.sbuf(f"{tag}_junk", [P, DM], F32),
    )


def emit_ln(ch, ln, xt, xln, dep_x, width=DM):
    nc = ch.nc
    W = width

    t0 = ch.step("vector", lambda nc: nc.vector.tensor_reduce(
        ln["mneg"][:], xt[:, :W], AXX, ALU.add),
        deps=[dep_x, ln.get("_prev_s1")])
    t1 = ch.step("vector", lambda nc: nc.vector.tensor_single_scalar(
        ln["mneg"][:], ln["mneg"][:], -1.0 / W, ALU.mult), deps=[t0])

    def s1(nc):
        return nc.scalar.activation(out=ln["junk"][:, :W], in_=xt[:, :W],
                                    func=AF.Square, bias=ln["mneg"][:],
                                    scale=1.0, accum_out=ln["ss"][:])
    t2 = ch.step("scalar", s1, deps=[dep_x, t1])
    ln["_prev_s1"] = t2

    def v2(nc):
        return nc.vector.tensor_scalar(out=ln["var"][:], in0=ln["ss"][:],
                                       scalar1=1.0 / W, scalar2=EPS,
                                       op0=ALU.mult, op1=ALU.add)
    t3 = ch.step("vector", v2, deps=[t2])

    def s2(nc):
        return nc.scalar.activation(out=ln["sd"][:], in_=ln["var"][:],
                                    func=AF.Sqrt)
    t4 = ch.step("scalar", s2, deps=[t3])

    t4b = emit_recip(ch, ln["rstd"], ln["sd"], ln["var"], 1, [t4])
    t5 = ch.step("vector", lambda nc: nc.vector.tensor_scalar(
        out=xln[:, :W], in0=xt[:, :W], scalar1=ln["mneg"][:],
        scalar2=ln["rstd"][:], op0=ALU.add, op1=ALU.mult),
        deps=[t4b, dep_x])
    return t5



def emit_l0ffn_phase(ch, bufs, R, x_in, x_out, wgT, wuT, wdT, xq_d, hq_d, tag,
                     scale=0.5):
    nc = ch.nc
    ntile = R // P
    ln = ln_scratch(bufs, f"{tag}ln")
    qx = quant_scratch(bufs, f"{tag}qx", DM)
    qh = quant_scratch(bufs, f"{tag}qh", 512)
    ident = bufs.sbuf("sb_ident", [P, P], BF16)

    xt = [bufs.sbuf(f"{tag}_xt{i}", [P, DM], F32) for i in range(2)]
    xlnt = bufs.sbuf(f"{tag}_xln", [P, DM], F32)
    xqbf = bufs.sbuf(f"{tag}_xqbf", [P, DM], BF16)
    tev = bufs.sbuf(f"{tag}_tev", [P, DM], BF16)
    pst = bufs.psum(f"{tag}_pst", [P, P], BF16)

    xt_free = [None, None]
    tev_free = None
    pst_free = None
    xqbf_free = None
    tok_xq_done = None
    for i in range(ntile):
        xb = xt[i % 2]
        dx = ch.dma("sync", xb[:], x_in[i * P:(i + 1) * P, :],
                    deps=[xt_free[i % 2]])
        tl = emit_ln(ch, ln, xb, xlnt, dx)
        xt_free[i % 2] = tl
        tq = emit_quant(ch, qx, xlnt, xqbf, DM, [tl, xqbf_free])
        evs = []
        for c in range(NCH):
            tt = ch.step("tensor",
                         lambda nc, c=c: nc.tensor.transpose(
                             pst[:], xqbf[:, c * P:(c + 1) * P], ident[:]),
                         deps=[tq, pst_free, tev_free])
            ev = ch.step("vector",
                         lambda nc, c=c: nc.vector.tensor_copy(
                             tev[:, c * P:(c + 1) * P], pst[:]),
                         deps=[tt])
            pst_free = ev
            evs.append(ev)
        xqbf_free = evs[-1]
        tev_free = ch.dma(
            "gpsimd",
            bass.AP(xq_d.tensor, i * P, [[R, P], [P * R, NCH], [1, P]]),
            tev[:].rearrange("p (c j) -> p c j", j=P),
            deps=[evs[-1]])
        tok_xq_done = tev_free

    wbuf = [dict(g=bufs.sbuf(f"{tag}_wg{i}", [P, NCH * 512], BF16),
                 u=bufs.sbuf(f"{tag}_wu{i}", [P, NCH * 512], BF16))
            for i in range(2)]
    xqb = [bufs.sbuf(f"{tag}_xqb{i}", [P, NCH * 512], BF16) for i in range(2)]
    psg = bufs.psum(f"{tag}_psg", [P, 512], F32)
    psu = bufs.psum(f"{tag}_psu", [P, 512], F32)
    hsil = bufs.sbuf(f"{tag}_hsil", [P, 512], F32)
    htm = bufs.sbuf(f"{tag}_htm", [P, 512], F32)
    hqbf = bufs.sbuf(f"{tag}_hqbf", [P, 512], BF16)
    htev = bufs.sbuf(f"{tag}_htev", [P, 22 * 128], BF16)

    mstarts = list(range(0, L0I, 512))
    tblocks = [(s, min(512, R - s)) for s in range(0, R, 512)]

    wb_free = [None, None]
    xq_free = [None, None]
    htev_free = None
    psgu_free = None
    hqbf_free = None
    tok_hq_done = None
    last_mm = None

    for mi, ms in enumerate(mstarts):
        mw = min(512, L0I - ms)
        nchk = (mw + P - 1) // P
        wb = wbuf[mi % 2]
        twg = ch.dma("sync", wb["g"][:, :NCH * mw].rearrange(
            "p (c j) -> p c j", j=mw),
            bass.AP(wgT.tensor, ms, [[L0I, P], [P * L0I, NCH], [1, mw]]),
            deps=[wb_free[mi % 2]])
        twu = ch.dma("sync", wb["u"][:, :NCH * mw].rearrange(
            "p (c j) -> p c j", j=mw),
            bass.AP(wuT.tensor, ms, [[L0I, P], [P * L0I, NCH], [1, mw]]),
            deps=[wb_free[mi % 2]])
        for ti, (ts_, tw) in enumerate(tblocks):
            xb = xqb[ti % 2]
            tx = ch.dma("sync", xb[:, :NCH * tw].rearrange(
                "p (c j) -> p c j", j=tw),
                bass.AP(xq_d.tensor, ts_, [[R, P], [P * R, NCH], [1, tw]]),
                deps=[xq_free[ti % 2], tok_xq_done])
            ntt = (tw + P - 1) // P
            for tj in range(ntt):
                tjw = min(P, tw - tj * P)

                def mm(nc, xb=xb, wb=wb, tj=tj, tjw=tjw, mw=mw, tw=tw):
                    i_ = None
                    for c in range(NCH):
                        nc.tensor.matmul(
                            psg[:tjw, :mw],
                            xb[:, c * tw + tj * P: c * tw + tj * P + tjw],
                            wb["g"][:, c * mw:(c + 1) * mw],
                            start=(c == 0), stop=(c == NCH - 1))
                    for c in range(NCH):
                        i_ = nc.tensor.matmul(
                            psu[:tjw, :mw],
                            xb[:, c * tw + tj * P: c * tw + tj * P + tjw],
                            wb["u"][:, c * mw:(c + 1) * mw],
                            start=(c == 0), stop=(c == NCH - 1))
                    return i_
                tm = ch.step("tensor", mm, deps=[tx, twg, twu, psgu_free])
                tsil = ch.step("scalar", lambda nc, tjw=tjw, mw=mw:
                               nc.scalar.activation(out=hsil[:tjw, :mw],
                                                    in_=psg[:tjw, :mw],
                                                    func=AF.Silu),
                               deps=[tm])
                tmul = ch.step("vector", lambda nc, tjw=tjw, mw=mw:
                               nc.vector.tensor_tensor(
                                   htm[:tjw, :mw], psu[:tjw, :mw],
                                   hsil[:tjw, :mw], ALU.mult),
                               deps=[tsil, tm])
                psgu_free = tmul
                tq2 = emit_quant(ch, qh, htm, hqbf, mw,
                                 [tmul, hqbf_free])
                evs = []
                for c in range(nchk):
                    cw = min(P, mw - c * P)
                    th = ch.step("tensor",
                                 lambda nc, c=c, cw=cw, tjw=tjw:
                                 nc.tensor.transpose(
                                     pst[:cw, :tjw],
                                     hqbf[:tjw, c * P:c * P + cw], ident[:]),
                                 deps=[tq2, pst_free, htev_free])
                    ev = ch.step("vector",
                                 lambda nc, c=c, cw=cw, tj=tj, tjw=tjw, tw=tw:
                                 nc.vector.tensor_copy(
                                     htev[:cw, c * tw + tj * P:
                                          c * tw + tj * P + tjw],
                                     pst[:cw, :tjw]),
                                 deps=[th])
                    pst_free = ev
                    evs.append(ev)
                hqbf_free = evs[-1]
            htev_free = ch.dma(
                "gpsimd",
                bass.AP(hq_d.tensor, ms * R + ts_,
                        [[R, P], [P * R, nchk], [1, tw]]),
                htev[:, :nchk * tw].rearrange("p (c t) -> p c t", t=tw),
                deps=[hqbf_free])
            tok_hq_done = htev_free
            xq_free[ti % 2] = tm
        wb_free[mi % 2] = last_mm = tm

    bufs.release(f"{tag}_psu", f"{tag}_psg", f"{tag}_pst")
    wdb = [bufs.sbuf(f"{tag}_wd{i}", [P, DM], BF16) for i in range(2)]
    hqb = [bufs.sbuf(f"{tag}_hqb{i}", [P, 512], BF16) for i in range(2)]
    psd = [bufs.psum(f"{tag}_psd{i}", [P, 512], F32) for i in range(8)]
    xrt = bufs.sbuf(f"{tag}_xrt", [P, DM], F32)
    n_hf = L0I // P

    hqb_free = [None, None]
    wdb_free = [None, None]
    psd_free = None
    xrt_free = None
    for ti, (ts_, tw) in enumerate(tblocks):
        ntt = (tw + P - 1) // P
        mm_toks = []
        for hf in range(n_hf):
            hb = hqb[hf % 2]
            wb2 = wdb[hf % 2]
            th = ch.dma("sync", hb[:, :tw],
                        bass.AP(hq_d.tensor, hf * P * R + ts_,
                                [[R, P], [1, tw]]),
                        deps=[hqb_free[hf % 2], tok_hq_done])
            tw_ = ch.dma("sync", wb2[:],
                         bass.AP(wdT.tensor, hf * P * DM, [[DM, P], [1, DM]]),
                         deps=[wdb_free[hf % 2]])

            def mmd(nc, hb=hb, wb2=wb2, hf=hf, ntt=ntt, tw=tw):
                i_ = None
                for tj in range(ntt):
                    tjw = min(P, tw - tj * P)
                    for nh in range(2):
                        i_ = nc.tensor.matmul(
                            psd[tj * 2 + nh][:tjw, :],
                            hb[:, tj * P:tj * P + tjw],
                            wb2[:, nh * 512:(nh + 1) * 512],
                            start=(hf == 0), stop=(hf == n_hf - 1))
                return i_
            dd = [th, tw_]
            if hf == 0:
                dd.append(psd_free)
            tmm = ch.step("tensor", mmd, deps=dd)
            hqb_free[hf % 2] = tmm
            wdb_free[hf % 2] = tmm
            mm_toks.append(tmm)
        last_adds = []
        for tj in range(ntt):
            tjw = min(P, tw - tj * P)
            row0 = ts_ + tj * P
            dxr = ch.dma("sync", xrt[:tjw, :], x_in[row0:row0 + tjw, :],
                         deps=[mm_toks[-1], xrt_free])
            tadds = []
            for nh in range(2):
                ta = ch.step("vector",
                             lambda nc, tj=tj, nh=nh, tjw=tjw:
                             nc.vector.scalar_tensor_tensor(
                                 out=xrt[:tjw, nh * 512:(nh + 1) * 512],
                                 in0=psd[tj * 2 + nh][:tjw, :],
                                 scalar=scale,
                                 in1=xrt[:tjw, nh * 512:(nh + 1) * 512],
                                 op0=ALU.mult, op1=ALU.add),
                             deps=[dxr, mm_toks[-1]])
                tadds.append(ta)
            xrt_free = ch.dma("gpsimd", x_out[row0:row0 + tjw, :],
                              xrt[:tjw, :], deps=tadds)
            last_adds = tadds
        psd_free = last_adds[-1]


def finish(ch):
    toks = [(ch.dsem[k], ch.dcnt[k]) for k in ch.dsem]
    ch.wait_only("sync", toks)



def build_phaseA(R, debug_scratch=False):
    nc = bass.Bass("TRN2", target_bir_lowering=False,
                   detect_race_conditions=False)
    x_in = nc.dram_tensor("x_in", [R, DM], F32, kind="ExternalInput").ap()
    wgT = nc.dram_tensor("wgT", [DM, L0I], BF16, kind="ExternalInput").ap()
    wuT = nc.dram_tensor("wuT", [DM, L0I], BF16, kind="ExternalInput").ap()
    wdT = nc.dram_tensor("wdT", [L0I, DM], BF16, kind="ExternalInput").ap()
    c_ident = nc.dram_tensor("c_ident", [P, P], BF16, kind="ExternalInput").ap()
    x_out = nc.dram_tensor("x_out", [R, DM], F32, kind="ExternalOutput").ap()
    kind = "ExternalOutput" if debug_scratch else "Internal"
    xq_d = nc.dram_tensor("xq_scr", [DM, R], BF16, kind=kind).ap()
    hq_d = nc.dram_tensor("hq_scr", [L0I, R], BF16, kind=kind).ap()

    with ExitStack() as ctx:
        bufs = Bufs(nc, ctx)
        ch = Chain(nc, ctx, "A")
        ident = bufs.sbuf("sb_ident", [P, P], BF16)
        block = ctx.enter_context(nc.Block())
        ti = ch.dma("sync", ident[:], c_ident)
        ch.wait_only("tensor", [ti])
        emit_l0ffn_phase(ch, bufs, R, x_in, x_out, wgT, wuT, wdT,
                         xq_d, hq_d, "A")
        finish(ch)
        ch.flush(block)
    return nc



DELTA = 32
NH = 16
BD = 16


def emit_l0attn_prep(ch, bufs, aps, tag="Bp"):
    nc = ch.nc
    out = {}
    ident = bufs.sbuf("sb_ident", [P, P], BF16)
    identf = bufs.sbuf("sb_identf", [P, P], F32)
    toks = []
    t0 = ch.dma("sync", ident[:], aps["c_ident"])
    t0b = ch.dma("sync", identf[:], aps["c_identf"])
    toks += [t0, t0b]
    wdisp = bufs.sbuf(f"{tag}_wdisp", [P, NCH * 256], BF16)
    wval = bufs.sbuf(f"{tag}_wval", [P, NCH * DM], BF16)
    wcp = bufs.sbuf(f"{tag}_wcp", [P, NCH * DM], BF16)
    for (nm, src, ncol, dstbuf) in (
            ("wdisp", aps["w_disp"], 256, wdisp),
            ("wval", aps["w_val"], DM, wval),
            ("wcproj", aps["w_cproj"], DM, wcp)):
        scr = ch.nc.dram_tensor(f"{tag}_{nm}_scr", [DM, ncol], BF16).ap()
        tp = emit_weight_prep(ch, bufs, src, scr, DM, ncol, f"{tag}_{nm}",
                              quant=False, transpose=False, deps=[])
        tl = ch.dma("sync", dstbuf[:, :NCH * ncol].rearrange(
            "p (c j) -> p c j", j=ncol),
            bass.AP(scr.tensor, 0, [[ncol, P], [P * ncol, NCH], [1, ncol]]),
            deps=[tp])
        toks.append(tl)
    out["wdisp"], out["wval"], out["wcp"] = wdisp, wval, wcp
    for nm, shape in (("b_disp2", [P, 2]), ("b_val_bcast", [P, DM]),
                      ("bbo_t", [P, 1]),
                      ("b_cproj8", [P, NCH]), ("b_state_t", [P, 1]),
                      ("b_damage_t", [P, 1]), ("cmask_t0", [P, NH * DELTA]),
                      ("invcnt_t0", [P, 640]), ("halo_mask", [P, P]),
                      ("rep8", [BD, P])):
        out[nm] = bufs.sbuf(f"{tag}_{nm}", shape, F32)
        toks.append(ch.dma("sync", out[nm][:], aps[nm]))
    wsm = bufs.sbuf(f"{tag}_wsmall", [BD, 6 * BD], F32)
    wsmb = bufs.sbuf(f"{tag}_wsmallb", [BD, 6 * BD], BF16)
    small = [("w_state", BD), ("w_strain", BD), ("w_state_i", BD),
             ("w_state_j", BD), ("w_damage", BD), ("w_pos", BD)]
    tload = []
    for i, (nm, w_) in enumerate(small):
        tload.append(ch.dma("sync", wsm[:, i * BD:(i + 1) * BD], aps[nm]))
    tc = ch.step("vector", lambda nc: nc.vector.tensor_copy(wsmb[:], wsm[:]),
                 deps=tload)
    bd_names = ("bd_state", "bd_strain", "bd_si", "bd_sj", "bd_damage")
    for j, bnm in enumerate(bd_names):
        bt = bufs.sbuf(f"{tag}_{bnm}", [P, P], BF16)
        out[bnm] = bt
        tm_ = ch.step("vector", lambda nc, bt=bt: nc.vector.memset(bt[:], 0.0),
                      deps=[])
        prev = tm_
        for i in range(8):
            prev = ch.dma("gpsimd", bt[i * BD:(i + 1) * BD, i * BD:(i + 1) * BD],
                          wsmb[:, j * BD:(j + 1) * BD], deps=[tc, tm_],
                          key=f"{tag}_{bnm}")
        toks.append(prev)
    wvo = bufs.sbuf(f"{tag}_wvo", [BD, 2], F32)
    wvob = bufs.sbuf(f"{tag}_wvob", [BD, 2], BF16)
    tv1 = ch.dma("sync", wvo[:, 0:1], aps["w_bond_out"])
    tv2 = ch.dma("sync", wvo[:, 1:2], aps["w_damage_out"])
    tvc = ch.step("vector", lambda nc: nc.vector.tensor_copy(wvob[:], wvo[:]),
                  deps=[tv1, tv2])
    for j, bnm in enumerate(("bv_bond", "bv_damage")):
        bt = bufs.sbuf(f"{tag}_{bnm}", [P, 8], BF16)
        out[bnm] = bt
        tm_ = ch.step("vector", lambda nc, bt=bt: nc.vector.memset(bt[:], 0.0),
                      deps=[])
        prev = tm_
        for i in range(8):
            prev = ch.dma("gpsimd", bt[i * BD:(i + 1) * BD, i:i + 1],
                          wvob[:, j:j + 1], deps=[tvc, tm_],
                          key=f"{tag}_{bnm}")
        toks.append(prev)
    rel = bufs.sbuf(f"{tag}_rel", [DELTA, BD], F32)
    bstr16 = bufs.sbuf(f"{tag}_bstr16", [BD, 1], F32)
    relT_s = bufs.sbuf(f"{tag}_relT", [BD, DELTA], F32)
    relW_s = bufs.sbuf(f"{tag}_relW", [BD, DELTA], F32)
    bias_bond = bufs.sbuf(f"{tag}_bias_bond", [P, DELTA], F32)
    out["bias_bond"] = bias_bond
    bdo1 = bufs.sbuf(f"{tag}_bdo1", [1, 1], F32)
    ones_r = bufs.sbuf(f"{tag}_ones_r", [1, P], F32)
    bdo = bufs.sbuf(f"{tag}_bdo", [P, 1], F32)
    out["bdo"] = bdo
    ps_small = bufs.psum(f"{tag}_ps_small", [P, DELTA], F32)
    tr1 = ch.dma("sync", rel[:], aps["rel_pos_emb"])
    tr2 = ch.dma("sync", bstr16[:], aps["b_strain16"])
    tt1 = ch.step("tensor", lambda nc: nc.tensor.transpose(
        ps_small[:BD, :DELTA], rel[:], identf[:DELTA, :DELTA]),
        deps=[tr1, t0b])
    te1 = ch.step("vector", lambda nc: nc.vector.tensor_copy(
        relT_s[:], ps_small[:BD, :DELTA]), deps=[tt1])
    tt2 = ch.step("tensor", lambda nc: nc.tensor.matmul(
        ps_small[:BD, :DELTA], wsm[:, 5 * BD:6 * BD], relT_s[:],
        start=True, stop=True), deps=[te1, tc])
    te2 = ch.step("vector", lambda nc: nc.vector.tensor_scalar(
        out=relW_s[:], in0=ps_small[:BD, :DELTA], scalar1=bstr16[:],
        scalar2=None, op0=ALU.add), deps=[tt2, tr2])
    tt3 = ch.step("tensor", lambda nc: nc.tensor.matmul(
        ps_small[:, :DELTA], out["rep8"][:], relW_s[:],
        start=True, stop=True), deps=[te2])
    te3 = ch.step("vector", lambda nc: nc.vector.tensor_copy(
        bias_bond[:], ps_small[:, :DELTA]), deps=[tt3])
    toks.append(te3)
    td1 = ch.dma("sync", bdo1[:], aps["b_damage_out"])
    td2 = ch.dma("sync", ones_r[:], aps["ones_row"])
    tt4 = ch.step("tensor", lambda nc: nc.tensor.matmul(
        ps_small[:, 0:1], ones_r[:], bdo1[:], start=True, stop=True),
        deps=[td1, td2, te3])
    te4 = ch.step("vector", lambda nc: nc.vector.tensor_copy(
        bdo[:], ps_small[:, 0:1]), deps=[tt4])
    toks.append(te4)
    bufs.release(f"{tag}_ps_small")
    out["ident"] = ident
    out["identf"] = identf
    out["tok"] = toks
    return out


def emit_l0attn_phase(ch, bufs, x_in, x_out, pr, tag="B"):
    nc = ch.nc
    ident = pr["ident"]
    ln = ln_scratch(bufs, f"{tag}ln")
    TB = 512
    W = TB + P
    NXT = 5

    xt = [bufs.sbuf(f"{tag}_xt{i}", [P, DM], F32) for i in range(2)]
    xlnt = bufs.sbuf(f"{tag}_xln", [P, DM], F32)
    xlnb = bufs.sbuf(f"{tag}_xlnb", [P, DM], BF16)
    xfm = bufs.sbuf(f"{tag}_xfm", [P, NCH * W], BF16)
    dispf = bufs.sbuf(f"{tag}_dispf", [P, 2 * W], BF16)
    valt = bufs.sbuf(f"{tag}_valt", [P, NXT * DM], BF16)
    dW = bufs.sbuf(f"{tag}_dW", [P, 2 * W], F32)
    dWs = bufs.sbuf(f"{tag}_dWs", [P, 2 * W], F32)
    dWd = bufs.sbuf(f"{tag}_dWd", [P, 2 * W], F32)
    ss1 = bufs.sbuf(f"{tag}_ss1", [P, 2 * 576], F32)
    ss2 = bufs.sbuf(f"{tag}_ss2", [P, 2 * 576], F32)
    stateb = bufs.sbuf(f"{tag}_stateb", [P, 2 * 544], BF16)
    U = bufs.sbuf(f"{tag}_U", [P, 2 * 544], F32)
    C = bufs.sbuf(f"{tag}_C", [P, 2 * TB], F32)
    pre = bufs.sbuf(f"{tag}_pre", [P, 2 * TB], F32)
    bondb = bufs.sbuf(f"{tag}_bondb", [P, 2 * TB], BF16)
    damb = bufs.sbuf(f"{tag}_damb", [P, 2 * TB], BF16)
    damf = bufs.sbuf(f"{tag}_damf", [P, NH], F32)
    logits = bufs.sbuf(f"{tag}_logits", [P, 4 * NH * DELTA], F32)
    lmax = bufs.sbuf(f"{tag}_lmax", [P, NH], F32)
    lsum = bufs.sbuf(f"{tag}_lsum", [P, NH], F32)
    lrec = bufs.sbuf(f"{tag}_lrec", [P, NH], F32)
    ltmp = bufs.sbuf(f"{tag}_ltmp", [P, NH], F32)
    wbf = bufs.sbuf(f"{tag}_wbf", [P, NH * DELTA], BF16)
    sksb = bufs.sbuf(f"{tag}_sksb", [P, NH * 256], BF16)
    b1sb = bufs.sbuf(f"{tag}_b1sb", [P, P], BF16)
    b2sb = bufs.sbuf(f"{tag}_b2sb", [P, P], BF16)
    ofm = bufs.sbuf(f"{tag}_ofm", [P, NCH * TB], BF16)
    yfm = bufs.sbuf(f"{tag}_yfm", [P, NCH * TB], BF16)
    xrt = bufs.sbuf(f"{tag}_xrt", [P, DM], F32)
    zrow = bufs.sbuf(f"{tag}_zrow", [P, 257], BF16)

    ps_mm = bufs.psum(f"{tag}_psmm", [P, 512], F32)
    ps_b = bufs.psum(f"{tag}_psb", [P, NH], F32)
    ps_d = bufs.psum(f"{tag}_psd", [P, NH], F32)
    ps_av = bufs.psum(f"{tag}_psav", [P, P], F32)
    pst = bufs.psum(f"{tag}_pst", [P, P], BF16)

    NZC = NH + 1
    skew = nc.dram_tensor(f"{tag}_skew", [NZC * P * 257], BF16).ap()

    tz0 = ch.step("vector", lambda nc: nc.vector.memset(zrow[:], 0.0),
                  deps=pr["tok"])
    tz = ch.dma("gpsimd",
                bass.AP(skew.tensor, 0,
                        [[257, P], [P * 257, NZC], [1, 257]]),
                bass.AP(zrow, 0, [[257, P], [0, NZC], [1, 257]]),
                deps=[tz0], key="skewz")

    prev = dict()

    for bt in range(TOK // TB):
        r0 = bt * TB
        for xi in range(NXT):
            xb = xt[xi % 2]
            dx = ch.dma("sync", xb[:], x_in[r0 + xi * P: r0 + (xi + 1) * P, :],
                        deps=[prev.get(("xt", xi % 2))] + (pr["tok"] if bt == 0 else []))
            tl = emit_ln(ch, ln, xb, xlnt, dx)
            prev[("xt", xi % 2)] = tl
            tcst = ch.step("vector", lambda nc: nc.vector.tensor_copy(
                xlnb[:], xlnt[:]), deps=[tl, prev.get("xlnb")])
            evs = []
            for c in range(NCH):
                tt = ch.step("tensor", lambda nc, c=c: nc.tensor.transpose(
                    pst[:], xlnb[:, c * P:(c + 1) * P], ident[:]),
                    deps=[tcst, prev.get("pst")])
                ev = ch.step("vector",
                             lambda nc, c=c, xi=xi: nc.vector.tensor_copy(
                                 xfm[:, c * W + xi * P: c * W + (xi + 1) * P],
                                 pst[:]), deps=[tt, prev.get("xfm_free")])
                prev["pst"] = ev
                evs.append(ev)
            prev["xlnb"] = evs[-1]
            prev[("xfm", xi)] = evs[-1]
            for vc in range(2):
                def vmm(nc, xi=xi, vc=vc):
                    i_ = None
                    for c in range(NCH):
                        i_ = nc.tensor.matmul(
                            ps_mm[:, :512],
                            xfm[:, c * W + xi * P: c * W + (xi + 1) * P],
                            pr["wval"][:, c * DM + vc * 512:
                                       c * DM + (vc + 1) * 512],
                            start=(c == 0), stop=(c == NCH - 1))
                    return i_
                tv = ch.step("tensor", vmm,
                             deps=evs + [prev.get("ps_mm")])
                ev2 = ch.step("vector", lambda nc, xi=xi, vc=vc:
                              nc.vector.tensor_tensor(
                                  valt[:, xi * DM + vc * 512:
                                       xi * DM + (vc + 1) * 512],
                                  ps_mm[:, :512],
                                  pr["b_val_bcast"][:, vc * 512:(vc + 1) * 512],
                                  ALU.add),
                              deps=[tv, prev.get(("valt", xi))])
                prev["ps_mm"] = ev2
            prev[("valt", xi)] = prev["ps_mm"]
        for mc in range(2):
            for (s0, sw) in ((0, 512), (512, P)):
                def dmm(nc, mc=mc, s0=s0, sw=sw):
                    i_ = None
                    for c in range(NCH):
                        i_ = nc.tensor.matmul(
                            ps_mm[:, :sw],
                            pr["wdisp"][:, c * 256 + mc * P:
                                        c * 256 + (mc + 1) * P],
                            xfm[:, c * W + s0: c * W + s0 + sw],
                            start=(c == 0), stop=(c == NCH - 1))
                    return i_
                tdm = ch.step("tensor", dmm,
                              deps=[prev[("xfm", 4)], prev.get("ps_mm")])
                prev["xfm_free"] = tdm
                edm = ch.step("vector", lambda nc, mc=mc, s0=s0, sw=sw:
                              nc.vector.tensor_scalar(
                                  out=dispf[:, mc * W + s0: mc * W + s0 + sw],
                                  in0=ps_mm[:, :sw],
                                  scalar1=pr["b_disp2"][:, mc:mc + 1],
                                  scalar2=None, op0=ALU.add),
                              deps=[tdm, prev.get("dispf_free")])
                prev["ps_mm"] = edm
        prev["dispf"] = prev["ps_mm"]
        for (dst, bdw) in ((dW, "bd_state"), (dWs, "bd_strain"),
                           (dWd, "bd_damage")):
            for mc in range(2):
                for (s0, sw) in ((0, 512), (512, P)):
                    def bmm(nc, bdw=bdw, mc=mc, s0=s0, sw=sw):
                        return nc.tensor.matmul(
                            ps_mm[:, :sw], pr[bdw][:],
                            dispf[:, mc * W + s0: mc * W + s0 + sw],
                            start=True, stop=True)
                    tb = ch.step("tensor", bmm,
                                 deps=[prev["dispf"], prev.get("ps_mm")])
                    prev["dispf_free"] = tb
                    eb = ch.step("scalar", lambda nc, dst=dst, mc=mc, s0=s0,
                                 sw=sw: nc.scalar.activation(
                                     out=dst[:, mc * W + s0: mc * W + s0 + sw],
                                     in_=ps_mm[:, :sw], func=AF.Copy),
                                 deps=[tb, prev.get(("dw", dst.name))])
                    prev["ps_mm"] = eb
            prev[("dw", dst.name)] = prev["ps_mm"]
        if bt == 0:
            thm = ch.step("vector", lambda nc: nc.vector.tensor_tensor(
                dW[:, 0:P], dW[:, 0:P], pr["halo_mask"][:], ALU.mult),
                deps=[prev[("dw", dW.name)]])
            thm2 = ch.step("vector", lambda nc: nc.vector.tensor_tensor(
                dW[:, W:W + P], dW[:, W:W + P], pr["halo_mask"][:], ALU.mult),
                deps=[thm])
            prev[("dw", dW.name)] = thm2
        cur, oth = ss1, ss2
        SSW = 576
        t_ = ch.step("vector", lambda nc: [
            nc.vector.tensor_copy(cur[:, mc * SSW: mc * SSW + SSW],
                                  dW[:, mc * W + 64: mc * W + 640])
            for mc in range(2)][-1],
            deps=[prev[("dw", dW.name)], prev.get("ss")])
        cur, oth = oth, cur
        for k in range(1, 6):
            sh = 1 << (k - 1)
            def sfix(nc, sh=sh, cur=cur, oth=oth):
                i_ = None
                for mc in range(2):
                    nc.vector.tensor_copy(
                        cur[:, mc * SSW: mc * SSW + sh],
                        oth[:, mc * SSW: mc * SSW + sh])
                    i_ = nc.vector.tensor_tensor(
                        cur[:, mc * SSW + sh: mc * SSW + SSW],
                        oth[:, mc * SSW + sh: mc * SSW + SSW],
                        oth[:, mc * SSW: mc * SSW + SSW - sh], ALU.add)
                return i_
            t_ = ch.step("vector", sfix, deps=[t_])
            cur, oth = oth, cur
        slide = oth
        if bt == 0:
            t_ = ch.step("vector", lambda nc, slide=slide: [
                nc.vector.tensor_tensor(
                    slide[:, mc * SSW + 32: mc * SSW + SSW],
                    slide[:, mc * SSW + 32: mc * SSW + SSW],
                    pr["invcnt_t0"][:, 96:640], ALU.mult)
                for mc in range(2)][-1], deps=[t_])
        else:
            t_ = ch.step("vector", lambda nc, slide=slide: [
                nc.vector.tensor_single_scalar(
                    slide[:, mc * SSW + 32: mc * SSW + SSW],
                    slide[:, mc * SSW + 32: mc * SSW + SSW],
                    1.0 / 32.0, ALU.mult)
                for mc in range(2)][-1], deps=[t_])
        def stfin(nc, slide=slide):
            i_ = None
            for mc in range(2):
                nc.vector.scalar_tensor_tensor(
                    out=slide[:, mc * SSW + 32: mc * SSW + SSW],
                    in0=dW[:, mc * W + 96: mc * W + 640], scalar=-1.0,
                    in1=slide[:, mc * SSW + 32: mc * SSW + SSW],
                    op0=ALU.mult, op1=ALU.add)
                i_ = nc.vector.tensor_scalar(
                    out=stateb[:, mc * 544: mc * 544 + 544],
                    in0=slide[:, mc * SSW + 32: mc * SSW + SSW],
                    scalar1=pr["b_state_t"][:], scalar2=None, op0=ALU.add)
            return i_
        t_ = ch.step("vector", stfin, deps=[t_, prev.get("stateb")])
        prev["stateb"] = t_
        prev["ss"] = t_
        prev[("dw", dW.name)] = t_
        for mc in range(2):
            for (s0, sw) in ((0, 512), (512, 32)):
                tb = ch.step("tensor", lambda nc, mc=mc, s0=s0, sw=sw:
                             nc.tensor.matmul(
                                 ps_mm[:, :sw], pr["bd_sj"][:],
                                 stateb[:, mc * 544 + s0: mc * 544 + s0 + sw],
                                 start=True, stop=True),
                             deps=[prev["stateb"], prev.get("ps_mm")])
                eb = ch.step("vector", lambda nc, mc=mc, s0=s0, sw=sw:
                             nc.vector.tensor_tensor(
                                 U[:, mc * 544 + s0: mc * 544 + s0 + sw],
                                 ps_mm[:, :sw],
                                 dWs[:, mc * W + 96 + s0:
                                     mc * W + 96 + s0 + sw], ALU.add),
                             deps=[tb, prev.get("U")])
                prev["ps_mm"] = eb
        prev["U"] = prev["ps_mm"]
        for mc in range(2):
            tb = ch.step("tensor", lambda nc, mc=mc: nc.tensor.matmul(
                ps_mm[:, :512], pr["bd_si"][:],
                stateb[:, mc * 544 + 32: mc * 544 + 544],
                start=True, stop=True),
                deps=[prev["stateb"], prev.get("ps_mm")])
            eb = ch.step("vector", lambda nc, mc=mc: nc.vector.scalar_tensor_tensor(
                out=C[:, mc * TB:(mc + 1) * TB],
                in0=dWs[:, mc * W + P: mc * W + W], scalar=-1.0,
                in1=ps_mm[:, :512], op0=ALU.mult, op1=ALU.add),
                deps=[tb, prev.get("C")])
            prev["ps_mm"] = eb
        prev["C"] = prev["ps_mm"]
        prev["stateb"] = prev["ps_mm"]
        prev[("dw", dWs.name)] = prev["ps_mm"]
        for sub in range(4):
            prev[("lg", sub)] = prev.get(("lg", sub))
        for d in range(DELTA):
            tp_ = ch.step("vector", lambda nc, d=d: [
                nc.vector.tensor_tensor(
                    pre[:, mc * TB:(mc + 1) * TB],
                    U[:, mc * 544 + 1 + d: mc * 544 + 513 + d],
                    C[:, mc * TB:(mc + 1) * TB], ALU.add)
                for mc in range(2)][-1],
                deps=[prev["U"], prev["C"], prev.get("pre")])
            tg = ch.step("scalar", lambda nc, d=d: [
                nc.scalar.activation(
                    out=bondb[:, mc * TB:(mc + 1) * TB],
                    in_=pre[:, mc * TB:(mc + 1) * TB], func=AF.Gelu,
                    bias=pr["bias_bond"][:, d:d + 1], scale=1.0)
                for mc in range(2)][-1], deps=[tp_, prev.get("bondb")])
            prev["pre"] = tg
            tdp = ch.step("vector", lambda nc, d=d: [
                nc.vector.tensor_tensor(
                    pre[:, mc * TB:(mc + 1) * TB],
                    dWd[:, mc * W + 97 + d: mc * W + 97 + d + 512],
                    dWd[:, mc * W + P: mc * W + W], ALU.subtract)
                for mc in range(2)][-1],
                deps=[tg, prev[("dw", dWd.name)], prev.get("pre")])
            tgd = ch.step("scalar", lambda nc: [
                nc.scalar.activation(
                    out=damb[:, mc * TB:(mc + 1) * TB],
                    in_=pre[:, mc * TB:(mc + 1) * TB], func=AF.Gelu,
                    bias=pr["b_damage_t"][:], scale=1.0)
                for mc in range(2)][-1], deps=[tdp, prev.get("damb")])
            prev["pre"] = tgd
            for sub in range(4):
                def pmm(nc, sub=sub):
                    for mc in range(2):
                        nc.tensor.matmul(
                            ps_b[:, mc * 8:(mc + 1) * 8],
                            bondb[:, mc * TB + sub * P: mc * TB + (sub + 1) * P],
                            pr["bv_bond"][:], start=True, stop=True)
                    i_ = None
                    for mc in range(2):
                        i_ = nc.tensor.matmul(
                            ps_d[:, mc * 8:(mc + 1) * 8],
                            damb[:, mc * TB + sub * P: mc * TB + (sub + 1) * P],
                            pr["bv_damage"][:], start=True, stop=True)
                    return i_
                tpm = ch.step("tensor", pmm,
                              deps=[tg, tgd, prev.get("ps_bd")])
                tsg = ch.step("scalar", lambda nc: nc.scalar.activation(
                    out=damf[:], in_=ps_d[:, :NH], func=AF.Sigmoid,
                    bias=pr["bdo"][:], scale=1.0),
                    deps=[tpm, prev.get("damf")])
                tlg = ch.step("vector", lambda nc, sub=sub, d=d:
                              nc.vector.scalar_tensor_tensor(
                                  out=bass.AP(logits,
                                          sub * NH * DELTA + d,
                                          [[4 * NH * DELTA, P], [DELTA, NH],
                                           [1, 1]]),
                                  in0=damf[:], scalar=-10.0,
                                  in1=ps_b[:, :NH], op0=ALU.mult, op1=ALU.add),
                              deps=[tsg, tpm, prev.get(("lg", sub))])
                prev["damf"] = tlg
                prev["ps_bd"] = tlg
                prev[("lg", sub)] = tlg
            prev["bondb"] = tpm
            prev["damb"] = tpm
            if d == DELTA - 1:
                prev["U"] = tp_
                prev["C"] = tp_
                prev[("dw", dWd.name)] = tdp
        for sub in range(4):
            lg = logits[:, sub * NH * DELTA:(sub + 1) * NH * DELTA]
            t_ = prev[("lg", sub)]
            if bt == 0 and sub == 0:
                t_ = ch.step("vector", lambda nc, lg=lg: nc.vector.tensor_tensor(
                    lg, lg, pr["cmask_t0"][:], ALU.add), deps=[t_])
            tm1 = ch.step("vector", lambda nc, lg=lg: nc.vector.tensor_reduce(
                lmax[:].unsqueeze(2), lg.rearrange("p (h d) -> p h d", d=DELTA),
                AXX, ALU.max), deps=[t_, prev.get("lmax")])
            tm1 = ch.step("vector", lambda nc: nc.vector.tensor_scalar(
                out=lmax[:], in0=lmax[:], scalar1=pr["bbo_t"][:],
                scalar2=None, op0=ALU.subtract), deps=[tm1])
            tm2 = ch.step("vector", lambda nc, lg=lg: nc.vector.tensor_tensor(
                lg.rearrange("p (h d) -> p h d", d=DELTA),
                lg.rearrange("p (h d) -> p h d", d=DELTA),
                lmax[:].unsqueeze(2).broadcast_to([P, NH, DELTA]),
                ALU.subtract), deps=[tm1])
            tm3 = ch.step("scalar", lambda nc, lg=lg: nc.scalar.activation(
                out=lg, in_=lg, func=AF.Exp), deps=[tm2])
            tm4 = ch.step("vector", lambda nc, lg=lg: nc.vector.tensor_reduce(
                lsum[:].unsqueeze(2), lg.rearrange("p (h d) -> p h d", d=DELTA),
                AXX, ALU.add), deps=[tm3, prev.get("lsum")])
            tm5 = emit_recip(ch, lrec, lsum, ltmp, NH, [tm4])
            tm6 = ch.step("vector", lambda nc, lg=lg: nc.vector.tensor_tensor(
                wbf[:].rearrange("p (h d) -> p h d", d=DELTA),
                lg.rearrange("p (h d) -> p h d", d=DELTA),
                lrec[:].unsqueeze(2).broadcast_to([P, NH, DELTA]),
                ALU.mult), deps=[tm5, prev.get("wbf")])
            prev["lmax"] = tm6
            prev["lsum"] = tm6
            prev[("lg", sub)] = tm6
            tw_ = ch.dma("gpsimd",
                         bass.AP(skew.tensor, 97,
                                 [[257, P], [P * 257, NH], [1, DELTA]]),
                         wbf[:].rearrange("p (h d) -> p h d", d=DELTA),
                         deps=[tm6, prev.get("skr", tz)], key="skew")
            trd = ch.dma("sync", sksb[:].rearrange("p (h c) -> p h c", c=256),
                         bass.AP(skew.tensor, 0,
                                 [[256, P], [P * 257, NH], [1, 256]]),
                         deps=[tw_, prev.get("sksb")], key="sksb")
            prev["wbf"] = tw_
            prev["skr"] = trd
            oevs = []
            for h in range(NH):
                tb1 = ch.step("tensor", lambda nc, h=h: nc.tensor.transpose(
                    pst[:], sksb[:, h * 256: h * 256 + P], ident[:]),
                    deps=[trd, prev.get("pst")])
                te1 = ch.step("vector", lambda nc: nc.vector.tensor_copy(
                    b1sb[:], pst[:]), deps=[tb1, prev.get("b1sb")])
                tb2 = ch.step("tensor", lambda nc, h=h: nc.tensor.transpose(
                    pst[:], sksb[:, h * 256 + P: h * 256 + 256], ident[:]),
                    deps=[te1])
                te2 = ch.step("vector", lambda nc: nc.vector.tensor_copy(
                    b2sb[:], pst[:]), deps=[tb2, prev.get("b2sb")])
                prev["pst"] = te2
                def avmm(nc, h=h, sub=sub):
                    nc.tensor.matmul(ps_av[:64, :],
                                     valt[:, sub * DM + h * 64:
                                          sub * DM + (h + 1) * 64],
                                     b1sb[:], start=True, stop=False)
                    return nc.tensor.matmul(
                        ps_av[:64, :],
                        valt[:, (sub + 1) * DM + h * 64:
                             (sub + 1) * DM + (h + 1) * 64],
                        b2sb[:], start=False, stop=True)
                tav = ch.step("tensor", avmm,
                              deps=[te1, te2, prev.get("ps_av"),
                                    prev.get(("valt", sub)),
                                    prev.get(("valt", sub + 1))])
                prev["b1sb"] = tav
                prev["b2sb"] = tav
                oev = ch.step("vector", lambda nc, h=h, sub=sub:
                              nc.vector.tensor_copy(
                                  ofm[(h % 2) * 64:(h % 2) * 64 + 64,
                                      (h // 2) * TB + sub * P:
                                      (h // 2) * TB + (sub + 1) * P],
                                  ps_av[:64, :]),
                              deps=[tav, prev.get(("ofm", h // 2))])
                prev["ps_av"] = oev
                prev[("ofm", h // 2)] = oev
                oevs.append(oev)
            prev["sksb"] = oevs[-1]
        for xi in range(NXT):
            prev[("valt", xi)] = prev["sksb"]
        ofm_toks = [prev[("ofm", c)] for c in range(NCH)]
        ecm_toks = []
        for ncx in range(NCH):
            def cmm(nc, ncx=ncx):
                i_ = None
                for c in range(NCH):
                    i_ = nc.tensor.matmul(
                        ps_mm[:, :512], pr["wcp"][:, c * DM + ncx * P:
                                                  c * DM + (ncx + 1) * P],
                        ofm[:, c * TB:(c + 1) * TB],
                        start=(c == 0), stop=(c == NCH - 1))
                return i_
            tcm = ch.step("tensor", cmm, deps=ofm_toks + [prev.get("ps_mm")])
            ecm = ch.step("vector", lambda nc, ncx=ncx: nc.vector.tensor_scalar(
                out=yfm[:, ncx * TB:(ncx + 1) * TB], in0=ps_mm[:, :512],
                scalar1=pr["b_cproj8"][:, ncx:ncx + 1], scalar2=None,
                op0=ALU.add), deps=[tcm, prev.get(("yfm", ncx))])
            prev["ps_mm"] = ecm
            ecm_toks.append(ecm)
        for c in range(NCH):
            prev[("ofm", c)] = ecm_toks[-1]
        for sub in range(4):
            dxr = ch.dma("sync", xrt[:],
                         x_in[P + r0 + sub * P: P + r0 + (sub + 1) * P, :],
                         deps=[prev.get("xrt_free")], key="xrt")
            tlast = None
            for ncx in range(NCH):
                ty = ch.step("tensor", lambda nc, ncx=ncx, sub=sub:
                             nc.tensor.transpose(
                                 pst[:], yfm[:, ncx * TB + sub * P:
                                             ncx * TB + (sub + 1) * P],
                                 ident[:]),
                             deps=[ecm_toks[ncx], prev.get("pst")])
                ta = ch.step("vector", lambda nc, ncx=ncx:
                             nc.vector.tensor_tensor(
                                 bass.AP(xrt, ncx * P, [[DM, P], [1, P]]),
                                 bass.AP(xrt, ncx * P, [[DM, P], [1, P]]),
                                 pst[:], ALU.add),
                             deps=[ty, dxr, tlast])
                prev["pst"] = ta
                tlast = ta
                if sub == 3:
                    prev[("yfm", ncx)] = ty
            prev["xrt_free"] = ch.dma(
                "gpsimd", x_out[r0 + sub * P: r0 + (sub + 1) * P, :],
                xrt[:], deps=[tlast], key="x_out")



def _np_fp4(x, block=16):
    shape = x.shape
    xb = x.reshape(shape[:-1] + (shape[-1] // block, block)).astype(np.float32)
    scale = np.maximum(np.abs(xb).max(-1, keepdims=True) / 6.0,
                       np.float32(1e-12)).astype(np.float32)
    xn = (xb / scale).astype(np.float32)
    GRID = np.array([0., .5, 1., 1.5, 2., 3., 4., 6.], np.float32)
    MID = np.array([.25, .75, 1.25, 1.75, 2.5, 3.5, 5.], np.float32)
    idx = np.searchsorted(MID, np.abs(xn))
    q = (GRID[idx] * np.sign(xn) * scale).astype(np.float32)
    return q.reshape(shape)



def emit_l1ffn_phase(ch, bufs, x_in, x_out, w1T, w2T, h1T, h2T, w3T,
                     sc, h0_d, hA_d, hB_d, tag="D"):
    nc = ch.nc
    ident = bufs.sbuf("sb_ident", [P, P], BF16)
    ln = ln_scratch(bufs, f"{tag}ln")
    HALF = TOK // 2
    NT = HALF // P
    NHC = L1H // P

    xt = [bufs.sbuf(f"{tag}_xt{i}", [P, DM], F32) for i in range(2)]
    xlnt = bufs.sbuf(f"{tag}_xln", [P, DM], F32)
    xlnb = bufs.sbuf(f"{tag}_xlnb", [P, DM], BF16)
    hsb = bufs.sbuf(f"{tag}_hsb", [P, NHC * HALF], BF16)
    xfm = hsb[:, :NCH * HALF]
    wa = [bufs.sbuf(f"{tag}_wa{i}", [P, NHC * P], BF16) for i in range(2)]
    wb = [bufs.sbuf(f"{tag}_wb{i}", [P, NCH * P], BF16) for i in range(2)]
    hsil = bufs.sbuf(f"{tag}_hsil", [P, 512], F32)
    ev1 = bufs.sbuf(f"{tag}_ev1", [P, 512], BF16)
    ybf = bufs.sbuf(f"{tag}_ybf", [P, NCH * 512], BF16)
    xrt = bufs.sbuf(f"{tag}_xrt", [P, DM], F32)
    psg = bufs.psum(f"{tag}_psg", [P, 512], F32)
    psu = bufs.psum(f"{tag}_psu", [P, 512], F32)
    pst = bufs.psum(f"{tag}_pst", [P, P], BF16)

    prev = {}

    def ln_half(half, deps):
        last = None
        for xi in range(NT):
            r0 = half * HALF + xi * P
            xb = xt[xi % 2]
            dx = ch.dma("sync", xb[:], x_in[r0:r0 + P, :],
                        deps=[prev.get(("xt", xi % 2))] + deps)
            tl = emit_ln(ch, ln, xb, xlnt, dx)
            prev[("xt", xi % 2)] = tl
            tc = ch.step("vector", lambda nc: nc.vector.tensor_copy(
                xlnb[:], xlnt[:]), deps=[tl, prev.get("xlnb")])
            for c in range(NCH):
                tt = ch.step("tensor", lambda nc, c=c: nc.tensor.transpose(
                    pst[:], xlnb[:, c * P:(c + 1) * P], ident[:]),
                    deps=[tc, prev.get("pst")])
                evx = ch.step("vector",
                              lambda nc, c=c, xi=xi: nc.vector.tensor_copy(
                                  xfm[:, c * HALF + xi * P:
                                      c * HALF + (xi + 1) * P], pst[:]),
                              deps=[tt, prev.get("xfm")])
                prev["pst"] = evx
                last = evx
            prev["xlnb"] = last
        return last

    for half in range(2):
        tln = ln_half(half, [prev.get("xfm")])
        for m in range(NHC):
            wla = ch.dma("sync", wa[m % 2][:, :NCH * P].rearrange(
                "p (c j) -> p c j", j=P),
                bass.AP(w1T.tensor, m * P, [[L1H, P], [P * L1H, NCH], [1, P]]),
                deps=[prev.get(("wa", m % 2))])
            wlb = ch.dma("sync", wb[m % 2][:, :NCH * P].rearrange(
                "p (c j) -> p c j", j=P),
                bass.AP(w2T.tensor, m * P, [[L1H, P], [P * L1H, NCH], [1, P]]),
                deps=[prev.get(("wb", m % 2))])
            for t in range(4):
                def mm(nc, m=m, t=t):
                    i_ = None
                    for c in range(NCH):
                        nc.tensor.matmul(
                            psg[:], wa[m % 2][:, c * P:(c + 1) * P],
                            xfm[:, c * HALF + t * 512: c * HALF + (t + 1) * 512],
                            start=(c == 0), stop=(c == NCH - 1))
                    for c in range(NCH):
                        i_ = nc.tensor.matmul(
                            psu[:], wb[m % 2][:, c * P:(c + 1) * P],
                            xfm[:, c * HALF + t * 512: c * HALF + (t + 1) * 512],
                            start=(c == 0), stop=(c == NCH - 1))
                    return i_
                tm = ch.step("tensor", mm,
                             deps=[tln, wla, wlb, prev.get("psgu")])
                tsl = ch.step("scalar", lambda nc, m=m: nc.scalar.activation(
                    out=hsil[:], in_=psg[:], func=AF.Silu,
                    scale=sc["s1"][:, m:m + 1]), deps=[tm, prev.get("hsil")])
                tmu = ch.step("vector", lambda nc, m=m:
                              nc.vector.scalar_tensor_tensor(
                                  out=ev1[:], in0=psu[:],
                                  scalar=sc["s2"][:, m:m + 1], in1=hsil[:],
                                  op0=ALU.mult, op1=ALU.mult),
                              deps=[tsl, tm, prev.get("ev1")])
                prev["psgu"] = tmu
                prev["hsil"] = tmu
                st_ = ch.dma("gpsimd",
                             bass.AP(h0_d.tensor, m * P * TOK + half * HALF
                                     + t * 512, [[TOK, P], [1, 512]]),
                             ev1[:], deps=[tmu], key="h0_d")
                prev["ev1"] = st_
            prev[("wa", m % 2)] = tm
            prev[("wb", m % 2)] = tm
        prev["xfm"] = tm
    tok_h0 = prev["ev1"]

    def square_stage(src_d, dst_d, wT, s_ap, dep0, stag):
        tok_done = None
        for half in range(2):
            hl = ch.dma("sync", hsb[:].rearrange("p (c j) -> p c j", j=HALF),
                        bass.AP(src_d.tensor, half * HALF,
                                [[TOK, P], [P * TOK, NHC], [1, HALF]]),
                        deps=[dep0, prev.get("hsb")])
            for m in range(NHC):
                wl = ch.dma("sync", wa[m % 2][:, :NHC * P].rearrange(
                    "p (c j) -> p c j", j=P),
                    bass.AP(wT.tensor, m * P,
                            [[L1H, P], [P * L1H, NHC], [1, P]]),
                    deps=[prev.get(("wa", m % 2))])
                for t in range(4):
                    def mm2(nc, m=m, t=t):
                        i_ = None
                        for c in range(NHC):
                            i_ = nc.tensor.matmul(
                                psg[:], wa[m % 2][:, c * P:(c + 1) * P],
                                hsb[:, c * HALF + t * 512:
                                    c * HALF + (t + 1) * 512],
                                start=(c == 0), stop=(c == NHC - 1))
                        return i_
                    tm = ch.step("tensor", mm2,
                                 deps=[hl, wl, prev.get("psgu")])
                    te = ch.step("scalar", lambda nc, m=m:
                                 nc.scalar.activation(
                                     out=ev1[:], in_=psg[:], func=AF.Gelu,
                                     scale=s_ap[:, m:m + 1]),
                                 deps=[tm, prev.get("ev1")])
                    prev["psgu"] = te
                    st_ = ch.dma("gpsimd",
                                 bass.AP(dst_d.tensor, m * P * TOK
                                         + half * HALF + t * 512,
                                         [[TOK, P], [1, 512]]),
                                 ev1[:], deps=[te])
                    prev["ev1"] = st_
                    tok_done = st_
                prev[("wa", m % 2)] = tm
            prev["hsb"] = tm
        return tok_done

    tok_hA = square_stage(h0_d, hA_d, h1T, sc["sh1"], tok_h0, f"{tag}2")
    tok_hB = square_stage(hA_d, hB_d, h2T, sc["sh2"], tok_hA, f"{tag}3")

    for half in range(2):
        hl = ch.dma("sync", hsb[:].rearrange("p (c j) -> p c j", j=HALF),
                    bass.AP(hB_d.tensor, half * HALF,
                            [[TOK, P], [P * TOK, NHC], [1, HALF]]),
                    deps=[tok_hB, prev.get("hsb")])
        for t in range(4):
            ytoks = []
            for n in range(NCH):
                wl = ch.dma("sync", wa[n % 2][:, :NHC * P].rearrange(
                    "p (c j) -> p c j", j=P),
                    bass.AP(w3T.tensor, n * P,
                            [[DM, P], [P * DM, NHC], [1, P]]),
                    deps=[prev.get(("wa", n % 2))], key=f"{tag}4w")                     if t == 0 or True else None
                def mm3(nc, n=n, t=t):
                    i_ = None
                    for c in range(NHC):
                        i_ = nc.tensor.matmul(
                            psg[:], wa[n % 2][:, c * P:(c + 1) * P],
                            hsb[:, c * HALF + t * 512:
                                c * HALF + (t + 1) * 512],
                            start=(c == 0), stop=(c == NHC - 1))
                    return i_
                tm = ch.step("tensor", mm3, deps=[hl, wl, prev.get("psgu")])
                ty = ch.step("vector", lambda nc, n=n: nc.vector.tensor_scalar(
                    out=ybf[:, n * 512:(n + 1) * 512], in0=psg[:],
                    scalar1=sc["s3h"][:, n:n + 1], scalar2=None, op0=ALU.mult),
                    deps=[tm, prev.get(("ybf", n))])
                prev["psgu"] = ty
                prev[("wa", n % 2)] = tm
                ytoks.append(ty)
            for sub in range(4):
                r0 = half * HALF + t * 512 + sub * P
                dxr = ch.dma("sync", xrt[:], x_in[r0:r0 + P, :],
                             deps=[prev.get("xrt_free")])
                tlast = None
                for n in range(NCH):
                    tt = ch.step("tensor", lambda nc, n=n, sub=sub:
                                 nc.tensor.transpose(
                                     pst[:], ybf[:, n * 512 + sub * P:
                                                 n * 512 + (sub + 1) * P],
                                     ident[:]),
                                 deps=[ytoks[n], prev.get("pst")])
                    ta = ch.step("vector", lambda nc, n=n:
                                 nc.vector.tensor_tensor(
                                     bass.AP(xrt, n * P, [[DM, P], [1, P]]),
                                     bass.AP(xrt, n * P, [[DM, P], [1, P]]),
                                     pst[:], ALU.add),
                                 deps=[tt, dxr, tlast])
                    prev["pst"] = ta
                    tlast = ta
                    if sub == 3:
                        prev[("ybf", n)] = tt
                prev["xrt_free"] = ch.dma(
                    "gpsimd", x_out[r0:r0 + P, :], xrt[:],
                    deps=[tlast], key="xout")
        prev["hsb"] = tm



KVH = 8
HD = 64
CC_RG = [[0, 1], [2, 3], [4, 5], [6, 7]]

E_AP_NAMES = [
    ("Wq", [DM, DM], F32), ("Wk", [DM, 512], F32), ("Wv", [DM, 512], F32),
    ("Wo", [DM, DM], F32),
    ("cosq", [P, TOK], F32), ("sinq", [P, TOK], F32),
    ("cosk", [P, TOK], F32), ("sink", [P, TOK], F32),
    ("hv16", [P, 16 * NCH], BF16), ("hvk", [P, KVH * 4], BF16),
    ("sel16", [KVH, NH], BF16), ("ones1p", [1, P], F32),
    ("bo8", [P, NCH], F32),
]


def host_e_consts(inp, half):
    f = np.float32
    ROT = 32
    t = (np.arange(TOK) + half * TOK).astype(np.float64)
    inv = 1.0 / 10000.0 ** (np.arange(0, ROT, 2) / ROT)
    d = np.arange(P) % HD
    rot_mask = d < ROT
    fr = inv[(d // 2) % 16]
    ang = np.where(rot_mask[:, None], fr[:, None] * t[None, :], 0.0)
    cos = np.where(rot_mask[:, None], np.cos(ang), 1.0).astype(f)
    sin = np.where(rot_mask[:, None], np.sin(ang), 0.0).astype(f)
    SCALE = np.float32(HD ** -0.5)
    hv16 = np.zeros((P, 16 * NCH), np.float32)
    for mc in range(NCH):
        for r in range(2):
            hv16[r * HD:(r + 1) * HD, mc * 16 + 2 * mc + r] = 1.0
    hvk = np.zeros((P, KVH * 4), np.float32)
    for c4 in range(4):
        for r in range(2):
            hvk[r * HD:(r + 1) * HD, c4 * KVH + 2 * c4 + r] = 1.0
    sel16 = np.zeros((KVH, NH), np.float32)
    for h in range(NH):
        sel16[h // 2, h] = 1.0
    return dict(
        Wq=np.asarray(inp["L1_attn_Wq"], f), Wk=np.asarray(inp["L1_attn_Wk"], f),
        Wv=np.asarray(inp["L1_attn_Wv"], f), Wo=np.asarray(inp["L1_attn_Wo"], f),
        cosq=(cos * SCALE).astype(f), sinq=(sin * SCALE).astype(f),
        cosk=cos.astype(f), sink=sin.astype(f),
        hv16=hv16.astype(ml_dtypes.bfloat16),
        hvk=hvk.astype(ml_dtypes.bfloat16),
        sel16=sel16.astype(ml_dtypes.bfloat16),
        ones1p=np.ones((1, P), f),
        bo8=np.ascontiguousarray(
            np.asarray(inp["L1_attn_bo"], f).reshape(NCH, P).T),
    )


def emit_l1attn_phase(ch, bufs, x_in, x_out, aps, tag="E"):
    nc = ch.nc
    ident = bufs.sbuf("sb_ident", [P, P], BF16)
    identf = bufs.sbuf("sb_identf", [P, P], F32)
    ln = ln_scratch(bufs, f"{tag}ln")
    NSUB = TOK // 512

    q_d = nc.dram_tensor(f"{tag}_qstash", [DM, TOK], BF16).ap()
    k_in = nc.dram_tensor(f"{tag}_k_in", [512, TOK], BF16).ap()
    v_in = nc.dram_tensor(f"{tag}_v_in", [TOK, KVH * 65], BF16).ap()
    k_full = nc.dram_tensor(f"{tag}_k_full", [2 * 512, TOK], BF16).ap()
    v_full = nc.dram_tensor(f"{tag}_v_full", [2 * TOK, KVH * 65], BF16).ap()

    xt = [bufs.sbuf(f"{tag}_xt{i}", [P, DM], F32) for i in range(2)]
    xlnt = bufs.sbuf(f"{tag}_xln", [P, DM], F32)
    xlnb = bufs.sbuf(f"{tag}_xlnb", [P, DM], BF16)
    xfm = bufs.sbuf(f"{tag}_xfm", [P, NCH * 512], BF16)
    qa = bufs.sbuf(f"{tag}_qa", [P, 512], F32)
    qb_ = bufs.sbuf(f"{tag}_qb", [P, 512], F32)
    qpr = bufs.sbuf(f"{tag}_qpr", [P, 512], F32)
    sqb = bufs.sbuf(f"{tag}_sqb", [P, 512], BF16)
    qbf = bufs.sbuf(f"{tag}_qbf", [P, 512], BF16)
    nrm = bufs.sbuf(f"{tag}_nrm", [NH, NSUB], F32)
    nrmk = bufs.sbuf(f"{tag}_nrmk", [KVH, 16], F32)
    nq = bufs.sbuf(f"{tag}_nq", [NH, 1], F32)
    nk = bufs.sbuf(f"{tag}_nk", [KVH, 1], F32)
    nkb = bufs.sbuf(f"{tag}_nkb", [KVH, 1], BF16)
    nk16 = bufs.sbuf(f"{tag}_nk16", [NH, 1], F32)
    mrow = bufs.sbuf(f"{tag}_mrow", [NH, 1], F32)
    mrowT = bufs.sbuf(f"{tag}_mrowT", [1, NH], F32)
    mbias = bufs.sbuf(f"{tag}_mbias", [P, NH], F32)
    kdup = bufs.sbuf(f"{tag}_kdup", [P, TOK], BF16)
    vsb = bufs.sbuf(f"{tag}_vsb", [P, 64 * 65], BF16)
    qdup = bufs.sbuf(f"{tag}_qdup", [P, TOK], BF16)
    exps = [bufs.sbuf(f"{tag}_exps{i}", [P, 1024], BF16) for i in range(2)]
    rec = bufs.sbuf(f"{tag}_rec", [1, 512], F32)
    recr = bufs.sbuf(f"{tag}_recr", [1, 512], F32)
    rtmp = bufs.sbuf(f"{tag}_rtmp", [1, 512], F32)
    obc = bufs.sbuf(f"{tag}_obc", [P, 512], F32)
    vstash = bufs.sbuf(f"{tag}_vstash", [P, KVH * 65], BF16)
    ybf = bufs.sbuf(f"{tag}_ybf", [P, NCH * 512], BF16)
    xrt = bufs.sbuf(f"{tag}_xrt", [P, DM], F32)
    consts = {}
    for nm, shape, dt in E_AP_NAMES[8:]:
        consts[nm] = bufs.sbuf(f"{tag}_{nm}", shape, dt)
    rope = {nm: bufs.sbuf(f"{tag}_{nm}", [P, 512], F32)
            for nm in ("cosq", "sinq", "cosk", "sink")}
    wo_sb = bufs.sbuf(f"{tag}_wo", [P, NCH * DM], BF16)
    wq_sb = bufs.sbuf_tmp(f"{tag}_wq", [P, NCH * DM], BF16)
    wqr_sb = bufs.sbuf_tmp(f"{tag}_wqr", [P, NCH * DM], BF16)
    wk_sb = bufs.sbuf_tmp(f"{tag}_wk", [P, NCH * 512], BF16)
    wkr_sb = bufs.sbuf_tmp(f"{tag}_wkr", [P, NCH * 512], BF16)
    wv_sb = bufs.sbuf_tmp(f"{tag}_wv", [P, NCH * 512], BF16)
    wrow = bufs.sbuf_tmp(f"{tag}_wrow", [P, DM], F32)

    ps_s = [bufs.psum(f"{tag}_pss{i}", [P, 1024], F32) for i in range(2)]
    ps_n = bufs.psum(f"{tag}_psn", [P, 512], F32)
    ps_o = bufs.psum(f"{tag}_pso", [P, 512], F32)
    ps_c = bufs.psum(f"{tag}_psc", [P, 512], F32)
    pst = bufs.psum(f"{tag}_pst", [P, P], BF16)

    prev = {}
    toks0 = []
    for nm, shape, dt in E_AP_NAMES[8:]:
        toks0.append(ch.dma("sync", consts[nm][:], aps[nm]))

    def load_cast(dst, src, ncol):
        t_ = None
        for rt in range(DM // P):
            dl = ch.dma("sync", wrow[:, :ncol], src[rt * P:(rt + 1) * P, :],
                        deps=[prev.get("wrow")], key="wrow")
            t_ = ch.step("vector", lambda nc, rt=rt, ncol=ncol, dst=dst:
                         nc.vector.tensor_copy(
                             dst[:, rt * ncol:(rt + 1) * ncol],
                             wrow[:, :ncol]), deps=[dl])
            prev["wrow"] = t_
        return t_

    tq_ = load_cast(wq_sb, aps["Wq"], DM)
    tk_ = load_cast(wk_sb, aps["Wk"], 512)
    tv_ = load_cast(wv_sb, aps["Wv"], 512)
    to_ = load_cast(wo_sb, aps["Wo"], DM)

    def build_rot(dst, src, ncol, dep):
        ap = lambda buf, o: bass.AP(buf, o, [[NCH * ncol, P], [ncol, NCH],
                                             [HD, ncol // HD], [2, 16]])
        tz = ch.step("vector", lambda nc, dst=dst: nc.vector.memset(
            dst[:], 0.0), deps=[dep])
        te = ch.step("vector", lambda nc, dst=dst, src=src:
                     nc.vector.tensor_single_scalar(
                         ap(dst, 0), ap(src, 1), -1.0, ALU.mult), deps=[tz])
        return ch.step("vector", lambda nc, dst=dst, src=src:
                       nc.vector.tensor_copy(ap(dst, 1), ap(src, 0)),
                       deps=[te])

    trq = build_rot(wqr_sb, wq_sb, DM, tq_)
    trk = build_rot(wkr_sb, wk_sb, 512, tk_)

    tvones = ch.step("vector", lambda nc: nc.vector.memset(
        bass.AP(vstash, 64, [[KVH * 65, P], [65, KVH], [1, 1]]), 1.0),
        deps=toks0)

    tnm = None
    for sub in range(NSUB):
        last = None
        for xi in range(4):
            r0 = sub * 512 + xi * P
            xb = xt[xi % 2]
            dx = ch.dma("sync", xb[:], x_in[r0:r0 + P, :],
                        deps=[prev.get(("xt", xi % 2))] + (
                            toks0 + [tv_, trq, trk, to_] if sub == 0 else []))
            tl = emit_ln(ch, ln, xb, xlnt, dx)
            prev[("xt", xi % 2)] = tl
            tc = ch.step("vector", lambda nc: nc.vector.tensor_copy(
                xlnb[:], xlnt[:]), deps=[tl, prev.get("xlnb")])
            for c in range(NCH):
                tt = ch.step("tensor", lambda nc, c=c: nc.tensor.transpose(
                    pst[:], xlnb[:, c * P:(c + 1) * P], ident[:]),
                    deps=[tc, prev.get("pst")])
                evx = ch.step("vector",
                              lambda nc, c=c, xi=xi: nc.vector.tensor_copy(
                                  xfm[:, c * 512 + xi * P:
                                      c * 512 + (xi + 1) * P], pst[:]),
                              deps=[tt, prev.get("xfm_free")])
                prev["pst"] = evx
                last = evx
            prev["xlnb"] = last
        rtoks = []
        for nm in ("cosq", "sinq", "cosk", "sink"):
            rtoks.append(ch.dma("sync", rope[nm][:],
                                aps[nm][:, sub * 512:(sub + 1) * 512],
                                deps=[prev.get(("rp", nm))]))
        xfm_readers = []

        def rope_mm(wsb, wrsb, mc, ncol, cosn, sinn):
            def mm(nc, wsb=wsb, wrsb=wrsb, mc=mc, ncol=ncol):
                for c in range(NCH):
                    nc.tensor.matmul(ps_c[:], wsb[:, c * ncol + mc * P:
                                                  c * ncol + (mc + 1) * P],
                                     xfm[:, c * 512:(c + 1) * 512],
                                     start=(c == 0), stop=(c == NCH - 1))
                i_ = None
                for c in range(NCH):
                    i_ = nc.tensor.matmul(ps_n[:], wrsb[:, c * ncol + mc * P:
                                                        c * ncol + (mc + 1) * P],
                                          xfm[:, c * 512:(c + 1) * 512],
                                          start=(c == 0), stop=(c == NCH - 1))
                return i_
            tm = ch.step("tensor", mm,
                         deps=[last, prev.get("psab")] + rtoks)
            tra = ch.step("vector", lambda nc, cosn=cosn:
                          nc.vector.tensor_tensor(
                              qa[:], ps_c[:], rope[cosn][:], ALU.mult),
                          deps=[tm, prev.get("qa")])
            trb = ch.step("vector", lambda nc, sinn=sinn:
                          nc.vector.tensor_tensor(
                              qb_[:], ps_n[:], rope[sinn][:], ALU.mult),
                          deps=[tm, prev.get("qb")])
            prev["psab"] = trb
            tsum = ch.step("vector", lambda nc: nc.vector.tensor_add(
                qpr[:], qa[:], qb_[:]), deps=[tra, trb, prev.get("qpr")])
            prev["qa"] = tsum
            prev["qb"] = tsum
            tcq = ch.step("vector", lambda nc: nc.vector.tensor_copy(
                qbf[:], qpr[:]), deps=[tsum, prev.get("qbf")])
            return tm, tsum, tcq

        for mc in range(NCH):
            tm, tsum, tcq = rope_mm(wq_sb, wqr_sb, mc, DM, "cosq", "sinq")
            xfm_readers.append(tm)
            tsq = ch.step("scalar", lambda nc: nc.scalar.activation(
                out=sqb[:], in_=qpr[:], func=AF.Square),
                deps=[tsum, prev.get("sqb")])
            prev["qpr"] = tsq
            tnm = ch.step("tensor", lambda nc, mc=mc: nc.tensor.matmul(
                ps_o[:NH, :], consts["hv16"][:, mc * 16:(mc + 1) * 16],
                sqb[:], start=(mc == 0), stop=(mc == NCH - 1)),
                deps=[tsq, prev.get("pson")])
            prev["sqb"] = tnm
            prev["qbf"] = ch.dma(
                "gpsimd", bass.AP(q_d.tensor, mc * P * TOK + sub * 512,
                                  [[TOK, P], [1, 512]]),
                qbf[:], deps=[tcq], key="q_d")
        tnr = ch.step("vector", lambda nc, sub=sub: nc.vector.tensor_reduce(
            nrm[:, sub:sub + 1], ps_o[:NH, :], AXX, ALU.max),
            deps=[tnm, prev.get("nrm")])
        prev["pson"] = tnr
        prev["nrm"] = tnr
        for mc in range(4):
            tm, tsum, tcq = rope_mm(wk_sb, wkr_sb, mc, 512, "cosk", "sink")
            xfm_readers.append(tm)
            prev["qpr"] = tcq
            prev["qbf"] = ch.dma(
                "gpsimd", bass.AP(k_in.tensor, mc * P * TOK + sub * 512,
                                  [[TOK, P], [1, 512]]),
                qbf[:], deps=[tcq], key="k_in")
        for nm in ("cosq", "sinq", "cosk", "sink"):
            prev[("rp", nm)] = prev["qa"]
        for tj in range(4):
            def vmm(nc, tj=tj):
                i_ = None
                for c in range(NCH):
                    i_ = nc.tensor.matmul(
                        ps_c[:],
                        xfm[:, c * 512 + tj * P: c * 512 + (tj + 1) * P],
                        wv_sb[:, c * 512:(c + 1) * 512],
                        start=(c == 0), stop=(c == NCH - 1))
                return i_
            tvm = ch.step("tensor", vmm, deps=[last, prev.get("psab")])
            xfm_readers.append(tvm)
            tvc = ch.step("vector", lambda nc: nc.vector.tensor_copy(
                bass.AP(vstash, 0, [[KVH * 65, P], [65, KVH], [1, 64]]),
                ps_c[:].rearrange("p (k d) -> p k d", d=64)),
                deps=[tvm, prev.get("vstash"), tvones])
            prev["psab"] = tvc
            prev["vstash"] = ch.dma(
                "gpsimd", v_in[sub * 512 + tj * P: sub * 512 + (tj + 1) * P, :],
                vstash[:], deps=[tvc], key="v_in")
        prev["xfm_free"] = xfm_readers[-1]
    bufs.release(f"{tag}_wrow", f"{tag}_wv", f"{tag}_wkr", f"{tag}_wk",
                 f"{tag}_wqr", f"{tag}_wq")
    ofm = bufs.sbuf(f"{tag}_ofm", [P, NCH * TOK], BF16)

    tok_k = prev["qbf"]
    tok_v = prev["vstash"]
    tcc1 = ch.step("gpsimd", lambda nc: nc.gpsimd.collective_compute(
        "AllGather", ALU.bypass, replica_groups=CC_RG,
        ins=[k_in.opt()], outs=[k_full.opt()]), deps=[tok_k])
    tcc2 = ch.step("gpsimd", lambda nc: nc.gpsimd.collective_compute(
        "AllGather", ALU.bypass, replica_groups=CC_RG,
        ins=[v_in.opt()], outs=[v_full.opt()]), deps=[tok_v, tcc1])

    for rb in range(2):
        for seg in range(8):
            tkm = None
            for c4 in range(4):
                dl = ch.dma("sync", kdup[:, :512],
                            bass.AP(k_full.tensor,
                                    (rb * 512 + c4 * P) * TOK + seg * 512,
                                    [[TOK, P], [1, 512]]),
                            deps=[tcc1, prev.get("kdup")], key="kdup")
                tsq = ch.step("scalar", lambda nc: nc.scalar.activation(
                    out=sqb[:], in_=kdup[:, :512], func=AF.Square),
                    deps=[dl, prev.get("sqb")])
                prev["kdup"] = tsq
                tkm = ch.step("tensor", lambda nc, c4=c4: nc.tensor.matmul(
                    ps_o[:KVH, :], consts["hvk"][:, c4 * KVH:(c4 + 1) * KVH],
                    sqb[:], start=(c4 == 0), stop=(c4 == 3)),
                    deps=[tsq, prev.get("pson")])
                prev["sqb"] = tkm
            tkr = ch.step("vector", lambda nc, rb=rb, seg=seg:
                          nc.vector.tensor_reduce(
                              nrmk[:, rb * 8 + seg: rb * 8 + seg + 1],
                              ps_o[:KVH, :], AXX, ALU.max),
                          deps=[tkm, prev.get("nrmk")])
            prev["pson"] = tkr
            prev["nrmk"] = tkr
    t1 = ch.step("vector", lambda nc: nc.vector.tensor_reduce(
        nq[:], nrm[:], AXX, ALU.max), deps=[prev["nrm"]])
    t2 = ch.step("vector", lambda nc: nc.vector.tensor_reduce(
        nk[:], nrmk[:], AXX, ALU.max), deps=[prev["nrmk"], t1])
    t3 = ch.step("scalar", lambda nc: nc.scalar.activation(
        out=nq[:], in_=nq[:], func=AF.Sqrt), deps=[t1])
    t4 = ch.step("scalar", lambda nc: nc.scalar.activation(
        out=nk[:], in_=nk[:], func=AF.Sqrt), deps=[t2, t3])
    t5 = ch.step("vector", lambda nc: nc.vector.tensor_copy(
        nkb[:], nk[:]), deps=[t4])
    t6 = ch.step("tensor", lambda nc: nc.tensor.matmul(
        ps_o[:NH, 0:1], consts["sel16"][:], nkb[:], start=True, stop=True),
        deps=[t5, prev.get("pson")])
    t7 = ch.step("vector", lambda nc: nc.vector.tensor_copy(
        nk16[:], ps_o[:NH, 0:1]), deps=[t6])
    t8 = ch.step("vector", lambda nc: nc.vector.tensor_tensor(
        mrow[:], nq[:], nk16[:], ALU.mult), deps=[t7, t3])
    t9 = ch.step("vector", lambda nc: nc.vector.tensor_single_scalar(
        mrow[:], mrow[:], -1.0, ALU.mult), deps=[t8])
    t10 = ch.step("tensor", lambda nc: nc.tensor.transpose(
        ps_o[:1, :NH], mrow[:], identf[:NH, :NH]), deps=[t9, t7])
    t11 = ch.step("vector", lambda nc: nc.vector.tensor_copy(
        mrowT[:], ps_o[:1, :NH]), deps=[t10])
    t12 = ch.step("tensor", lambda nc: nc.tensor.matmul(
        ps_o[:, :NH], consts["ones1p"][:], mrowT[:], start=True, stop=True),
        deps=[t11])
    t13 = ch.step("vector", lambda nc: nc.vector.tensor_copy(
        mbias[:], ps_o[:, :NH]), deps=[t12])
    prev["pson"] = t13
    prev["kdup"] = t13

    NKP = 32
    for kv in range(KVH):
        kts = []
        for pi, pofs in enumerate((0, 64)):
            for rb in range(2):
                dst = kdup[pofs:pofs + 64, rb * 2048:(rb + 1) * 2048]
                kts.append(ch.dma(
                    "sync", dst.rearrange("p (j i) -> p j i", i=P),
                    bass.AP(k_full.tensor,
                            (rb * 512 + kv * HD) * TOK + pi * P,
                            [[TOK, 64], [256, 16], [1, P]]),
                    deps=[tcc1, prev.get("kdup")], key="kdup"))
        vl = ch.dma("sync", vsb[:].rearrange("p (c j) -> p c j", j=65),
                    bass.AP(v_full.tensor, kv * 65,
                            [[KVH * 65, P], [P * KVH * 65, 64], [1, 65]]),
                    deps=[tcc2, prev.get("vsb")], key="vsb")
        for qi in range(2):
            qh = 2 * kv + qi
            qts = [ch.dma("sync", qdup[0:64, :],
                          bass.AP(q_d.tensor, qh * HD * TOK,
                                  [[TOK, 64], [1, TOK]]),
                          deps=[prev.get("qdup")], key="qdup"),
                   ch.dma("sync", qdup[64:128, :],
                          bass.AP(q_d.tensor, qh * HD * TOK,
                                  [[TOK, 64], [1, TOK]]),
                          deps=[prev.get("qdup")], key="qdup")]
            for qb in range(NSUB):
                av_last = None
                exp_toks = [None, None]
                mm_toks = [None, None]
                for kp in range(NKP):
                    pb = ps_s[kp % 2]

                    def smm(nc, kp=kp, qb=qb, pb=pb):
                        nc.tensor.matmul(
                            pb[:, 0:512], kdup[0:64, kp * P:(kp + 1) * P],
                            qdup[0:64, qb * 512:(qb + 1) * 512],
                            start=True, stop=True)
                        return nc.tensor.matmul(
                            pb[:, 512:1024], kdup[64:128, kp * P:(kp + 1) * P],
                            qdup[64:128, qb * 512:(qb + 1) * 512],
                            start=True, stop=True)
                    tsm = ch.step("tensor", smm,
                                  deps=kts + qts + [exp_toks[kp % 2]])
                    mm_toks[kp % 2] = tsm
                    texp = ch.step("scalar", lambda nc, kp=kp, qh=qh, pb=pb:
                                   nc.scalar.activation(
                                       out=exps[kp % 2][:], in_=pb[:],
                                       func=AF.Exp,
                                       bias=mbias[:, qh:qh + 1]),
                                   deps=[tsm, av_last])
                    exp_toks[kp % 2] = texp

                    def avmm(nc, kp=kp, pb=pb):
                        nc.tensor.matmul(
                            ps_o[:65, :], vsb[:, (2 * kp) * 65:
                                              (2 * kp) * 65 + 65],
                            exps[kp % 2][:, 0:512],
                            start=(kp == 0), stop=False)
                        return nc.tensor.matmul(
                            ps_o[:65, :], vsb[:, (2 * kp + 1) * 65:
                                              (2 * kp + 1) * 65 + 65],
                            exps[kp % 2][:, 512:1024],
                            start=False, stop=(kp == NKP - 1))
                    av_last = ch.step("tensor", avmm,
                                      deps=[texp, vl, prev.get("ps_o_free")])
                tr1 = ch.step("vector", lambda nc: nc.vector.tensor_copy(
                    rec[:], ps_o[64:65, :]), deps=[av_last, prev.get("rec")])
                trec = emit_recip(ch, recr, rec, rtmp, 512, [tr1])
                prev["rec"] = trec
                tbc = ch.step("tensor", lambda nc: nc.tensor.matmul(
                    ps_c[:, :512], consts["ones1p"][:], recr[:],
                    start=True, stop=True), deps=[trec, prev.get("psc4")])
                tec = ch.step("vector", lambda nc: nc.vector.tensor_copy(
                    obc[:], ps_c[:, :512]), deps=[tbc, prev.get("obc")])
                prev["psc4"] = tec
                prev["recr"] = tec
                tof = ch.step("vector", lambda nc, qh=qh, qb=qb:
                              nc.vector.tensor_tensor(
                                  ofm[(qh % 2) * 64:(qh % 2) * 64 + 64,
                                      (qh // 2) * TOK + qb * 512:
                                      (qh // 2) * TOK + (qb + 1) * 512],
                                  ps_o[0:64, :], obc[0:64, :], ALU.mult),
                              deps=[tec, av_last,
                                    prev.get(("ofm", qh // 2))])
                prev["obc"] = tof
                prev["ps_o_free"] = tof
                prev[("ofm", qh // 2)] = tof
            prev["qdup"] = tsm
        prev["kdup"] = tsm
        prev["vsb"] = av_last

    ofm_toks = [prev[("ofm", c)] for c in range(NCH)]
    for qb in range(NSUB):
        ytoks = []
        for n in range(NCH):
            def omm(nc, n=n, qb=qb):
                i_ = None
                for c in range(NCH):
                    i_ = nc.tensor.matmul(
                        ps_c[:, :512],
                        wo_sb[:, c * DM + n * P: c * DM + (n + 1) * P],
                        ofm[:, c * TOK + qb * 512: c * TOK + (qb + 1) * 512],
                        start=(c == 0), stop=(c == NCH - 1))
                return i_
            tcm = ch.step("tensor", omm,
                          deps=ofm_toks + [prev.get("psc4")])
            ty = ch.step("vector", lambda nc, n=n: nc.vector.tensor_scalar(
                out=ybf[:, n * 512:(n + 1) * 512], in0=ps_c[:, :512],
                scalar1=consts["bo8"][:, n:n + 1], scalar2=None, op0=ALU.add),
                deps=[tcm, prev.get(("ybf", n))])
            prev["psc4"] = ty
            ytoks.append(ty)
        for sub in range(4):
            r0 = qb * 512 + sub * P
            dxr = ch.dma("sync", xrt[:], x_in[r0:r0 + P, :],
                         deps=[prev.get("xrt_free")], key="xrt")
            tlast = None
            for n in range(NCH):
                tt = ch.step("tensor", lambda nc, n=n, sub=sub:
                             nc.tensor.transpose(
                                 pst[:], ybf[:, n * 512 + sub * P:
                                             n * 512 + (sub + 1) * P],
                                 ident[:]), deps=[ytoks[n], prev.get("pst")])
                ta = ch.step("vector", lambda nc, n=n: nc.vector.tensor_tensor(
                    bass.AP(xrt, n * P, [[DM, P], [1, P]]),
                    bass.AP(xrt, n * P, [[DM, P], [1, P]]),
                    pst[:], ALU.add), deps=[tt, dxr, tlast])
                prev["pst"] = ta
                tlast = ta
                if sub == 3:
                    prev[("ybf", n)] = tt
            prev["xrt_free"] = ch.dma(
                "gpsimd", x_out[r0:r0 + P, :], xrt[:],
                deps=[tlast], key="xout")


B_AP_NAMES = [
    ("c_ident", [P, P], BF16), ("c_identf", [P, P], F32),
    ("w_disp", [DM, 256], F32), ("w_val", [DM, DM], F32),
    ("w_cproj", [DM, DM], F32),
    ("rel_pos_emb", [DELTA, BD], F32), ("b_strain16", [BD, 1], F32),
    ("b_damage_out", [1, 1], F32), ("ones_row", [1, P], F32),
    ("b_disp2", [P, 2], F32), ("b_val_bcast", [P, DM], F32),
    ("b_cproj8", [P, NCH], F32), ("b_state_t", [P, 1], F32),
    ("b_damage_t", [P, 1], F32), ("cmask_t0", [P, NH * DELTA], F32),
    ("invcnt_t0", [P, 640], F32), ("halo_mask", [P, P], F32),
    ("rep8", [BD, P], F32),
    ("w_state", [BD, BD], F32), ("w_strain", [BD, BD], F32),
    ("w_state_i", [BD, BD], F32), ("w_state_j", [BD, BD], F32),
    ("w_damage", [BD, BD], F32), ("w_pos", [BD, BD], F32),
    ("w_bond_out", [BD, 1], F32), ("w_damage_out16", [BD, 1], F32),
    ("bbo_t", [P, 1], F32),
]


def host_b_consts(inp, half):
    f = np.float32
    out = dict(
        c_ident=np.eye(P, dtype=ml_dtypes.bfloat16),
        c_identf=np.eye(P, dtype=f),
        w_disp=np.asarray(inp["L0_attn_w_disp"], f),
        w_val=np.asarray(inp["L0_attn_w_val"], f),
        w_cproj=np.asarray(inp["L0_attn_w_cproj"], f),
        rel_pos_emb=np.asarray(inp["L0_attn_rel_pos_emb"], f),
        b_strain16=np.asarray(inp["L0_attn_b_strain"], f).reshape(BD, 1),
        b_damage_out=np.asarray(inp["L0_attn_b_damage_out"], f).reshape(1, 1),
        ones_row=np.ones((1, P), f),
        b_disp2=np.ascontiguousarray(
            np.asarray(inp["L0_attn_b_disp"], f).reshape(2, P).T),
        b_val_bcast=np.ascontiguousarray(np.broadcast_to(
            np.asarray(inp["L0_attn_b_val"], f), (P, DM))),
        b_cproj8=np.ascontiguousarray(
            np.asarray(inp["L0_attn_b_cproj"], f).reshape(NCH, P).T),
        b_state_t=np.tile(np.asarray(inp["L0_attn_b_state"], f), NCH)
        .reshape(P, 1),
        b_damage_t=np.tile(np.asarray(inp["L0_attn_b_damage"], f), NCH)
        .reshape(P, 1),
        rep8=np.ascontiguousarray(
            np.tile(np.eye(BD, dtype=f), (1, NCH))),
        bbo_t=np.broadcast_to(
            np.asarray(inp["L0_attn_b_bond_out"], f).reshape(1, 1),
            (P, 1)).copy(),
        w_state=np.asarray(inp["L0_attn_w_state"], f),
        w_strain=np.asarray(inp["L0_attn_w_strain"], f),
        w_state_i=np.asarray(inp["L0_attn_w_state_i"], f),
        w_state_j=np.asarray(inp["L0_attn_w_state_j"], f),
        w_damage=np.asarray(inp["L0_attn_w_damage"], f),
        w_pos=np.asarray(inp["L0_attn_w_pos"], f),
        w_bond_out=np.asarray(inp["L0_attn_w_bond_out"], f).reshape(BD, 1),
        w_damage_out16=np.asarray(inp["L0_attn_w_damage_out"], f)
        .reshape(BD, 1),
    )
    if half == 0:
        cm = np.zeros((P, NH, DELTA), f)
        t = np.arange(P)[:, None]
        d = np.arange(DELTA)[None, :]
        cm[:, :, :] = np.where((t - 31 + d) < 0, -1e9, 0.0)[:, None, :]
        out["cmask_t0"] = cm.reshape(P, NH * DELTA)
        g = np.arange(640) - 128
        cnt = np.where(g < 0, 32, np.minimum(g + 1, 32)).astype(f)
        out["invcnt_t0"] = np.broadcast_to(1.0 / cnt, (P, 640)).copy()
        out["halo_mask"] = np.zeros((P, P), f)
    else:
        out["cmask_t0"] = np.zeros((P, NH * DELTA), f)
        out["invcnt_t0"] = np.full((P, 640), 1.0 / 32.0, f)
        out["halo_mask"] = np.ones((P, P), f)
    return out


def build_phaseB():
    nc = bass.Bass("TRN2", target_bir_lowering=False,
                   detect_race_conditions=False)
    x_in = nc.dram_tensor("x_in", [RFULL, DM], F32, kind="ExternalInput").ap()
    x_out = nc.dram_tensor("x_out", [TOK, DM], F32, kind="ExternalOutput").ap()
    aps = {}
    for nm, shp, dt in B_AP_NAMES:
        aps[nm] = nc.dram_tensor(nm, shp, dt, kind="ExternalInput").ap()
    with ExitStack() as ctx:
        bufs = Bufs(nc, ctx)
        ch = Chain(nc, ctx, "B")
        block = ctx.enter_context(nc.Block())
        aps2 = dict(aps)
        aps2["w_damage_out"] = aps["w_damage_out16"]
        pr = emit_l0attn_prep(ch, bufs, aps2)
        ch.fence()
        emit_l0attn_phase(ch, bufs, x_in, x_out, pr)
        finish(ch)
        ch.flush(block)
    return nc


def emit_l1ffn_prep(ch, bufs, aps, tag="Dp"):
    nc = ch.nc
    scr = {}
    for nm, nr, ncol in (("w1_w", L1H, DM), ("w2_w", L1H, DM),
                         ("h1_w", L1H, L1H), ("h2_w", L1H, L1H),
                         ("w3_w", DM, L1H)):
        scr[nm] = nc.dram_tensor(f"{tag}_{nm}T", [ncol, nr], BF16).ap()
    toks = []
    for nm, nr, ncol in (("w1_w", L1H, DM), ("w2_w", L1H, DM),
                         ("h1_w", L1H, L1H), ("h2_w", L1H, L1H),
                         ("w3_w", DM, L1H)):
        toks.append(emit_weight_prep(ch, bufs, aps[nm], scr[nm], nr, ncol,
                                     f"{tag}_{nm}", quant=False,
                                     transpose=True, deps=[]))
    sc = {}
    for nm, w_ in (("s1", 32), ("s2", 32), ("sh1", 32), ("sh2", 32)):
        sc[nm] = bufs.sbuf(f"{tag}_{nm}", [P, w_], F32)
        toks.append(ch.dma("sync", sc[nm][:], aps[nm]))
    s3c = bufs.sbuf(f"{tag}_s3c", [P, NCH], F32)
    sc["s3h"] = bufs.sbuf(f"{tag}_s3h", [P, NCH], F32)
    t3 = ch.dma("sync", s3c[:], aps["s3"])
    toks.append(ch.step("vector", lambda nc: nc.vector.tensor_single_scalar(
        sc["s3h"][:], s3c[:], 0.5, ALU.mult), deps=[t3]))
    return scr, sc, toks


def build_phaseD():
    nc = bass.Bass("TRN2", target_bir_lowering=False,
                   detect_race_conditions=False)
    x_in = nc.dram_tensor("x_in", [TOK, DM], F32, kind="ExternalInput").ap()
    x_out = nc.dram_tensor("x_out", [TOK, DM], F32, kind="ExternalOutput").ap()
    aps = {}
    for nm, shp in (("w1_w", [L1H, DM]), ("w2_w", [L1H, DM]),
                    ("h1_w", [L1H, L1H]), ("h2_w", [L1H, L1H]),
                    ("w3_w", [DM, L1H]), ("s1", [P, 32]), ("s2", [P, 32]),
                    ("sh1", [P, 32]), ("sh2", [P, 32]), ("s3", [P, NCH])):
        aps[nm] = nc.dram_tensor(nm, shp, F32, kind="ExternalInput").ap()
    c_ident = nc.dram_tensor("c_ident", [P, P], BF16, kind="ExternalInput").ap()
    h0_d = nc.dram_tensor("h0_scr", [L1H, TOK], BF16).ap()
    hA_d = nc.dram_tensor("hA_scr", [L1H, TOK], BF16).ap()
    hB_d = nc.dram_tensor("hB_scr", [L1H, TOK], BF16).ap()
    with ExitStack() as ctx:
        bufs = Bufs(nc, ctx)
        ch = Chain(nc, ctx, "D")
        ident = bufs.sbuf("sb_ident", [P, P], BF16)
        block = ctx.enter_context(nc.Block())
        ti = ch.dma("sync", ident[:], c_ident)
        scr, sc, toks = emit_l1ffn_prep(ch, bufs, aps)
        ch.wait_only("tensor", toks + [ti])
        ch.wait_only("sync", toks + [ti])
        emit_l1ffn_phase(ch, bufs, x_in, x_out, scr["w1_w"], scr["w2_w"],
                         scr["h1_w"], scr["h2_w"], scr["w3_w"], sc,
                         h0_d, hA_d, hB_d)
        finish(ch)
        ch.flush(block)
    return nc


def host_d_consts(inp):
    f = np.float32
    rs = lambda v, n: np.ascontiguousarray(
        np.asarray(v, f).reshape(n, P).T)
    return dict(
        w1_w=np.asarray(inp["L1_ffn_w1_w"], f),
        w2_w=np.asarray(inp["L1_ffn_w2_w"], f),
        h1_w=np.asarray(inp["L1_ffn_h1_w"], f),
        h2_w=np.asarray(inp["L1_ffn_h2_w"], f),
        w3_w=np.asarray(inp["L1_ffn_w3_w"], f),
        s1=rs(inp["L1_ffn_w1_s"], 32), s2=rs(inp["L1_ffn_w2_s"], 32),
        sh1=rs(inp["L1_ffn_h1_s"], 32), sh2=rs(inp["L1_ffn_h2_s"], 32),
        s3=rs(inp["L1_ffn_w3_s"], NCH),
        c_ident=np.eye(P, dtype=ml_dtypes.bfloat16),
    )


def build_phaseE():
    nc = bass.Bass("TRN2", target_bir_lowering=False,
                   detect_race_conditions=False)
    x_in = nc.dram_tensor("x_in", [TOK, DM], F32, kind="ExternalInput").ap()
    x_out = nc.dram_tensor("x_out", [TOK, DM], F32, kind="ExternalOutput").ap()
    aps = {}
    for nm, shp, dt in E_AP_NAMES:
        aps[nm] = nc.dram_tensor(nm, shp, dt, kind="ExternalInput").ap()
    c_ident = nc.dram_tensor("c_ident", [P, P], BF16, kind="ExternalInput").ap()
    c_identf = nc.dram_tensor("c_identf", [P, P], F32,
                              kind="ExternalInput").ap()
    with ExitStack() as ctx:
        bufs = Bufs(nc, ctx)
        ch = Chain(nc, ctx, "E")
        ident = bufs.sbuf("sb_ident", [P, P], BF16)
        identf = bufs.sbuf("sb_identf", [P, P], F32)
        block = ctx.enter_context(nc.Block())
        t1 = ch.dma("sync", ident[:], c_ident)
        t2 = ch.dma("sync", identf[:], c_identf)
        ch.wait_only("tensor", [t1, t2])
        ch.wait_only("vector", [t1, t2])
        emit_l1attn_phase(ch, bufs, x_in, x_out, aps)
        finish(ch)
        ch.flush(block)
    return nc


def build_phaseA_full(R):
    nc = bass.Bass("TRN2", target_bir_lowering=False,
                   detect_race_conditions=False)
    x_in = nc.dram_tensor("x_in", [R, DM], F32, kind="ExternalInput").ap()
    wg = nc.dram_tensor("wg", [L0I, DM], F32, kind="ExternalInput").ap()
    wu = nc.dram_tensor("wu", [L0I, DM], F32, kind="ExternalInput").ap()
    wd = nc.dram_tensor("wd", [DM, L0I], F32, kind="ExternalInput").ap()
    c_ident = nc.dram_tensor("c_ident", [P, P], BF16, kind="ExternalInput").ap()
    x_out = nc.dram_tensor("x_out", [R, DM], F32, kind="ExternalOutput").ap()
    wgT = nc.dram_tensor("wgT", [DM, L0I], BF16, kind="ExternalOutput").ap()
    wuT = nc.dram_tensor("wuT", [DM, L0I], BF16, kind="ExternalOutput").ap()
    wdT = nc.dram_tensor("wdT", [L0I, DM], BF16, kind="ExternalOutput").ap()
    xq_d = nc.dram_tensor("xq_scr", [DM, R], BF16).ap()
    hq_d = nc.dram_tensor("hq_scr", [L0I, R], BF16).ap()

    with ExitStack() as ctx:
        bufs = Bufs(nc, ctx)
        ch = Chain(nc, ctx, "A")
        ident = bufs.sbuf("sb_ident", [P, P], BF16)
        block = ctx.enter_context(nc.Block())
        ti = ch.dma("sync", ident[:], c_ident)
        t1 = emit_weight_prep(ch, bufs, wg, wgT, L0I, DM, "pg", quant=True,
                              transpose=True, deps=[ti])
        t2 = emit_weight_prep(ch, bufs, wu, wuT, L0I, DM, "pu", quant=True,
                              transpose=True, deps=[t1])
        t3 = emit_weight_prep(ch, bufs, wd, wdT, DM, L0I, "pd", quant=True,
                              transpose=True, deps=[t2])
        for e in ENGINES:
            ch.wait_only(e, [t1, t2, t3])
        emit_l0ffn_phase(ch, bufs, R, x_in, x_out, wgT, wuT, wdT,
                         xq_d, hq_d, "A")
        finish(ch)
        ch.flush(block)
    return nc


def build_fused():
    nc = bass.Bass("TRN2", target_bir_lowering=False,
                   detect_race_conditions=False)
    x_in = nc.dram_tensor("x_in", [RFULL, DM], F32, kind="ExternalInput").ap()
    x_out = nc.dram_tensor("x_out", [TOK, DM], F32, kind="ExternalOutput").ap()
    wg = nc.dram_tensor("wg", [L0I, DM], F32, kind="ExternalInput").ap()
    wu = nc.dram_tensor("wu", [L0I, DM], F32, kind="ExternalInput").ap()
    wd = nc.dram_tensor("wd", [DM, L0I], F32, kind="ExternalInput").ap()
    c_ident = nc.dram_tensor("c_ident", [P, P], BF16, kind="ExternalInput").ap()
    c_identf = nc.dram_tensor("c_identf", [P, P], F32,
                              kind="ExternalInput").ap()
    bap = {}
    for nm, shp, dt in B_AP_NAMES:
        if nm in ("c_ident", "c_identf"):
            continue
        bap[nm] = nc.dram_tensor(nm, shp, dt, kind="ExternalInput").ap()
    bap["c_ident"], bap["c_identf"] = c_ident, c_identf
    dap = {}
    for nm, shp in (("w1_w", [L1H, DM]), ("w2_w", [L1H, DM]),
                    ("h1_w", [L1H, L1H]), ("h2_w", [L1H, L1H]),
                    ("w3_w", [DM, L1H]), ("s1", [P, 32]), ("s2", [P, 32]),
                    ("sh1", [P, 32]), ("sh2", [P, 32]), ("s3", [P, NCH])):
        dap[nm] = nc.dram_tensor(nm, shp, F32, kind="ExternalInput").ap()
    eap = {}
    for nm, shp, dt in E_AP_NAMES:
        eap[nm] = nc.dram_tensor("e_" + nm, shp, dt, kind="ExternalInput").ap()

    wgT = nc.dram_tensor("wgT", [DM, L0I], BF16).ap()
    wuT = nc.dram_tensor("wuT", [DM, L0I], BF16).ap()
    wdT = nc.dram_tensor("wdT", [L0I, DM], BF16).ap()
    xq_a = nc.dram_tensor("xq_a", [DM, RFULL], BF16).ap()
    hq_a = nc.dram_tensor("hq_a", [L0I, RFULL], BF16).ap()
    xq_c = nc.dram_tensor("xq_c", [DM, TOK], BF16).ap()
    hq_c = nc.dram_tensor("hq_c", [L0I, TOK], BF16).ap()
    h0_d = nc.dram_tensor("h0_scr", [L1H, TOK], BF16).ap()
    hA_d = nc.dram_tensor("hA_scr", [L1H, TOK], BF16).ap()
    hB_d = nc.dram_tensor("hB_scr", [L1H, TOK], BF16).ap()
    x1_d = nc.dram_tensor("x1_d", [RFULL, DM], F32).ap()
    x2_d = nc.dram_tensor("x2_d", [TOK, DM], F32).ap()
    x3_d = nc.dram_tensor("x3_d", [TOK, DM], F32).ap()
    x4_d = nc.dram_tensor("x4_d", [TOK, DM], F32).ap()
    x5_d = nc.dram_tensor("x5_d", [TOK, DM], F32).ap()

    with ExitStack() as ctx:
        bufs = Bufs(nc, ctx)
        ch = Chain(nc, ctx, "M")
        ident = bufs.sbuf("sb_ident", [P, P], BF16)
        identf = bufs.sbuf("sb_identf", [P, P], F32)
        block = ctx.enter_context(nc.Block())
        t1 = ch.dma("sync", ident[:], c_ident)
        t2 = ch.dma("sync", identf[:], c_identf)
        for e in ENGINES:
            ch.wait_only(e, [t1, t2])
        bufs.push_scope()
        tp1 = emit_weight_prep(ch, bufs, wg, wgT, L0I, DM, "pg", quant=True,
                               transpose=True, deps=[])
        tp2 = emit_weight_prep(ch, bufs, wu, wuT, L0I, DM, "pu", quant=True,
                               transpose=True, deps=[tp1])
        emit_weight_prep(ch, bufs, wd, wdT, DM, L0I, "pd", quant=True,
                         transpose=True, deps=[tp2])
        bufs.pop_scope()
        ch.fence()
        scr, sc, _ = emit_l1ffn_prep(ch, bufs, dap)
        ch.fence()
        bufs.push_scope()
        emit_l0ffn_phase(ch, bufs, RFULL, x_in, x1_d, wgT, wuT, wdT,
                         xq_a, hq_a, "A")
        bufs.pop_scope()
        ch.fence()
        bufs.push_scope()
        bap2 = dict(bap)
        bap2["w_damage_out"] = bap["w_damage_out16"]
        pr = emit_l0attn_prep(ch, bufs, bap2)
        ch.fence()
        emit_l0attn_phase(ch, bufs, x1_d, x2_d, pr)
        bufs.pop_scope()
        ch.fence()
        bufs.push_scope()
        emit_l0ffn_phase(ch, bufs, TOK, x2_d, x3_d, wgT, wuT, wdT,
                         xq_c, hq_c, "C")
        bufs.pop_scope()
        ch.fence()
        bufs.push_scope()
        emit_l1ffn_phase(ch, bufs, x3_d, x4_d, scr["w1_w"], scr["w2_w"],
                         scr["h1_w"], scr["h2_w"], scr["w3_w"], sc,
                         h0_d, hA_d, hB_d, tag="D")
        bufs.pop_scope()
        ch.fence()
        bufs.push_scope()
        emit_l1attn_phase(ch, bufs, x4_d, x5_d, eap)
        bufs.pop_scope()
        ch.fence()
        bufs.push_scope()
        emit_l1ffn_phase(ch, bufs, x5_d, x_out, scr["w1_w"], scr["w2_w"],
                         scr["h1_w"], scr["h2_w"], scr["w3_w"], sc,
                         h0_d, hA_d, hB_d, tag="F")
        bufs.pop_scope()
        finish(ch)
        ch.flush(block)
    return nc


def kernel_fused(inputs):
    f = np.float32
    x0 = np.asarray(inputs["L0_attn_x"], f)
    ident = np.eye(P, dtype=ml_dtypes.bfloat16)
    identf = np.eye(P, dtype=f)
    cores = list(range(8))
    nc = _get("fused", build_fused)
    dconsts = host_d_consts(inputs)
    dconsts.pop("c_ident")
    ec = [host_e_consts(inputs, h) for h in range(2)]
    maps = []
    for cid in cores:
        b, h = cid // 2, cid % 2
        own = x0[b, h * TOK:(h + 1) * TOK]
        halo = (np.zeros((HALO, DM), f) if h == 0
                else x0[b, h * TOK - HALO:h * TOK])
        m = dict(x_in=np.concatenate([halo, own], 0),
                 wg=np.asarray(inputs["L0_ffn_w_gate"], f),
                 wu=np.asarray(inputs["L0_ffn_w_up"], f),
                 wd=np.asarray(inputs["L0_ffn_w_down"], f),
                 c_ident=ident, c_identf=identf)
        bc = host_b_consts(inputs, h)
        bc.pop("c_ident"); bc.pop("c_identf")
        m.update(bc)
        m.update(dconsts)
        m.update({"e_" + k: v for k, v in ec[h].items()})
        maps.append(m)
    res = run_bass_kernel_spmd(nc, maps, core_ids=cores)
    out = np.empty((4, 2 * TOK, DM), f)
    for cid in cores:
        b, h = cid // 2, cid % 2
        out[b, h * TOK:(h + 1) * TOK] = np.asarray(res.results[cid]["x_out"])
    return out


_BUILD_CACHE = {}


def _get(name, fn):
    if name not in _BUILD_CACHE:
        _BUILD_CACHE[name] = fn()
    return _BUILD_CACHE[name]


def kernel(**inputs):
    return kernel_fused(inputs)


def kernel_unfused(**inputs):
    f = np.float32
    x0 = np.asarray(inputs["L0_attn_x"], f)
    ident = np.eye(P, dtype=ml_dtypes.bfloat16)
    identf = np.eye(P, dtype=f)
    cores = list(range(8))
    wg = np.asarray(inputs["L0_ffn_w_gate"], f)
    wu = np.asarray(inputs["L0_ffn_w_up"], f)
    wd = np.asarray(inputs["L0_ffn_w_down"], f)

    def shard(cid):
        b, h = cid // 2, cid % 2
        own = x0[b, h * TOK:(h + 1) * TOK]
        halo = (np.zeros((HALO, DM), f) if h == 0
                else x0[b, h * TOK - HALO:h * TOK])
        return np.concatenate([halo, own], 0)

    ncA = _get("Afull", lambda: build_phaseA_full(RFULL))
    maps = [dict(x_in=shard(c), wg=wg, wu=wu, wd=wd, c_ident=ident)
            for c in cores]
    resA = run_bass_kernel_spmd(ncA, maps, core_ids=cores)
    x1 = [np.asarray(r["x_out"]) for r in resA.results]
    wT = [(np.asarray(r["wgT"]), np.asarray(r["wuT"]), np.asarray(r["wdT"]))
          for r in resA.results]

    ncB = _get("B", build_phaseB)
    maps = [dict(x_in=x1[c], **host_b_consts(inputs, c % 2)) for c in cores]
    resB = run_bass_kernel_spmd(ncB, maps, core_ids=cores)
    x2 = [np.asarray(r["x_out"]) for r in resB.results]

    ncC = _get("C", lambda: build_phaseA(TOK))
    maps = [dict(x_in=x2[c], wgT=wT[c][0], wuT=wT[c][1], wdT=wT[c][2],
                 c_ident=ident) for c in cores]
    resC = run_bass_kernel_spmd(ncC, maps, core_ids=cores)
    x3 = [np.asarray(r["x_out"]) for r in resC.results]

    ncD = _get("D", build_phaseD)
    dconsts = host_d_consts(inputs)
    maps = [dict(x_in=x3[c], **dconsts) for c in cores]
    resD = run_bass_kernel_spmd(ncD, maps, core_ids=cores)
    x4 = [np.asarray(r["x_out"]) for r in resD.results]

    ncE = _get("E", build_phaseE)
    ec = [host_e_consts(inputs, h) for h in range(2)]
    maps = [dict(x_in=x4[c], **ec[c % 2], c_ident=ident, c_identf=identf)
            for c in cores]
    resE = run_bass_kernel_spmd(ncE, maps, core_ids=cores)
    x5 = [np.asarray(r["x_out"]) for r in resE.results]

    maps = [dict(x_in=x5[c], **dconsts) for c in cores]
    resF = run_bass_kernel_spmd(ncD, maps, core_ids=cores)
    x6 = [np.asarray(r["x_out"]) for r in resF.results]

    out = np.empty((4, 2 * TOK, DM), f)
    for c in cores:
        b, h = c // 2, c % 2
        out[b, h * TOK:(h + 1) * TOK] = x6[c]
    return out


# version 05 (final)
# speedup vs baseline: 1.0618x; 1.0618x over previous
import re
import numpy as np
import ml_dtypes
import concourse.bass as bass
import concourse.mybir as mybir
from concourse.bass_utils import run_bass_kernel_spmd
from contextlib import ExitStack

F32 = mybir.dt.float32
BF16 = mybir.dt.bfloat16
I32 = mybir.dt.int32
U8 = mybir.dt.uint8
AF = mybir.ActivationFunctionType
ALU = mybir.AluOpType
AXX = mybir.AxisListType.X

P = 128
DM = 1024
NCH = DM // P
TOK = 4096
HALO = 128
RFULL = TOK + HALO
EPS = 1e-5

C_RNE = float(1.5 * 2.0 ** 23)
SH_HI = int(0x1FFFFF - 0x3F800000)
MASK_HI = int(np.uint32(0xFFC00000)) - (1 << 32)
ADD_BACK = 0x3F800000
ABS_MASK = 0x7FFFFFFF
SGN_MASK = int(np.uint32(0x80000000)) - (1 << 32)
ONE_BITS = 0x3F800000

L0I = 2816
L1H = 4096

ENGINES = ("sync", "gpsimd", "vector", "scalar", "tensor")


class Chain:

    def __init__(self, nc, ctxstack, tag):
        self.nc = nc
        self.q = {e: [] for e in ENGINES}
        self.sem = {}
        self.cnt = {}
        self.e = ctxstack.enter_context
        for e in ENGINES:
            self.sem[e] = self.e(nc.semaphore(f"{tag}_{e}"))
            self.cnt[e] = 0
        self.tag = tag
        self.dsem = {}
        self.dcnt = {}
        self.region = 0
        self.slotmap = {}
        self.region_nslots = {}

    def _waits(self, eng, deps):
        best = {}
        for tok in deps:
            if tok is None:
                continue
            sem, val = tok
            if val > best.get(sem, -1):
                best[sem] = val
        return list(best.items())

    def step(self, eng, emitfn, deps=()):
        waits = self._waits(eng, deps)
        self.cnt[eng] += 1
        val = self.cnt[eng]
        sem = self.sem[eng]

        def run(e, waits=waits, emitfn=emitfn, sem=sem):
            for s, v in waits:
                e.wait_ge(s, v)
            last = emitfn(self.nc)
            last.then_inc(sem, 1)
        self.q[eng].append(run)
        return (sem, val)

    def canon(self, key):
        rk = (self.region, key)
        if rk not in self.slotmap:
            idx = self.region_nslots.get(self.region, 0)
            self.region_nslots[self.region] = idx + 1
            self.slotmap[rk] = f"s{idx}"
        return self.slotmap[rk]

    def dma(self, eng, out, in_, deps=(), key=None, **kw):
        waits = self._waits(eng, deps)
        key = self.canon(key or out.name)
        if key not in self.dsem:
            self.dsem[key] = self.e(self.nc.semaphore(f"{self.tag}_d_{key}"))
            self.dcnt[key] = 0
        self.dcnt[key] += 16
        val = self.dcnt[key]
        sem = self.dsem[key]

        def run(e, waits=waits, out=out, in_=in_, sem=sem, kw=kw):
            for s, v in waits:
                e.wait_ge(s, v)
            e.dma_start(out=out, in_=in_, **kw).then_inc(sem, 16)
        self.q[eng].append(run)
        return (sem, val)

    def barrier_token(self, eng):
        return (self.sem[eng], self.cnt[eng])

    def wait_only(self, eng, deps):
        waits = self._waits(eng, deps)

        def run(e, waits=waits):
            for s, v in waits:
                e.wait_ge(s, v)
        self.q[eng].append(run)

    def fence(self):
        toks = [(self.sem[e], self.cnt[e]) for e in ENGINES]
        toks += [(self.dsem[k], self.dcnt[k]) for k in self.dsem]
        for e in ENGINES:
            waits = [(sm, v) for sm, v in toks
                     if v > 0 and sm is not self.sem[e]]

            def run(eng, waits=waits):
                for sm, v in waits:
                    eng.wait_ge(sm, v)
            self.q[e].append(run)
        self.region += 1

    def flush(self, block):
        for ename in ENGINES:
            if not self.q[ename]:
                continue
            lst = self.q[ename]

            def body(eng, lst=lst):
                for fn in lst:
                    fn(eng)
            getattr(block, ename)(body)
        self.q = {e: [] for e in ENGINES}


class Bufs:
    def __init__(self, nc, ctxstack):
        self.nc = nc
        self.e = ctxstack.enter_context
        self.map = {}
        self._psum_cms = {}
        self._scopes = []

    def push_scope(self):
        self._scopes.append([])

    def pop_scope(self):
        names = self._scopes.pop()
        for name in reversed(names):
            cm = self._psum_cms.pop(name)
            cm.__exit__(None, None, None)
            self.map.pop(name, None)

    def sbuf(self, name, shape, dtype):
        if name not in self.map:
            if self._scopes:
                cm = self.nc.sbuf_tensor(name, shape, dtype)
                t = cm.__enter__()
                self._psum_cms[name] = cm
                self._scopes[-1].append(name)
                self.map[name] = t
            else:
                self.map[name] = self.e(
                    self.nc.sbuf_tensor(name, shape, dtype))
        return self.map[name]

    def sbuf_tmp(self, name, shape, dtype):
        cm = self.nc.sbuf_tensor(name, shape, dtype)
        t = cm.__enter__()
        self._psum_cms[name] = cm
        if self._scopes:
            self._scopes[-1].append(name)
        self.map[name] = t
        return t

    def psum(self, name, shape, dtype=F32):
        cm = self.nc.psum_tensor(name, shape, dtype)
        t = cm.__enter__()
        self._psum_cms[name] = cm
        if self._scopes:
            self._scopes[-1].append(name)
        self.map[name] = t
        return t

    def release(self, *names):
        for name in names:
            cm = self._psum_cms.pop(name)
            cm.__exit__(None, None, None)
            self.map.pop(name, None)
            for sc in self._scopes:
                if name in sc:
                    sc.remove(name)



def emit_recip(ch, out, in_, tmp, n, deps):
    t = ch.step("vector", lambda nc: nc.vector.tensor_scalar(
        out=out[:, :n].bitcast(I32), in0=in_[:, :n].bitcast(I32),
        scalar1=-1, scalar2=0x7EF311C3, op0=ALU.mult, op1=ALU.add), deps=deps)
    for _ in range(3):
        t = ch.step("vector", lambda nc: nc.vector.tensor_tensor(
            tmp[:, :n], in_[:, :n], out[:, :n], ALU.mult), deps=[t])
        t = ch.step("vector", lambda nc: nc.vector.tensor_scalar(
            out=tmp[:, :n], in0=tmp[:, :n], scalar1=-1.0, scalar2=2.0,
            op0=ALU.mult, op1=ALU.add), deps=[t])
        t = ch.step("vector", lambda nc: nc.vector.tensor_tensor(
            out[:, :n], out[:, :n], tmp[:, :n], ALU.mult), deps=[t])
    return t


def quant_scratch(bufs, tag, F, tmp=False):
    al = bufs.sbuf_tmp if tmp else bufs.sbuf
    return dict(
        F=F,
        amax=al(f"{tag}_amax", [P, F // 16], F32),
        s=al(f"{tag}_s", [P, F // 16], F32),
        xn=al(f"{tag}_xn", [P, F], F32),
        a=al(f"{tag}_a", [P, F], F32),
        ql=al(f"{tag}_ql", [P, F], F32),
        qh=al(f"{tag}_qh", [P, F], F32),
        mask=al(f"{tag}_mask", [P, F], U8),
        sgn=al(f"{tag}_sgn", [P, F], I32),
        qs=al(f"{tag}_qs", [P, F], F32),
    )


def emit_quant(ch, q, xin, out_bf, F, deps):
    B = F // 16
    t1 = ch.step("vector", lambda nc: nc.vector.tensor_reduce(
        q["amax"][:, :B].unsqueeze(2),
        xin[:, :F].rearrange("p (b s) -> p b s", s=16),
        AXX, ALU.max, apply_absolute_value=True), deps=deps)

    ts_ = ch.step("vector", lambda nc: nc.vector.tensor_scalar(
        out=q["s"][:, :B], in0=q["amax"][:, :B], scalar1=1.0 / 6.0,
        scalar2=1e-12, op0=ALU.mult, op1=ALU.max), deps=[t1])
    t2 = emit_recip(ch, q["amax"], q["s"], q["ql"], B, [ts_])
    return ch.step("vector",
                   lambda nc: emit_quant_group(nc, q, xin, out_bf, F),
                   deps=list(deps) + [t2])


def emit_quant_group(nc, q, xin, out_bf, F):
    B = F // 16
    v = nc.vector
    v.tensor_tensor(q["xn"][:, :F].rearrange("p (b s) -> p b s", s=16),
                    xin[:, :F].rearrange("p (b s) -> p b s", s=16),
                    q["amax"][:, :B].unsqueeze(2).broadcast_to([P, B, 16]),
                    ALU.mult)
    v.tensor_single_scalar(q["a"][:, :F].bitcast(I32),
                           q["xn"][:, :F].bitcast(I32),
                           ABS_MASK, ALU.bitwise_and)
    v.tensor_scalar(out=q["ql"][:, :F], in0=q["a"][:, :F],
                    scalar1=2.0, scalar2=C_RNE, op0=ALU.mult, op1=ALU.add)
    v.tensor_scalar(out=q["ql"][:, :F], in0=q["ql"][:, :F],
                    scalar1=C_RNE, scalar2=0.5, op0=ALU.subtract, op1=ALU.mult)
    v.tensor_single_scalar(q["qh"][:, :F].bitcast(I32),
                           q["a"][:, :F].bitcast(I32), SH_HI, ALU.add)
    v.tensor_single_scalar(q["qh"][:, :F].bitcast(I32),
                           q["qh"][:, :F].bitcast(I32), MASK_HI,
                           ALU.bitwise_and)
    v.tensor_single_scalar(q["qh"][:, :F].bitcast(I32),
                           q["qh"][:, :F].bitcast(I32), ADD_BACK, ALU.add)
    v.tensor_single_scalar(q["mask"][:, :F], q["a"][:, :F].bitcast(I32),
                           ONE_BITS, ALU.is_lt)
    v.select(q["qs"][:, :F], q["mask"][:, :F], q["ql"][:, :F], q["qh"][:, :F])
    v.tensor_single_scalar(q["sgn"][:, :F], q["xn"][:, :F].bitcast(I32),
                           SGN_MASK, ALU.bitwise_and)
    v.tensor_tensor(q["qs"][:, :F].bitcast(I32), q["qs"][:, :F].bitcast(I32),
                    q["sgn"][:, :F], ALU.bitwise_or)
    return v.tensor_tensor(
        out_bf[:, :F].rearrange("p (b s) -> p b s", s=16),
        q["qs"][:, :F].rearrange("p (b s) -> p b s", s=16),
        q["s"][:, :B].unsqueeze(2).broadcast_to([P, B, 16]), ALU.mult)



def emit_weight_prep(ch, bufs, src, dst, nr, ncol, tag, quant=False,
                     transpose=True, deps=()):
    nc = ch.nc
    wrow = [bufs.sbuf_tmp(f"{tag}_wrow{i}", [P, ncol], F32) for i in range(2)]
    qbf = bufs.sbuf_tmp(f"{tag}_qbf", [P, ncol], BF16)
    q = quant_scratch(bufs, f"{tag}_q", ncol, tmp=True) if quant else None
    if transpose:
        tev = bufs.sbuf_tmp(f"{tag}_tev", [P, ncol], BF16)
        ident = bufs.sbuf("sb_ident", [P, P], BF16)
        pst = bufs.psum(f"{tag}_pst", [P, P], BF16)
    row_free = [None, None]
    qbf_free = None
    tev_free = None
    pst_free = None
    last_store = None
    for rt in range(nr // P):
        wb = wrow[rt % 2]
        dl = ch.dma("sync", wb[:], src[rt * P:(rt + 1) * P, :],
                    deps=[row_free[rt % 2]] + list(deps))
        if quant:
            tq = emit_quant(ch, q, wb, qbf, ncol, [dl, qbf_free])
        else:
            tq = ch.step("vector", lambda nc, wb=wb: nc.vector.tensor_copy(
                qbf[:], wb[:]), deps=[dl, qbf_free])
        row_free[rt % 2] = tq
        if not transpose:
            qbf_free = ch.dma("gpsimd", dst[rt * P:(rt + 1) * P, :], qbf[:],
                              deps=[tq], key=dst.name)
            last_store = qbf_free
            continue
        evs = []
        for ct in range(ncol // P):
            tt = ch.step("tensor", lambda nc, ct=ct: nc.tensor.transpose(
                pst[:], qbf[:, ct * P:(ct + 1) * P], ident[:]),
                deps=[tq, pst_free, tev_free])
            ev = ch.step("vector", lambda nc, ct=ct: nc.vector.tensor_copy(
                tev[:, ct * P:(ct + 1) * P], pst[:]), deps=[tt])
            pst_free = ev
            evs.append(ev)
        qbf_free = evs[-1]
        tev_free = ch.dma(
            "gpsimd",
            bass.AP(dst.tensor, rt * P, [[nr, P], [P * nr, ncol // P], [1, P]]),
            tev[:].rearrange("p (c j) -> p c j", j=P),
            deps=[evs[-1]], key=dst.name)
        last_store = tev_free
    names = [f"{tag}_wrow0", f"{tag}_wrow1", f"{tag}_qbf"]
    if quant:
        names += [f"{tag}_q_{k}" for k in
                  ("amax", "s", "xn", "a", "ql", "qh", "mask", "sgn", "qs")]
    if transpose:
        names += [f"{tag}_tev", f"{tag}_pst"]
    bufs.release(*reversed(names))
    return last_store



def ln_scratch(bufs, tag):
    return dict(
        mneg=bufs.sbuf(f"{tag}_mneg", [P, 1], F32),
        ss=bufs.sbuf(f"{tag}_ss", [P, 1], F32),
        var=bufs.sbuf(f"{tag}_var", [P, 1], F32),
        sd=bufs.sbuf(f"{tag}_sd", [P, 1], F32),
        rstd=bufs.sbuf(f"{tag}_rstd", [P, 1], F32),
        junk=bufs.sbuf(f"{tag}_junk", [P, DM], F32),
    )


def emit_ln(ch, ln, xt, xln, dep_x, width=DM):
    nc = ch.nc
    W = width

    t0 = ch.step("vector", lambda nc: nc.vector.tensor_reduce(
        ln["mneg"][:], xt[:, :W], AXX, ALU.add),
        deps=[dep_x, ln.get("_prev_s1")])
    t1 = ch.step("vector", lambda nc: nc.vector.tensor_single_scalar(
        ln["mneg"][:], ln["mneg"][:], -1.0 / W, ALU.mult), deps=[t0])

    def s1(nc):
        return nc.scalar.activation(out=ln["junk"][:, :W], in_=xt[:, :W],
                                    func=AF.Square, bias=ln["mneg"][:],
                                    scale=1.0, accum_out=ln["ss"][:])
    t2 = ch.step("scalar", s1, deps=[dep_x, t1])
    ln["_prev_s1"] = t2

    def v2(nc):
        return nc.vector.tensor_scalar(out=ln["var"][:], in0=ln["ss"][:],
                                       scalar1=1.0 / W, scalar2=EPS,
                                       op0=ALU.mult, op1=ALU.add)
    t3 = ch.step("vector", v2, deps=[t2])

    def s2(nc):
        return nc.scalar.activation(out=ln["sd"][:], in_=ln["var"][:],
                                    func=AF.Sqrt)
    t4 = ch.step("scalar", s2, deps=[t3])

    t4b = emit_recip(ch, ln["rstd"], ln["sd"], ln["var"], 1, [t4])
    t5 = ch.step("vector", lambda nc: nc.vector.tensor_scalar(
        out=xln[:, :W], in0=xt[:, :W], scalar1=ln["mneg"][:],
        scalar2=ln["rstd"][:], op0=ALU.add, op1=ALU.mult),
        deps=[t4b, dep_x])
    return t5



def emit_l0ffn_phase(ch, bufs, R, x_in, x_out, wgT, wuT, wdT, xq_d, hq_d, tag,
                     scale=0.5):
    nc = ch.nc
    ntile = R // P
    ln = ln_scratch(bufs, f"{tag}ln")
    qx = quant_scratch(bufs, f"{tag}qx", DM)
    qh = quant_scratch(bufs, f"{tag}qh", 512)
    ident = bufs.sbuf("sb_ident", [P, P], BF16)

    xt = [bufs.sbuf(f"{tag}_xt{i}", [P, DM], F32) for i in range(2)]
    xlnt = bufs.sbuf(f"{tag}_xln", [P, DM], F32)
    xqbf = bufs.sbuf(f"{tag}_xqbf", [P, DM], BF16)
    tev = bufs.sbuf(f"{tag}_tev", [P, DM], BF16)
    pst = bufs.psum(f"{tag}_pst", [P, P], BF16)

    xt_free = [None, None]
    tev_free = None
    pst_free = None
    xqbf_free = None
    tok_xq_done = None
    for i in range(ntile):
        xb = xt[i % 2]
        dx = ch.dma("sync", xb[:], x_in[i * P:(i + 1) * P, :],
                    deps=[xt_free[i % 2]])
        tl = emit_ln(ch, ln, xb, xlnt, dx)
        xt_free[i % 2] = tl
        tq = emit_quant(ch, qx, xlnt, xqbf, DM, [tl, xqbf_free])
        evs = []
        for c in range(NCH):
            tt = ch.step("tensor",
                         lambda nc, c=c: nc.tensor.transpose(
                             pst[:], xqbf[:, c * P:(c + 1) * P], ident[:]),
                         deps=[tq, pst_free, tev_free])
            ev = ch.step("vector",
                         lambda nc, c=c: nc.vector.tensor_copy(
                             tev[:, c * P:(c + 1) * P], pst[:]),
                         deps=[tt])
            pst_free = ev
            evs.append(ev)
        xqbf_free = evs[-1]
        tev_free = ch.dma(
            "gpsimd",
            bass.AP(xq_d.tensor, i * P, [[R, P], [P * R, NCH], [1, P]]),
            tev[:].rearrange("p (c j) -> p c j", j=P),
            deps=[evs[-1]])
        tok_xq_done = tev_free

    wbuf = [dict(g=bufs.sbuf(f"{tag}_wg{i}", [P, NCH * 512], BF16),
                 u=bufs.sbuf(f"{tag}_wu{i}", [P, NCH * 512], BF16))
            for i in range(2)]
    xqb = [bufs.sbuf(f"{tag}_xqb{i}", [P, NCH * 512], BF16) for i in range(2)]
    psg = bufs.psum(f"{tag}_psg", [P, 512], F32)
    psu = bufs.psum(f"{tag}_psu", [P, 512], F32)
    hsil = bufs.sbuf(f"{tag}_hsil", [P, 512], F32)
    htm = bufs.sbuf(f"{tag}_htm", [P, 512], F32)
    hqbf = bufs.sbuf(f"{tag}_hqbf", [P, 512], BF16)
    htev = bufs.sbuf(f"{tag}_htev", [P, 22 * 128], BF16)

    mstarts = list(range(0, L0I, 512))
    tblocks = [(s, min(512, R - s)) for s in range(0, R, 512)]

    wb_free = [None, None]
    xq_free = [None, None]
    htev_free = None
    psgu_free = None
    hqbf_free = None
    tok_hq_done = None
    last_mm = None

    for mi, ms in enumerate(mstarts):
        mw = min(512, L0I - ms)
        nchk = (mw + P - 1) // P
        wb = wbuf[mi % 2]
        twg = ch.dma("sync", wb["g"][:, :NCH * mw].rearrange(
            "p (c j) -> p c j", j=mw),
            bass.AP(wgT.tensor, ms, [[L0I, P], [P * L0I, NCH], [1, mw]]),
            deps=[wb_free[mi % 2]])
        twu = ch.dma("sync", wb["u"][:, :NCH * mw].rearrange(
            "p (c j) -> p c j", j=mw),
            bass.AP(wuT.tensor, ms, [[L0I, P], [P * L0I, NCH], [1, mw]]),
            deps=[wb_free[mi % 2]])
        for ti, (ts_, tw) in enumerate(tblocks):
            xb = xqb[ti % 2]
            tx = ch.dma("sync", xb[:, :NCH * tw].rearrange(
                "p (c j) -> p c j", j=tw),
                bass.AP(xq_d.tensor, ts_, [[R, P], [P * R, NCH], [1, tw]]),
                deps=[xq_free[ti % 2], tok_xq_done])
            ntt = (tw + P - 1) // P
            for tj in range(ntt):
                tjw = min(P, tw - tj * P)

                def mm(nc, xb=xb, wb=wb, tj=tj, tjw=tjw, mw=mw, tw=tw):
                    i_ = None
                    for c in range(NCH):
                        nc.tensor.matmul(
                            psg[:tjw, :mw],
                            xb[:, c * tw + tj * P: c * tw + tj * P + tjw],
                            wb["g"][:, c * mw:(c + 1) * mw],
                            start=(c == 0), stop=(c == NCH - 1))
                    for c in range(NCH):
                        i_ = nc.tensor.matmul(
                            psu[:tjw, :mw],
                            xb[:, c * tw + tj * P: c * tw + tj * P + tjw],
                            wb["u"][:, c * mw:(c + 1) * mw],
                            start=(c == 0), stop=(c == NCH - 1))
                    return i_
                tm = ch.step("tensor", mm, deps=[tx, twg, twu, psgu_free])
                tsil = ch.step("scalar", lambda nc, tjw=tjw, mw=mw:
                               nc.scalar.activation(out=hsil[:tjw, :mw],
                                                    in_=psg[:tjw, :mw],
                                                    func=AF.Silu),
                               deps=[tm])
                tmul = ch.step("vector", lambda nc, tjw=tjw, mw=mw:
                               nc.vector.tensor_tensor(
                                   htm[:tjw, :mw], psu[:tjw, :mw],
                                   hsil[:tjw, :mw], ALU.mult),
                               deps=[tsil, tm])
                psgu_free = tmul
                tq2 = emit_quant(ch, qh, htm, hqbf, mw,
                                 [tmul, hqbf_free])
                evs = []
                for c in range(nchk):
                    cw = min(P, mw - c * P)
                    th = ch.step("tensor",
                                 lambda nc, c=c, cw=cw, tjw=tjw:
                                 nc.tensor.transpose(
                                     pst[:cw, :tjw],
                                     hqbf[:tjw, c * P:c * P + cw], ident[:]),
                                 deps=[tq2, pst_free, htev_free])
                    ev = ch.step("vector",
                                 lambda nc, c=c, cw=cw, tj=tj, tjw=tjw, tw=tw:
                                 nc.vector.tensor_copy(
                                     htev[:cw, c * tw + tj * P:
                                          c * tw + tj * P + tjw],
                                     pst[:cw, :tjw]),
                                 deps=[th])
                    pst_free = ev
                    evs.append(ev)
                hqbf_free = evs[-1]
            htev_free = ch.dma(
                "gpsimd",
                bass.AP(hq_d.tensor, ms * R + ts_,
                        [[R, P], [P * R, nchk], [1, tw]]),
                htev[:, :nchk * tw].rearrange("p (c t) -> p c t", t=tw),
                deps=[hqbf_free])
            tok_hq_done = htev_free
            xq_free[ti % 2] = tm
        wb_free[mi % 2] = last_mm = tm

    bufs.release(f"{tag}_psu", f"{tag}_psg", f"{tag}_pst")
    wdb = [bufs.sbuf(f"{tag}_wd{i}", [P, DM], BF16) for i in range(2)]
    hqb = [bufs.sbuf(f"{tag}_hqb{i}", [P, 512], BF16) for i in range(2)]
    psd = [bufs.psum(f"{tag}_psd{i}", [P, 512], F32) for i in range(8)]
    xrt = bufs.sbuf(f"{tag}_xrt", [P, DM], F32)
    n_hf = L0I // P

    hqb_free = [None, None]
    wdb_free = [None, None]
    psd_free = None
    xrt_free = None
    for ti, (ts_, tw) in enumerate(tblocks):
        ntt = (tw + P - 1) // P
        mm_toks = []
        for hf in range(n_hf):
            hb = hqb[hf % 2]
            wb2 = wdb[hf % 2]
            th = ch.dma("sync", hb[:, :tw],
                        bass.AP(hq_d.tensor, hf * P * R + ts_,
                                [[R, P], [1, tw]]),
                        deps=[hqb_free[hf % 2], tok_hq_done])
            tw_ = ch.dma("sync", wb2[:],
                         bass.AP(wdT.tensor, hf * P * DM, [[DM, P], [1, DM]]),
                         deps=[wdb_free[hf % 2]])

            def mmd(nc, hb=hb, wb2=wb2, hf=hf, ntt=ntt, tw=tw):
                i_ = None
                for tj in range(ntt):
                    tjw = min(P, tw - tj * P)
                    for nh in range(2):
                        i_ = nc.tensor.matmul(
                            psd[tj * 2 + nh][:tjw, :],
                            hb[:, tj * P:tj * P + tjw],
                            wb2[:, nh * 512:(nh + 1) * 512],
                            start=(hf == 0), stop=(hf == n_hf - 1))
                return i_
            dd = [th, tw_]
            if hf == 0:
                dd.append(psd_free)
            tmm = ch.step("tensor", mmd, deps=dd)
            hqb_free[hf % 2] = tmm
            wdb_free[hf % 2] = tmm
            mm_toks.append(tmm)
        last_adds = []
        for tj in range(ntt):
            tjw = min(P, tw - tj * P)
            row0 = ts_ + tj * P
            dxr = ch.dma("sync", xrt[:tjw, :], x_in[row0:row0 + tjw, :],
                         deps=[mm_toks[-1], xrt_free])
            tadds = []
            for nh in range(2):
                ta = ch.step("vector",
                             lambda nc, tj=tj, nh=nh, tjw=tjw:
                             nc.vector.scalar_tensor_tensor(
                                 out=xrt[:tjw, nh * 512:(nh + 1) * 512],
                                 in0=psd[tj * 2 + nh][:tjw, :],
                                 scalar=scale,
                                 in1=xrt[:tjw, nh * 512:(nh + 1) * 512],
                                 op0=ALU.mult, op1=ALU.add),
                             deps=[dxr, mm_toks[-1]])
                tadds.append(ta)
            xrt_free = ch.dma("gpsimd", x_out[row0:row0 + tjw, :],
                              xrt[:tjw, :], deps=tadds)
            last_adds = tadds
        psd_free = last_adds[-1]


def finish(ch):
    toks = [(ch.dsem[k], ch.dcnt[k]) for k in ch.dsem]
    ch.wait_only("sync", toks)



def build_phaseA(R, debug_scratch=False):
    nc = bass.Bass("TRN2", target_bir_lowering=False,
                   detect_race_conditions=False)
    x_in = nc.dram_tensor("x_in", [R, DM], F32, kind="ExternalInput").ap()
    wgT = nc.dram_tensor("wgT", [DM, L0I], BF16, kind="ExternalInput").ap()
    wuT = nc.dram_tensor("wuT", [DM, L0I], BF16, kind="ExternalInput").ap()
    wdT = nc.dram_tensor("wdT", [L0I, DM], BF16, kind="ExternalInput").ap()
    c_ident = nc.dram_tensor("c_ident", [P, P], BF16, kind="ExternalInput").ap()
    x_out = nc.dram_tensor("x_out", [R, DM], F32, kind="ExternalOutput").ap()
    kind = "ExternalOutput" if debug_scratch else "Internal"
    xq_d = nc.dram_tensor("xq_scr", [DM, R], BF16, kind=kind).ap()
    hq_d = nc.dram_tensor("hq_scr", [L0I, R], BF16, kind=kind).ap()

    with ExitStack() as ctx:
        bufs = Bufs(nc, ctx)
        ch = Chain(nc, ctx, "A")
        ident = bufs.sbuf("sb_ident", [P, P], BF16)
        block = ctx.enter_context(nc.Block())
        ti = ch.dma("sync", ident[:], c_ident)
        ch.wait_only("tensor", [ti])
        emit_l0ffn_phase(ch, bufs, R, x_in, x_out, wgT, wuT, wdT,
                         xq_d, hq_d, "A")
        finish(ch)
        ch.flush(block)
    return nc



DELTA = 32
NH = 16
BD = 16


def emit_l0attn_prep(ch, bufs, aps, tag="Bp"):
    nc = ch.nc
    out = {}
    ident = bufs.sbuf("sb_ident", [P, P], BF16)
    identf = bufs.sbuf("sb_identf", [P, P], F32)
    toks = []
    t0 = ch.dma("sync", ident[:], aps["c_ident"])
    t0b = ch.dma("sync", identf[:], aps["c_identf"])
    toks += [t0, t0b]
    wdisp = bufs.sbuf(f"{tag}_wdisp", [P, NCH * 256], BF16)
    wval = bufs.sbuf(f"{tag}_wval", [P, NCH * DM], BF16)
    wcp = bufs.sbuf(f"{tag}_wcp", [P, NCH * DM], BF16)
    for (nm, src, ncol, dstbuf) in (
            ("wdisp", aps["w_disp"], 256, wdisp),
            ("wval", aps["w_val"], DM, wval),
            ("wcproj", aps["w_cproj"], DM, wcp)):
        scr = ch.nc.dram_tensor(f"{tag}_{nm}_scr", [DM, ncol], BF16).ap()
        tp = emit_weight_prep(ch, bufs, src, scr, DM, ncol, f"{tag}_{nm}",
                              quant=False, transpose=False, deps=[])
        tl = ch.dma("sync", dstbuf[:, :NCH * ncol].rearrange(
            "p (c j) -> p c j", j=ncol),
            bass.AP(scr.tensor, 0, [[ncol, P], [P * ncol, NCH], [1, ncol]]),
            deps=[tp])
        toks.append(tl)
    out["wdisp"], out["wval"], out["wcp"] = wdisp, wval, wcp
    for nm, shape in (("b_disp2", [P, 2]), ("b_val_bcast", [P, DM]),
                      ("bbo_t", [P, 1]),
                      ("b_cproj8", [P, NCH]), ("b_state_t", [P, 1]),
                      ("b_damage_t", [P, 1]), ("cmask_t0", [P, NH * DELTA]),
                      ("invcnt_t0", [P, 640]), ("halo_mask", [P, P]),
                      ("rep8", [BD, P])):
        out[nm] = bufs.sbuf(f"{tag}_{nm}", shape, F32)
        toks.append(ch.dma("sync", out[nm][:], aps[nm]))
    wsm = bufs.sbuf(f"{tag}_wsmall", [BD, 6 * BD], F32)
    wsmb = bufs.sbuf(f"{tag}_wsmallb", [BD, 6 * BD], BF16)
    small = [("w_state", BD), ("w_strain", BD), ("w_state_i", BD),
             ("w_state_j", BD), ("w_damage", BD), ("w_pos", BD)]
    tload = []
    for i, (nm, w_) in enumerate(small):
        tload.append(ch.dma("sync", wsm[:, i * BD:(i + 1) * BD], aps[nm]))
    tc = ch.step("vector", lambda nc: nc.vector.tensor_copy(wsmb[:], wsm[:]),
                 deps=tload)
    bd_names = ("bd_state", "bd_strain", "bd_si", "bd_sj", "bd_damage")
    for j, bnm in enumerate(bd_names):
        bt = bufs.sbuf(f"{tag}_{bnm}", [P, P], BF16)
        out[bnm] = bt
        tm_ = ch.step("vector", lambda nc, bt=bt: nc.vector.memset(bt[:], 0.0),
                      deps=[])
        prev = tm_
        for i in range(8):
            prev = ch.dma("gpsimd", bt[i * BD:(i + 1) * BD, i * BD:(i + 1) * BD],
                          wsmb[:, j * BD:(j + 1) * BD], deps=[tc, tm_],
                          key=f"{tag}_{bnm}")
        toks.append(prev)
    wvo = bufs.sbuf(f"{tag}_wvo", [BD, 2], F32)
    wvob = bufs.sbuf(f"{tag}_wvob", [BD, 2], BF16)
    tv1 = ch.dma("sync", wvo[:, 0:1], aps["w_bond_out"])
    tv2 = ch.dma("sync", wvo[:, 1:2], aps["w_damage_out"])
    tvc = ch.step("vector", lambda nc: nc.vector.tensor_copy(wvob[:], wvo[:]),
                  deps=[tv1, tv2])
    for j, bnm in enumerate(("bv_bond", "bv_damage")):
        bt = bufs.sbuf(f"{tag}_{bnm}", [P, 8], BF16)
        out[bnm] = bt
        tm_ = ch.step("vector", lambda nc, bt=bt: nc.vector.memset(bt[:], 0.0),
                      deps=[])
        prev = tm_
        for i in range(8):
            prev = ch.dma("gpsimd", bt[i * BD:(i + 1) * BD, i:i + 1],
                          wvob[:, j:j + 1], deps=[tvc, tm_],
                          key=f"{tag}_{bnm}")
        toks.append(prev)
    rel = bufs.sbuf(f"{tag}_rel", [DELTA, BD], F32)
    bstr16 = bufs.sbuf(f"{tag}_bstr16", [BD, 1], F32)
    relT_s = bufs.sbuf(f"{tag}_relT", [BD, DELTA], F32)
    relW_s = bufs.sbuf(f"{tag}_relW", [BD, DELTA], F32)
    bias_bond = bufs.sbuf(f"{tag}_bias_bond", [P, DELTA], F32)
    out["bias_bond"] = bias_bond
    bdo1 = bufs.sbuf(f"{tag}_bdo1", [1, 1], F32)
    ones_r = bufs.sbuf(f"{tag}_ones_r", [1, P], F32)
    bdo = bufs.sbuf(f"{tag}_bdo", [P, 1], F32)
    out["bdo"] = bdo
    ps_small = bufs.psum(f"{tag}_ps_small", [P, DELTA], F32)
    tr1 = ch.dma("sync", rel[:], aps["rel_pos_emb"])
    tr2 = ch.dma("sync", bstr16[:], aps["b_strain16"])
    tt1 = ch.step("tensor", lambda nc: nc.tensor.transpose(
        ps_small[:BD, :DELTA], rel[:], identf[:DELTA, :DELTA]),
        deps=[tr1, t0b])
    te1 = ch.step("vector", lambda nc: nc.vector.tensor_copy(
        relT_s[:], ps_small[:BD, :DELTA]), deps=[tt1])
    tt2 = ch.step("tensor", lambda nc: nc.tensor.matmul(
        ps_small[:BD, :DELTA], wsm[:, 5 * BD:6 * BD], relT_s[:],
        start=True, stop=True), deps=[te1, tc])
    te2 = ch.step("vector", lambda nc: nc.vector.tensor_scalar(
        out=relW_s[:], in0=ps_small[:BD, :DELTA], scalar1=bstr16[:],
        scalar2=None, op0=ALU.add), deps=[tt2, tr2])
    tt3 = ch.step("tensor", lambda nc: nc.tensor.matmul(
        ps_small[:, :DELTA], out["rep8"][:], relW_s[:],
        start=True, stop=True), deps=[te2])
    te3 = ch.step("vector", lambda nc: nc.vector.tensor_copy(
        bias_bond[:], ps_small[:, :DELTA]), deps=[tt3])
    toks.append(te3)
    td1 = ch.dma("sync", bdo1[:], aps["b_damage_out"])
    td2 = ch.dma("sync", ones_r[:], aps["ones_row"])
    tt4 = ch.step("tensor", lambda nc: nc.tensor.matmul(
        ps_small[:, 0:1], ones_r[:], bdo1[:], start=True, stop=True),
        deps=[td1, td2, te3])
    te4 = ch.step("vector", lambda nc: nc.vector.tensor_copy(
        bdo[:], ps_small[:, 0:1]), deps=[tt4])
    toks.append(te4)
    bufs.release(f"{tag}_ps_small")
    out["ident"] = ident
    out["identf"] = identf
    out["tok"] = toks
    return out


def emit_l0attn_phase(ch, bufs, x_in, x_out, pr, tag="B"):
    nc = ch.nc
    ident = pr["ident"]
    ln = ln_scratch(bufs, f"{tag}ln")
    TB = 512
    W = TB + P
    NXT = 5

    xt = [bufs.sbuf(f"{tag}_xt{i}", [P, DM], F32) for i in range(2)]
    xlnt = bufs.sbuf(f"{tag}_xln", [P, DM], F32)
    xlnb = bufs.sbuf(f"{tag}_xlnb", [P, DM], BF16)
    xfm = bufs.sbuf(f"{tag}_xfm", [P, NCH * W], BF16)
    dispf = bufs.sbuf(f"{tag}_dispf", [P, 2 * W], BF16)
    valt = bufs.sbuf(f"{tag}_valt", [P, NXT * DM], BF16)
    dW = bufs.sbuf(f"{tag}_dW", [P, 2 * W], F32)
    dWs = bufs.sbuf(f"{tag}_dWs", [P, 2 * W], F32)
    dWd = bufs.sbuf(f"{tag}_dWd", [P, 2 * W], F32)
    ss1 = bufs.sbuf(f"{tag}_ss1", [P, 2 * 576], F32)
    ss2 = bufs.sbuf(f"{tag}_ss2", [P, 2 * 576], F32)
    stateb = bufs.sbuf(f"{tag}_stateb", [P, 2 * 544], BF16)
    U = bufs.sbuf(f"{tag}_U", [P, 2 * 544], F32)
    C = bufs.sbuf(f"{tag}_C", [P, 2 * TB], F32)
    pre = bufs.sbuf(f"{tag}_pre", [P, 2 * TB], F32)
    bondb = bufs.sbuf(f"{tag}_bondb", [P, 2 * TB], BF16)
    damb = bufs.sbuf(f"{tag}_damb", [P, 2 * TB], BF16)
    damf = bufs.sbuf(f"{tag}_damf", [P, NH], F32)
    logits = bufs.sbuf(f"{tag}_logits", [P, 4 * NH * DELTA], F32)
    lmax = bufs.sbuf(f"{tag}_lmax", [P, NH], F32)
    lsum = bufs.sbuf(f"{tag}_lsum", [P, NH], F32)
    lrec = bufs.sbuf(f"{tag}_lrec", [P, NH], F32)
    ltmp = bufs.sbuf(f"{tag}_ltmp", [P, NH], F32)
    wbf = bufs.sbuf(f"{tag}_wbf", [P, NH * DELTA], BF16)
    sksb = bufs.sbuf(f"{tag}_sksb", [P, NH * 256], BF16)
    b1sb = bufs.sbuf(f"{tag}_b1sb", [P, P], BF16)
    b2sb = bufs.sbuf(f"{tag}_b2sb", [P, P], BF16)
    ofm = bufs.sbuf(f"{tag}_ofm", [P, NCH * TB], BF16)
    yfm = bufs.sbuf(f"{tag}_yfm", [P, NCH * TB], BF16)
    xrt = bufs.sbuf(f"{tag}_xrt", [P, DM], F32)
    zrow = bufs.sbuf(f"{tag}_zrow", [P, 257], BF16)

    ps_mm = bufs.psum(f"{tag}_psmm", [P, 512], F32)
    ps_b = bufs.psum(f"{tag}_psb", [P, NH], F32)
    ps_d = bufs.psum(f"{tag}_psd", [P, NH], F32)
    ps_av = bufs.psum(f"{tag}_psav", [P, P], F32)
    pst = bufs.psum(f"{tag}_pst", [P, P], BF16)

    NZC = NH + 1
    skew = nc.dram_tensor(f"{tag}_skew", [NZC * P * 257], BF16).ap()

    tz0 = ch.step("vector", lambda nc: nc.vector.memset(zrow[:], 0.0),
                  deps=pr["tok"])
    tz = ch.dma("gpsimd",
                bass.AP(skew.tensor, 0,
                        [[257, P], [P * 257, NZC], [1, 257]]),
                bass.AP(zrow, 0, [[257, P], [0, NZC], [1, 257]]),
                deps=[tz0], key="skewz")

    prev = dict()

    for bt in range(TOK // TB):
        r0 = bt * TB
        for xi in range(NXT):
            xb = xt[xi % 2]
            dx = ch.dma("sync", xb[:], x_in[r0 + xi * P: r0 + (xi + 1) * P, :],
                        deps=[prev.get(("xt", xi % 2))] + (pr["tok"] if bt == 0 else []))
            tl = emit_ln(ch, ln, xb, xlnt, dx)
            prev[("xt", xi % 2)] = tl
            tcst = ch.step("vector", lambda nc: nc.vector.tensor_copy(
                xlnb[:], xlnt[:]), deps=[tl, prev.get("xlnb")])
            evs = []
            for c in range(NCH):
                tt = ch.step("tensor", lambda nc, c=c: nc.tensor.transpose(
                    pst[:], xlnb[:, c * P:(c + 1) * P], ident[:]),
                    deps=[tcst, prev.get("pst")])
                ev = ch.step("vector",
                             lambda nc, c=c, xi=xi: nc.vector.tensor_copy(
                                 xfm[:, c * W + xi * P: c * W + (xi + 1) * P],
                                 pst[:]), deps=[tt, prev.get("xfm_free")])
                prev["pst"] = ev
                evs.append(ev)
            prev["xlnb"] = evs[-1]
            prev[("xfm", xi)] = evs[-1]
            for vc in range(2):
                def vmm(nc, xi=xi, vc=vc):
                    i_ = None
                    for c in range(NCH):
                        i_ = nc.tensor.matmul(
                            ps_mm[:, :512],
                            xfm[:, c * W + xi * P: c * W + (xi + 1) * P],
                            pr["wval"][:, c * DM + vc * 512:
                                       c * DM + (vc + 1) * 512],
                            start=(c == 0), stop=(c == NCH - 1))
                    return i_
                tv = ch.step("tensor", vmm,
                             deps=evs + [prev.get("ps_mm")])
                ev2 = ch.step("vector", lambda nc, xi=xi, vc=vc:
                              nc.vector.tensor_tensor(
                                  valt[:, xi * DM + vc * 512:
                                       xi * DM + (vc + 1) * 512],
                                  ps_mm[:, :512],
                                  pr["b_val_bcast"][:, vc * 512:(vc + 1) * 512],
                                  ALU.add),
                              deps=[tv, prev.get(("valt", xi))])
                prev["ps_mm"] = ev2
            prev[("valt", xi)] = prev["ps_mm"]
        for mc in range(2):
            for (s0, sw) in ((0, 512), (512, P)):
                def dmm(nc, mc=mc, s0=s0, sw=sw):
                    i_ = None
                    for c in range(NCH):
                        i_ = nc.tensor.matmul(
                            ps_mm[:, :sw],
                            pr["wdisp"][:, c * 256 + mc * P:
                                        c * 256 + (mc + 1) * P],
                            xfm[:, c * W + s0: c * W + s0 + sw],
                            start=(c == 0), stop=(c == NCH - 1))
                    return i_
                tdm = ch.step("tensor", dmm,
                              deps=[prev[("xfm", 4)], prev.get("ps_mm")])
                prev["xfm_free"] = tdm
                edm = ch.step("vector", lambda nc, mc=mc, s0=s0, sw=sw:
                              nc.vector.tensor_scalar(
                                  out=dispf[:, mc * W + s0: mc * W + s0 + sw],
                                  in0=ps_mm[:, :sw],
                                  scalar1=pr["b_disp2"][:, mc:mc + 1],
                                  scalar2=None, op0=ALU.add),
                              deps=[tdm, prev.get("dispf_free")])
                prev["ps_mm"] = edm
        prev["dispf"] = prev["ps_mm"]
        for (dst, bdw) in ((dW, "bd_state"), (dWs, "bd_strain"),
                           (dWd, "bd_damage")):
            for mc in range(2):
                for (s0, sw) in ((0, 512), (512, P)):
                    def bmm(nc, bdw=bdw, mc=mc, s0=s0, sw=sw):
                        return nc.tensor.matmul(
                            ps_mm[:, :sw], pr[bdw][:],
                            dispf[:, mc * W + s0: mc * W + s0 + sw],
                            start=True, stop=True)
                    tb = ch.step("tensor", bmm,
                                 deps=[prev["dispf"], prev.get("ps_mm")])
                    prev["dispf_free"] = tb
                    eb = ch.step("scalar", lambda nc, dst=dst, mc=mc, s0=s0,
                                 sw=sw: nc.scalar.activation(
                                     out=dst[:, mc * W + s0: mc * W + s0 + sw],
                                     in_=ps_mm[:, :sw], func=AF.Copy),
                                 deps=[tb, prev.get(("dw", dst.name))])
                    prev["ps_mm"] = eb
            prev[("dw", dst.name)] = prev["ps_mm"]
        if bt == 0:
            thm = ch.step("vector", lambda nc: nc.vector.tensor_tensor(
                dW[:, 0:P], dW[:, 0:P], pr["halo_mask"][:], ALU.mult),
                deps=[prev[("dw", dW.name)]])
            thm2 = ch.step("vector", lambda nc: nc.vector.tensor_tensor(
                dW[:, W:W + P], dW[:, W:W + P], pr["halo_mask"][:], ALU.mult),
                deps=[thm])
            prev[("dw", dW.name)] = thm2
        cur, oth = ss1, ss2
        SSW = 576
        t_ = ch.step("vector", lambda nc: [
            nc.vector.tensor_copy(cur[:, mc * SSW: mc * SSW + SSW],
                                  dW[:, mc * W + 64: mc * W + 640])
            for mc in range(2)][-1],
            deps=[prev[("dw", dW.name)], prev.get("ss")])
        cur, oth = oth, cur
        for k in range(1, 6):
            sh = 1 << (k - 1)
            def sfix(nc, sh=sh, cur=cur, oth=oth):
                i_ = None
                for mc in range(2):
                    nc.vector.tensor_copy(
                        cur[:, mc * SSW: mc * SSW + sh],
                        oth[:, mc * SSW: mc * SSW + sh])
                    i_ = nc.vector.tensor_tensor(
                        cur[:, mc * SSW + sh: mc * SSW + SSW],
                        oth[:, mc * SSW + sh: mc * SSW + SSW],
                        oth[:, mc * SSW: mc * SSW + SSW - sh], ALU.add)
                return i_
            t_ = ch.step("vector", sfix, deps=[t_])
            cur, oth = oth, cur
        slide = oth
        if bt == 0:
            t_ = ch.step("vector", lambda nc, slide=slide: [
                nc.vector.tensor_tensor(
                    slide[:, mc * SSW + 32: mc * SSW + SSW],
                    slide[:, mc * SSW + 32: mc * SSW + SSW],
                    pr["invcnt_t0"][:, 96:640], ALU.mult)
                for mc in range(2)][-1], deps=[t_])
        else:
            t_ = ch.step("vector", lambda nc, slide=slide: [
                nc.vector.tensor_single_scalar(
                    slide[:, mc * SSW + 32: mc * SSW + SSW],
                    slide[:, mc * SSW + 32: mc * SSW + SSW],
                    1.0 / 32.0, ALU.mult)
                for mc in range(2)][-1], deps=[t_])
        def stfin(nc, slide=slide):
            i_ = None
            for mc in range(2):
                nc.vector.scalar_tensor_tensor(
                    out=slide[:, mc * SSW + 32: mc * SSW + SSW],
                    in0=dW[:, mc * W + 96: mc * W + 640], scalar=-1.0,
                    in1=slide[:, mc * SSW + 32: mc * SSW + SSW],
                    op0=ALU.mult, op1=ALU.add)
                i_ = nc.vector.tensor_scalar(
                    out=stateb[:, mc * 544: mc * 544 + 544],
                    in0=slide[:, mc * SSW + 32: mc * SSW + SSW],
                    scalar1=pr["b_state_t"][:], scalar2=None, op0=ALU.add)
            return i_
        t_ = ch.step("vector", stfin, deps=[t_, prev.get("stateb")])
        prev["stateb"] = t_
        prev["ss"] = t_
        prev[("dw", dW.name)] = t_
        for mc in range(2):
            for (s0, sw) in ((0, 512), (512, 32)):
                tb = ch.step("tensor", lambda nc, mc=mc, s0=s0, sw=sw:
                             nc.tensor.matmul(
                                 ps_mm[:, :sw], pr["bd_sj"][:],
                                 stateb[:, mc * 544 + s0: mc * 544 + s0 + sw],
                                 start=True, stop=True),
                             deps=[prev["stateb"], prev.get("ps_mm")])
                eb = ch.step("vector", lambda nc, mc=mc, s0=s0, sw=sw:
                             nc.vector.tensor_tensor(
                                 U[:, mc * 544 + s0: mc * 544 + s0 + sw],
                                 ps_mm[:, :sw],
                                 dWs[:, mc * W + 96 + s0:
                                     mc * W + 96 + s0 + sw], ALU.add),
                             deps=[tb, prev.get("U")])
                prev["ps_mm"] = eb
        prev["U"] = prev["ps_mm"]
        for mc in range(2):
            tb = ch.step("tensor", lambda nc, mc=mc: nc.tensor.matmul(
                ps_mm[:, :512], pr["bd_si"][:],
                stateb[:, mc * 544 + 32: mc * 544 + 544],
                start=True, stop=True),
                deps=[prev["stateb"], prev.get("ps_mm")])
            eb = ch.step("vector", lambda nc, mc=mc: nc.vector.scalar_tensor_tensor(
                out=C[:, mc * TB:(mc + 1) * TB],
                in0=dWs[:, mc * W + P: mc * W + W], scalar=-1.0,
                in1=ps_mm[:, :512], op0=ALU.mult, op1=ALU.add),
                deps=[tb, prev.get("C")])
            prev["ps_mm"] = eb
        prev["C"] = prev["ps_mm"]
        prev["stateb"] = prev["ps_mm"]
        prev[("dw", dWs.name)] = prev["ps_mm"]
        for sub in range(4):
            prev[("lg", sub)] = prev.get(("lg", sub))
        for d in range(DELTA):
            tp_ = ch.step("vector", lambda nc, d=d: [
                nc.vector.tensor_tensor(
                    pre[:, mc * TB:(mc + 1) * TB],
                    U[:, mc * 544 + 1 + d: mc * 544 + 513 + d],
                    C[:, mc * TB:(mc + 1) * TB], ALU.add)
                for mc in range(2)][-1],
                deps=[prev["U"], prev["C"], prev.get("pre")])
            tg = ch.step("scalar", lambda nc, d=d: [
                nc.scalar.activation(
                    out=bondb[:, mc * TB:(mc + 1) * TB],
                    in_=pre[:, mc * TB:(mc + 1) * TB], func=AF.Gelu,
                    bias=pr["bias_bond"][:, d:d + 1], scale=1.0)
                for mc in range(2)][-1], deps=[tp_, prev.get("bondb")])
            prev["pre"] = tg
            tdp = ch.step("vector", lambda nc, d=d: [
                nc.vector.tensor_tensor(
                    pre[:, mc * TB:(mc + 1) * TB],
                    dWd[:, mc * W + 97 + d: mc * W + 97 + d + 512],
                    dWd[:, mc * W + P: mc * W + W], ALU.subtract)
                for mc in range(2)][-1],
                deps=[tg, prev[("dw", dWd.name)], prev.get("pre")])
            tgd = ch.step("scalar", lambda nc: [
                nc.scalar.activation(
                    out=damb[:, mc * TB:(mc + 1) * TB],
                    in_=pre[:, mc * TB:(mc + 1) * TB], func=AF.Gelu,
                    bias=pr["b_damage_t"][:], scale=1.0)
                for mc in range(2)][-1], deps=[tdp, prev.get("damb")])
            prev["pre"] = tgd
            for sub in range(4):
                def pmm(nc, sub=sub):
                    for mc in range(2):
                        nc.tensor.matmul(
                            ps_b[:, mc * 8:(mc + 1) * 8],
                            bondb[:, mc * TB + sub * P: mc * TB + (sub + 1) * P],
                            pr["bv_bond"][:], start=True, stop=True)
                    i_ = None
                    for mc in range(2):
                        i_ = nc.tensor.matmul(
                            ps_d[:, mc * 8:(mc + 1) * 8],
                            damb[:, mc * TB + sub * P: mc * TB + (sub + 1) * P],
                            pr["bv_damage"][:], start=True, stop=True)
                    return i_
                tpm = ch.step("tensor", pmm,
                              deps=[tg, tgd, prev.get("ps_bd")])
                tsg = ch.step("scalar", lambda nc: nc.scalar.activation(
                    out=damf[:], in_=ps_d[:, :NH], func=AF.Sigmoid,
                    bias=pr["bdo"][:], scale=1.0),
                    deps=[tpm, prev.get("damf")])
                tlg = ch.step("vector", lambda nc, sub=sub, d=d:
                              nc.vector.scalar_tensor_tensor(
                                  out=bass.AP(logits,
                                          sub * NH * DELTA + d,
                                          [[4 * NH * DELTA, P], [DELTA, NH],
                                           [1, 1]]),
                                  in0=damf[:], scalar=-10.0,
                                  in1=ps_b[:, :NH], op0=ALU.mult, op1=ALU.add),
                              deps=[tsg, tpm, prev.get(("lg", sub))])
                prev["damf"] = tlg
                prev["ps_bd"] = tlg
                prev[("lg", sub)] = tlg
            prev["bondb"] = tpm
            prev["damb"] = tpm
            if d == DELTA - 1:
                prev["U"] = tp_
                prev["C"] = tp_
                prev[("dw", dWd.name)] = tdp
        for sub in range(4):
            lg = logits[:, sub * NH * DELTA:(sub + 1) * NH * DELTA]
            t_ = prev[("lg", sub)]
            if bt == 0 and sub == 0:
                t_ = ch.step("vector", lambda nc, lg=lg: nc.vector.tensor_tensor(
                    lg, lg, pr["cmask_t0"][:], ALU.add), deps=[t_])
            tm1 = ch.step("vector", lambda nc, lg=lg: nc.vector.tensor_reduce(
                lmax[:].unsqueeze(2), lg.rearrange("p (h d) -> p h d", d=DELTA),
                AXX, ALU.max), deps=[t_, prev.get("lmax")])
            tm1 = ch.step("vector", lambda nc: nc.vector.tensor_scalar(
                out=lmax[:], in0=lmax[:], scalar1=pr["bbo_t"][:],
                scalar2=None, op0=ALU.subtract), deps=[tm1])
            tm2 = ch.step("vector", lambda nc, lg=lg: nc.vector.tensor_tensor(
                lg.rearrange("p (h d) -> p h d", d=DELTA),
                lg.rearrange("p (h d) -> p h d", d=DELTA),
                lmax[:].unsqueeze(2).broadcast_to([P, NH, DELTA]),
                ALU.subtract), deps=[tm1])
            tm3 = ch.step("scalar", lambda nc, lg=lg: nc.scalar.activation(
                out=lg, in_=lg, func=AF.Exp), deps=[tm2])
            tm4 = ch.step("vector", lambda nc, lg=lg: nc.vector.tensor_reduce(
                lsum[:].unsqueeze(2), lg.rearrange("p (h d) -> p h d", d=DELTA),
                AXX, ALU.add), deps=[tm3, prev.get("lsum")])
            tm5 = emit_recip(ch, lrec, lsum, ltmp, NH, [tm4])
            tm6 = ch.step("vector", lambda nc, lg=lg: nc.vector.tensor_tensor(
                wbf[:].rearrange("p (h d) -> p h d", d=DELTA),
                lg.rearrange("p (h d) -> p h d", d=DELTA),
                lrec[:].unsqueeze(2).broadcast_to([P, NH, DELTA]),
                ALU.mult), deps=[tm5, prev.get("wbf")])
            prev["lmax"] = tm6
            prev["lsum"] = tm6
            prev[("lg", sub)] = tm6
            tw_ = ch.dma("gpsimd",
                         bass.AP(skew.tensor, 97,
                                 [[257, P], [P * 257, NH], [1, DELTA]]),
                         wbf[:].rearrange("p (h d) -> p h d", d=DELTA),
                         deps=[tm6, prev.get("skr", tz)], key="skew")
            trd = ch.dma("sync", sksb[:].rearrange("p (h c) -> p h c", c=256),
                         bass.AP(skew.tensor, 0,
                                 [[256, P], [P * 257, NH], [1, 256]]),
                         deps=[tw_, prev.get("sksb")], key="sksb")
            prev["wbf"] = tw_
            prev["skr"] = trd
            oevs = []
            for h in range(NH):
                tb1 = ch.step("tensor", lambda nc, h=h: nc.tensor.transpose(
                    pst[:], sksb[:, h * 256: h * 256 + P], ident[:]),
                    deps=[trd, prev.get("pst")])
                te1 = ch.step("vector", lambda nc: nc.vector.tensor_copy(
                    b1sb[:], pst[:]), deps=[tb1, prev.get("b1sb")])
                tb2 = ch.step("tensor", lambda nc, h=h: nc.tensor.transpose(
                    pst[:], sksb[:, h * 256 + P: h * 256 + 256], ident[:]),
                    deps=[te1])
                te2 = ch.step("vector", lambda nc: nc.vector.tensor_copy(
                    b2sb[:], pst[:]), deps=[tb2, prev.get("b2sb")])
                prev["pst"] = te2
                def avmm(nc, h=h, sub=sub):
                    nc.tensor.matmul(ps_av[:64, :],
                                     valt[:, sub * DM + h * 64:
                                          sub * DM + (h + 1) * 64],
                                     b1sb[:], start=True, stop=False)
                    return nc.tensor.matmul(
                        ps_av[:64, :],
                        valt[:, (sub + 1) * DM + h * 64:
                             (sub + 1) * DM + (h + 1) * 64],
                        b2sb[:], start=False, stop=True)
                tav = ch.step("tensor", avmm,
                              deps=[te1, te2, prev.get("ps_av"),
                                    prev.get(("valt", sub)),
                                    prev.get(("valt", sub + 1))])
                prev["b1sb"] = tav
                prev["b2sb"] = tav
                oev = ch.step("vector", lambda nc, h=h, sub=sub:
                              nc.vector.tensor_copy(
                                  ofm[(h % 2) * 64:(h % 2) * 64 + 64,
                                      (h // 2) * TB + sub * P:
                                      (h // 2) * TB + (sub + 1) * P],
                                  ps_av[:64, :]),
                              deps=[tav, prev.get(("ofm", h // 2))])
                prev["ps_av"] = oev
                prev[("ofm", h // 2)] = oev
                oevs.append(oev)
            prev["sksb"] = oevs[-1]
        for xi in range(NXT):
            prev[("valt", xi)] = prev["sksb"]
        ofm_toks = [prev[("ofm", c)] for c in range(NCH)]
        ecm_toks = []
        for ncx in range(NCH):
            def cmm(nc, ncx=ncx):
                i_ = None
                for c in range(NCH):
                    i_ = nc.tensor.matmul(
                        ps_mm[:, :512], pr["wcp"][:, c * DM + ncx * P:
                                                  c * DM + (ncx + 1) * P],
                        ofm[:, c * TB:(c + 1) * TB],
                        start=(c == 0), stop=(c == NCH - 1))
                return i_
            tcm = ch.step("tensor", cmm, deps=ofm_toks + [prev.get("ps_mm")])
            ecm = ch.step("vector", lambda nc, ncx=ncx: nc.vector.tensor_scalar(
                out=yfm[:, ncx * TB:(ncx + 1) * TB], in0=ps_mm[:, :512],
                scalar1=pr["b_cproj8"][:, ncx:ncx + 1], scalar2=None,
                op0=ALU.add), deps=[tcm, prev.get(("yfm", ncx))])
            prev["ps_mm"] = ecm
            ecm_toks.append(ecm)
        for c in range(NCH):
            prev[("ofm", c)] = ecm_toks[-1]
        for sub in range(4):
            dxr = ch.dma("sync", xrt[:],
                         x_in[P + r0 + sub * P: P + r0 + (sub + 1) * P, :],
                         deps=[prev.get("xrt_free")], key="xrt")
            tlast = None
            for ncx in range(NCH):
                ty = ch.step("tensor", lambda nc, ncx=ncx, sub=sub:
                             nc.tensor.transpose(
                                 pst[:], yfm[:, ncx * TB + sub * P:
                                             ncx * TB + (sub + 1) * P],
                                 ident[:]),
                             deps=[ecm_toks[ncx], prev.get("pst")])
                ta = ch.step("vector", lambda nc, ncx=ncx:
                             nc.vector.tensor_tensor(
                                 bass.AP(xrt, ncx * P, [[DM, P], [1, P]]),
                                 bass.AP(xrt, ncx * P, [[DM, P], [1, P]]),
                                 pst[:], ALU.add),
                             deps=[ty, dxr, tlast])
                prev["pst"] = ta
                tlast = ta
                if sub == 3:
                    prev[("yfm", ncx)] = ty
            prev["xrt_free"] = ch.dma(
                "gpsimd", x_out[r0 + sub * P: r0 + (sub + 1) * P, :],
                xrt[:], deps=[tlast], key="x_out")



def _np_fp4(x, block=16):
    shape = x.shape
    xb = x.reshape(shape[:-1] + (shape[-1] // block, block)).astype(np.float32)
    scale = np.maximum(np.abs(xb).max(-1, keepdims=True) / 6.0,
                       np.float32(1e-12)).astype(np.float32)
    xn = (xb / scale).astype(np.float32)
    GRID = np.array([0., .5, 1., 1.5, 2., 3., 4., 6.], np.float32)
    MID = np.array([.25, .75, 1.25, 1.75, 2.5, 3.5, 5.], np.float32)
    idx = np.searchsorted(MID, np.abs(xn))
    q = (GRID[idx] * np.sign(xn) * scale).astype(np.float32)
    return q.reshape(shape)



def emit_l1ffn_phase(ch, bufs, x_in, x_out, w1T, w2T, h1T, h2T, w3T,
                     sc, h0_d, hA_d, hB_d, tag="D"):
    nc = ch.nc
    ident = bufs.sbuf("sb_ident", [P, P], BF16)
    ln = ln_scratch(bufs, f"{tag}ln")
    HALF = TOK // 2
    NT = HALF // P
    NHC = L1H // P

    xt = [bufs.sbuf(f"{tag}_xt{i}", [P, DM], F32) for i in range(2)]
    xlnt = bufs.sbuf(f"{tag}_xln", [P, DM], F32)
    xlnb = bufs.sbuf(f"{tag}_xlnb", [P, DM], BF16)
    hsb = bufs.sbuf(f"{tag}_hsb", [P, NHC * HALF], BF16)
    xfm = hsb[:, :NCH * HALF]
    wa = [bufs.sbuf(f"{tag}_wa{i}", [P, NHC * P], BF16) for i in range(2)]
    wb = [bufs.sbuf(f"{tag}_wb{i}", [P, NCH * P], BF16) for i in range(2)]
    hsil = [bufs.sbuf(f"{tag}_hsil{i}", [P, 512], F32) for i in range(2)]
    ev1 = bufs.sbuf(f"{tag}_ev1", [P, 512], BF16)
    ybf = bufs.sbuf(f"{tag}_ybf", [P, NCH * 512], BF16)
    xrt = bufs.sbuf(f"{tag}_xrt", [P, DM], F32)
    psg = [bufs.psum(f"{tag}_psg{i}", [P, 512], F32) for i in range(2)]
    psu = [bufs.psum(f"{tag}_psu{i}", [P, 512], F32) for i in range(2)]
    pst = bufs.psum(f"{tag}_pst", [P, P], BF16)

    prev = {}

    def ln_half(half, deps):
        last = None
        for xi in range(NT):
            r0 = half * HALF + xi * P
            xb = xt[xi % 2]
            dx = ch.dma("sync", xb[:], x_in[r0:r0 + P, :],
                        deps=[prev.get(("xt", xi % 2))] + deps)
            tl = emit_ln(ch, ln, xb, xlnt, dx)
            prev[("xt", xi % 2)] = tl
            tc = ch.step("vector", lambda nc: nc.vector.tensor_copy(
                xlnb[:], xlnt[:]), deps=[tl, prev.get("xlnb")])
            for c in range(NCH):
                tt = ch.step("tensor", lambda nc, c=c: nc.tensor.transpose(
                    pst[:], xlnb[:, c * P:(c + 1) * P], ident[:]),
                    deps=[tc, prev.get("pst")])
                evx = ch.step("vector",
                              lambda nc, c=c, xi=xi: nc.vector.tensor_copy(
                                  xfm[:, c * HALF + xi * P:
                                      c * HALF + (xi + 1) * P], pst[:]),
                              deps=[tt, prev.get("xfm")])
                prev["pst"] = evx
                last = evx
            prev["xlnb"] = last
        return last

    for half in range(2):
        tln = ln_half(half, [prev.get("xfm")])
        for m in range(NHC):
            wla = ch.dma("sync", wa[m % 2][:, :NCH * P].rearrange(
                "p (c j) -> p c j", j=P),
                bass.AP(w1T.tensor, m * P, [[L1H, P], [P * L1H, NCH], [1, P]]),
                deps=[prev.get(("wa", m % 2))])
            wlb = ch.dma("sync", wb[m % 2][:, :NCH * P].rearrange(
                "p (c j) -> p c j", j=P),
                bass.AP(w2T.tensor, m * P, [[L1H, P], [P * L1H, NCH], [1, P]]),
                deps=[prev.get(("wb", m % 2))])
            for t in range(4):
                par = t % 2
                def mm(nc, m=m, t=t, par=par):
                    i_ = None
                    for c in range(NCH):
                        nc.tensor.matmul(
                            psg[par][:], wa[m % 2][:, c * P:(c + 1) * P],
                            xfm[:, c * HALF + t * 512: c * HALF + (t + 1) * 512],
                            start=(c == 0), stop=(c == NCH - 1))
                    for c in range(NCH):
                        i_ = nc.tensor.matmul(
                            psu[par][:], wb[m % 2][:, c * P:(c + 1) * P],
                            xfm[:, c * HALF + t * 512: c * HALF + (t + 1) * 512],
                            start=(c == 0), stop=(c == NCH - 1))
                    return i_
                tm = ch.step("tensor", mm,
                             deps=[tln, wla, wlb, prev.get(("psgu", par))])
                tsl = ch.step("scalar",
                              lambda nc, m=m, par=par: nc.scalar.activation(
                                  out=hsil[par][:], in_=psg[par][:],
                                  func=AF.Silu, scale=sc["s1"][:, m:m + 1]),
                              deps=[tm, prev.get(("psgu", par))])
                tmu = ch.step("vector", lambda nc, m=m, par=par:
                              nc.vector.scalar_tensor_tensor(
                                  out=ev1[:], in0=psu[par][:],
                                  scalar=sc["s2"][:, m:m + 1],
                                  in1=hsil[par][:],
                                  op0=ALU.mult, op1=ALU.mult),
                              deps=[tsl, tm, prev.get("ev1")])
                prev[("psgu", par)] = tmu
                st_ = ch.dma("gpsimd",
                             bass.AP(h0_d.tensor, m * P * TOK + half * HALF
                                     + t * 512, [[TOK, P], [1, 512]]),
                             ev1[:], deps=[tmu], key="h0_d")
                prev["ev1"] = st_
            prev[("wa", m % 2)] = tm
            prev[("wb", m % 2)] = tm
        prev["xfm"] = tm
    tok_h0 = prev["ev1"]

    def square_stage(src_d, dst_d, wT, s_ap, dep0, stag):
        tok_done = None
        for half in range(2):
            hl = ch.dma("sync", hsb[:].rearrange("p (c j) -> p c j", j=HALF),
                        bass.AP(src_d.tensor, half * HALF,
                                [[TOK, P], [P * TOK, NHC], [1, HALF]]),
                        deps=[dep0, prev.get("hsb")])
            for m in range(NHC):
                wl = ch.dma("sync", wa[m % 2][:, :NHC * P].rearrange(
                    "p (c j) -> p c j", j=P),
                    bass.AP(wT.tensor, m * P,
                            [[L1H, P], [P * L1H, NHC], [1, P]]),
                    deps=[prev.get(("wa", m % 2))])
                for t in range(4):
                    par = t % 2
                    def mm2(nc, m=m, t=t, par=par):
                        i_ = None
                        for c in range(NHC):
                            i_ = nc.tensor.matmul(
                                psg[par][:], wa[m % 2][:, c * P:(c + 1) * P],
                                hsb[:, c * HALF + t * 512:
                                    c * HALF + (t + 1) * 512],
                                start=(c == 0), stop=(c == NHC - 1))
                        return i_
                    tm = ch.step("tensor", mm2,
                                 deps=[hl, wl, prev.get(("psgu", par))])
                    te = ch.step("scalar", lambda nc, m=m, par=par:
                                 nc.scalar.activation(
                                     out=ev1[:], in_=psg[par][:],
                                     func=AF.Gelu,
                                     scale=s_ap[:, m:m + 1]),
                                 deps=[tm, prev.get("ev1")])
                    prev[("psgu", par)] = te
                    st_ = ch.dma("gpsimd",
                                 bass.AP(dst_d.tensor, m * P * TOK
                                         + half * HALF + t * 512,
                                         [[TOK, P], [1, 512]]),
                                 ev1[:], deps=[te])
                    prev["ev1"] = st_
                    tok_done = st_
                prev[("wa", m % 2)] = tm
            prev["hsb"] = tm
        return tok_done

    tok_hA = square_stage(h0_d, hA_d, h1T, sc["sh1"], tok_h0, f"{tag}2")
    tok_hB = square_stage(hA_d, hB_d, h2T, sc["sh2"], tok_hA, f"{tag}3")

    for half in range(2):
        hl = ch.dma("sync", hsb[:].rearrange("p (c j) -> p c j", j=HALF),
                    bass.AP(hB_d.tensor, half * HALF,
                            [[TOK, P], [P * TOK, NHC], [1, HALF]]),
                    deps=[tok_hB, prev.get("hsb")])
        for t in range(4):
            ytoks = []
            for n in range(NCH):
                wl = ch.dma("sync", wa[n % 2][:, :NHC * P].rearrange(
                    "p (c j) -> p c j", j=P),
                    bass.AP(w3T.tensor, n * P,
                            [[DM, P], [P * DM, NHC], [1, P]]),
                    deps=[prev.get(("wa", n % 2))], key=f"{tag}4w")                     if t == 0 or True else None
                par = n % 2
                def mm3(nc, n=n, t=t, par=par):
                    i_ = None
                    for c in range(NHC):
                        i_ = nc.tensor.matmul(
                            psg[par][:], wa[n % 2][:, c * P:(c + 1) * P],
                            hsb[:, c * HALF + t * 512:
                                c * HALF + (t + 1) * 512],
                            start=(c == 0), stop=(c == NHC - 1))
                    return i_
                tm = ch.step("tensor", mm3,
                             deps=[hl, wl, prev.get(("psgu", par))])
                ty = ch.step("vector",
                             lambda nc, n=n, par=par: nc.vector.tensor_scalar(
                                 out=ybf[:, n * 512:(n + 1) * 512],
                                 in0=psg[par][:],
                                 scalar1=sc["s3h"][:, n:n + 1], scalar2=None,
                                 op0=ALU.mult),
                             deps=[tm, prev.get(("ybf", n))])
                prev[("psgu", par)] = ty
                prev[("wa", n % 2)] = tm
                ytoks.append(ty)
            for sub in range(4):
                r0 = half * HALF + t * 512 + sub * P
                dxr = ch.dma("sync", xrt[:], x_in[r0:r0 + P, :],
                             deps=[prev.get("xrt_free")])
                tlast = None
                for n in range(NCH):
                    tt = ch.step("tensor", lambda nc, n=n, sub=sub:
                                 nc.tensor.transpose(
                                     pst[:], ybf[:, n * 512 + sub * P:
                                                 n * 512 + (sub + 1) * P],
                                     ident[:]),
                                 deps=[ytoks[n], prev.get("pst")])
                    ta = ch.step("vector", lambda nc, n=n:
                                 nc.vector.tensor_tensor(
                                     bass.AP(xrt, n * P, [[DM, P], [1, P]]),
                                     bass.AP(xrt, n * P, [[DM, P], [1, P]]),
                                     pst[:], ALU.add),
                                 deps=[tt, dxr, tlast])
                    prev["pst"] = ta
                    tlast = ta
                    if sub == 3:
                        prev[("ybf", n)] = tt
                prev["xrt_free"] = ch.dma(
                    "gpsimd", x_out[r0:r0 + P, :], xrt[:],
                    deps=[tlast], key="xout")
        prev["hsb"] = tm



KVH = 8
HD = 64
CC_RG = [[0, 1], [2, 3], [4, 5], [6, 7]]

E_AP_NAMES = [
    ("Wq", [DM, DM], F32), ("Wk", [DM, 512], F32), ("Wv", [DM, 512], F32),
    ("Wo", [DM, DM], F32),
    ("cosq", [P, TOK], F32), ("sinq", [P, TOK], F32),
    ("cosk", [P, TOK], F32), ("sink", [P, TOK], F32),
    ("hv16", [P, 16 * NCH], BF16), ("hvk", [P, KVH * 4], BF16),
    ("sel16", [KVH, NH], BF16), ("ones1p", [1, P], F32),
    ("bo8", [P, NCH], F32),
]


def host_e_consts(inp, half):
    f = np.float32
    ROT = 32
    t = (np.arange(TOK) + half * TOK).astype(np.float64)
    inv = 1.0 / 10000.0 ** (np.arange(0, ROT, 2) / ROT)
    d = np.arange(P) % HD
    rot_mask = d < ROT
    fr = inv[(d // 2) % 16]
    ang = np.where(rot_mask[:, None], fr[:, None] * t[None, :], 0.0)
    cos = np.where(rot_mask[:, None], np.cos(ang), 1.0).astype(f)
    sin = np.where(rot_mask[:, None], np.sin(ang), 0.0).astype(f)
    SCALE = np.float32(HD ** -0.5)
    hv16 = np.zeros((P, 16 * NCH), np.float32)
    for mc in range(NCH):
        for r in range(2):
            hv16[r * HD:(r + 1) * HD, mc * 16 + 2 * mc + r] = 1.0
    hvk = np.zeros((P, KVH * 4), np.float32)
    for c4 in range(4):
        for r in range(2):
            hvk[r * HD:(r + 1) * HD, c4 * KVH + 2 * c4 + r] = 1.0
    sel16 = np.zeros((KVH, NH), np.float32)
    for h in range(NH):
        sel16[h // 2, h] = 1.0
    return dict(
        Wq=np.asarray(inp["L1_attn_Wq"], f), Wk=np.asarray(inp["L1_attn_Wk"], f),
        Wv=np.asarray(inp["L1_attn_Wv"], f), Wo=np.asarray(inp["L1_attn_Wo"], f),
        cosq=(cos * SCALE).astype(f), sinq=(sin * SCALE).astype(f),
        cosk=cos.astype(f), sink=sin.astype(f),
        hv16=hv16.astype(ml_dtypes.bfloat16),
        hvk=hvk.astype(ml_dtypes.bfloat16),
        sel16=sel16.astype(ml_dtypes.bfloat16),
        ones1p=np.ones((1, P), f),
        bo8=np.ascontiguousarray(
            np.asarray(inp["L1_attn_bo"], f).reshape(NCH, P).T),
    )


def emit_l1attn_phase(ch, bufs, x_in, x_out, aps, tag="E"):
    nc = ch.nc
    ident = bufs.sbuf("sb_ident", [P, P], BF16)
    identf = bufs.sbuf("sb_identf", [P, P], F32)
    ln = ln_scratch(bufs, f"{tag}ln")
    NSUB = TOK // 512

    q_d = nc.dram_tensor(f"{tag}_qstash", [DM, TOK], BF16).ap()
    k_in = nc.dram_tensor(f"{tag}_k_in", [512, TOK], BF16).ap()
    v_in = nc.dram_tensor(f"{tag}_v_in", [TOK, KVH * 65], BF16).ap()
    k_full = nc.dram_tensor(f"{tag}_k_full", [2 * 512, TOK], BF16).ap()
    v_full = nc.dram_tensor(f"{tag}_v_full", [2 * TOK, KVH * 65], BF16).ap()

    xt = [bufs.sbuf(f"{tag}_xt{i}", [P, DM], F32) for i in range(2)]
    xlnt = bufs.sbuf(f"{tag}_xln", [P, DM], F32)
    xlnb = bufs.sbuf(f"{tag}_xlnb", [P, DM], BF16)
    xfm = bufs.sbuf(f"{tag}_xfm", [P, NCH * 512], BF16)
    qa = bufs.sbuf(f"{tag}_qa", [P, 512], F32)
    qb_ = bufs.sbuf(f"{tag}_qb", [P, 512], F32)
    qpr = bufs.sbuf(f"{tag}_qpr", [P, 512], F32)
    sqb = bufs.sbuf(f"{tag}_sqb", [P, 512], BF16)
    qbf = bufs.sbuf(f"{tag}_qbf", [P, 512], BF16)
    nrm = bufs.sbuf(f"{tag}_nrm", [NH, NSUB], F32)
    nrmk = bufs.sbuf(f"{tag}_nrmk", [KVH, 16], F32)
    nq = bufs.sbuf(f"{tag}_nq", [NH, 1], F32)
    nk = bufs.sbuf(f"{tag}_nk", [KVH, 1], F32)
    nkb = bufs.sbuf(f"{tag}_nkb", [KVH, 1], BF16)
    nk16 = bufs.sbuf(f"{tag}_nk16", [NH, 1], F32)
    mrow = bufs.sbuf(f"{tag}_mrow", [NH, 1], F32)
    mrowT = bufs.sbuf(f"{tag}_mrowT", [1, NH], F32)
    mbias = bufs.sbuf(f"{tag}_mbias", [P, NH], F32)
    kdup = bufs.sbuf(f"{tag}_kdup", [P, TOK], BF16)
    vsb = bufs.sbuf(f"{tag}_vsb", [P, 64 * 65], BF16)
    qdup = bufs.sbuf(f"{tag}_qdup", [P, TOK], BF16)
    exps = [bufs.sbuf(f"{tag}_exps{i}", [P, 1024], BF16) for i in range(2)]
    rec = bufs.sbuf(f"{tag}_rec", [1, 512], F32)
    recr = bufs.sbuf(f"{tag}_recr", [1, 512], F32)
    rtmp = bufs.sbuf(f"{tag}_rtmp", [1, 512], F32)
    obc = bufs.sbuf(f"{tag}_obc", [P, 512], F32)
    vstash = bufs.sbuf(f"{tag}_vstash", [P, KVH * 65], BF16)
    ybf = bufs.sbuf(f"{tag}_ybf", [P, NCH * 512], BF16)
    xrt = bufs.sbuf(f"{tag}_xrt", [P, DM], F32)
    consts = {}
    for nm, shape, dt in E_AP_NAMES[8:]:
        consts[nm] = bufs.sbuf(f"{tag}_{nm}", shape, dt)
    rope = {nm: bufs.sbuf(f"{tag}_{nm}", [P, 512], F32)
            for nm in ("cosq", "sinq", "cosk", "sink")}
    wo_sb = bufs.sbuf(f"{tag}_wo", [P, NCH * DM], BF16)
    wq_sb = bufs.sbuf_tmp(f"{tag}_wq", [P, NCH * DM], BF16)
    wqr_sb = bufs.sbuf_tmp(f"{tag}_wqr", [P, NCH * DM], BF16)
    wk_sb = bufs.sbuf_tmp(f"{tag}_wk", [P, NCH * 512], BF16)
    wkr_sb = bufs.sbuf_tmp(f"{tag}_wkr", [P, NCH * 512], BF16)
    wv_sb = bufs.sbuf_tmp(f"{tag}_wv", [P, NCH * 512], BF16)
    wrow = bufs.sbuf_tmp(f"{tag}_wrow", [P, DM], F32)

    ps_s = [bufs.psum(f"{tag}_pss{i}", [P, 1024], F32) for i in range(2)]
    ps_n = bufs.psum(f"{tag}_psn", [P, 512], F32)
    ps_o = bufs.psum(f"{tag}_pso", [P, 512], F32)
    ps_c = bufs.psum(f"{tag}_psc", [P, 512], F32)
    pst = bufs.psum(f"{tag}_pst", [P, P], BF16)

    prev = {}
    toks0 = []
    for nm, shape, dt in E_AP_NAMES[8:]:
        toks0.append(ch.dma("sync", consts[nm][:], aps[nm]))

    def load_cast(dst, src, ncol):
        t_ = None
        for rt in range(DM // P):
            dl = ch.dma("sync", wrow[:, :ncol], src[rt * P:(rt + 1) * P, :],
                        deps=[prev.get("wrow")], key="wrow")
            t_ = ch.step("vector", lambda nc, rt=rt, ncol=ncol, dst=dst:
                         nc.vector.tensor_copy(
                             dst[:, rt * ncol:(rt + 1) * ncol],
                             wrow[:, :ncol]), deps=[dl])
            prev["wrow"] = t_
        return t_

    tq_ = load_cast(wq_sb, aps["Wq"], DM)
    tk_ = load_cast(wk_sb, aps["Wk"], 512)
    tv_ = load_cast(wv_sb, aps["Wv"], 512)
    to_ = load_cast(wo_sb, aps["Wo"], DM)

    def build_rot(dst, src, ncol, dep):
        ap = lambda buf, o: bass.AP(buf, o, [[NCH * ncol, P], [ncol, NCH],
                                             [HD, ncol // HD], [2, 16]])
        tz = ch.step("vector", lambda nc, dst=dst: nc.vector.memset(
            dst[:], 0.0), deps=[dep])
        te = ch.step("vector", lambda nc, dst=dst, src=src:
                     nc.vector.tensor_single_scalar(
                         ap(dst, 0), ap(src, 1), -1.0, ALU.mult), deps=[tz])
        return ch.step("vector", lambda nc, dst=dst, src=src:
                       nc.vector.tensor_copy(ap(dst, 1), ap(src, 0)),
                       deps=[te])

    trq = build_rot(wqr_sb, wq_sb, DM, tq_)
    trk = build_rot(wkr_sb, wk_sb, 512, tk_)

    tvones = ch.step("vector", lambda nc: nc.vector.memset(
        bass.AP(vstash, 64, [[KVH * 65, P], [65, KVH], [1, 1]]), 1.0),
        deps=toks0)

    tnm = None
    for sub in range(NSUB):
        last = None
        for xi in range(4):
            r0 = sub * 512 + xi * P
            xb = xt[xi % 2]
            dx = ch.dma("sync", xb[:], x_in[r0:r0 + P, :],
                        deps=[prev.get(("xt", xi % 2))] + (
                            toks0 + [tv_, trq, trk, to_] if sub == 0 else []))
            tl = emit_ln(ch, ln, xb, xlnt, dx)
            prev[("xt", xi % 2)] = tl
            tc = ch.step("vector", lambda nc: nc.vector.tensor_copy(
                xlnb[:], xlnt[:]), deps=[tl, prev.get("xlnb")])
            for c in range(NCH):
                tt = ch.step("tensor", lambda nc, c=c: nc.tensor.transpose(
                    pst[:], xlnb[:, c * P:(c + 1) * P], ident[:]),
                    deps=[tc, prev.get("pst")])
                evx = ch.step("vector",
                              lambda nc, c=c, xi=xi: nc.vector.tensor_copy(
                                  xfm[:, c * 512 + xi * P:
                                      c * 512 + (xi + 1) * P], pst[:]),
                              deps=[tt, prev.get("xfm_free")])
                prev["pst"] = evx
                last = evx
            prev["xlnb"] = last
        rtoks = []
        for nm in ("cosq", "sinq", "cosk", "sink"):
            rtoks.append(ch.dma("sync", rope[nm][:],
                                aps[nm][:, sub * 512:(sub + 1) * 512],
                                deps=[prev.get(("rp", nm))]))
        xfm_readers = []

        def rope_mm(wsb, wrsb, mc, ncol, cosn, sinn):
            def mm(nc, wsb=wsb, wrsb=wrsb, mc=mc, ncol=ncol):
                for c in range(NCH):
                    nc.tensor.matmul(ps_c[:], wsb[:, c * ncol + mc * P:
                                                  c * ncol + (mc + 1) * P],
                                     xfm[:, c * 512:(c + 1) * 512],
                                     start=(c == 0), stop=(c == NCH - 1))
                i_ = None
                for c in range(NCH):
                    i_ = nc.tensor.matmul(ps_n[:], wrsb[:, c * ncol + mc * P:
                                                        c * ncol + (mc + 1) * P],
                                          xfm[:, c * 512:(c + 1) * 512],
                                          start=(c == 0), stop=(c == NCH - 1))
                return i_
            tm = ch.step("tensor", mm,
                         deps=[last, prev.get("psab")] + rtoks)
            tra = ch.step("vector", lambda nc, cosn=cosn:
                          nc.vector.tensor_tensor(
                              qa[:], ps_c[:], rope[cosn][:], ALU.mult),
                          deps=[tm, prev.get("qa")])
            trb = ch.step("vector", lambda nc, sinn=sinn:
                          nc.vector.tensor_tensor(
                              qb_[:], ps_n[:], rope[sinn][:], ALU.mult),
                          deps=[tm, prev.get("qb")])
            prev["psab"] = trb
            tsum = ch.step("vector", lambda nc: nc.vector.tensor_add(
                qpr[:], qa[:], qb_[:]), deps=[tra, trb, prev.get("qpr")])
            prev["qa"] = tsum
            prev["qb"] = tsum
            tcq = ch.step("vector", lambda nc: nc.vector.tensor_copy(
                qbf[:], qpr[:]), deps=[tsum, prev.get("qbf")])
            return tm, tsum, tcq

        for mc in range(NCH):
            tm, tsum, tcq = rope_mm(wq_sb, wqr_sb, mc, DM, "cosq", "sinq")
            xfm_readers.append(tm)
            tsq = ch.step("scalar", lambda nc: nc.scalar.activation(
                out=sqb[:], in_=qpr[:], func=AF.Square),
                deps=[tsum, prev.get("sqb")])
            prev["qpr"] = tsq
            tnm = ch.step("tensor", lambda nc, mc=mc: nc.tensor.matmul(
                ps_o[:NH, :], consts["hv16"][:, mc * 16:(mc + 1) * 16],
                sqb[:], start=(mc == 0), stop=(mc == NCH - 1)),
                deps=[tsq, prev.get("pson")])
            prev["sqb"] = tnm
            prev["qbf"] = ch.dma(
                "gpsimd", bass.AP(q_d.tensor, mc * P * TOK + sub * 512,
                                  [[TOK, P], [1, 512]]),
                qbf[:], deps=[tcq], key="q_d")
        tnr = ch.step("vector", lambda nc, sub=sub: nc.vector.tensor_reduce(
            nrm[:, sub:sub + 1], ps_o[:NH, :], AXX, ALU.max),
            deps=[tnm, prev.get("nrm")])
        prev["pson"] = tnr
        prev["nrm"] = tnr
        for mc in range(4):
            tm, tsum, tcq = rope_mm(wk_sb, wkr_sb, mc, 512, "cosk", "sink")
            xfm_readers.append(tm)
            prev["qpr"] = tcq
            prev["qbf"] = ch.dma(
                "gpsimd", bass.AP(k_in.tensor, mc * P * TOK + sub * 512,
                                  [[TOK, P], [1, 512]]),
                qbf[:], deps=[tcq], key="k_in")
        for nm in ("cosq", "sinq", "cosk", "sink"):
            prev[("rp", nm)] = prev["qa"]
        for tj in range(4):
            def vmm(nc, tj=tj):
                i_ = None
                for c in range(NCH):
                    i_ = nc.tensor.matmul(
                        ps_c[:],
                        xfm[:, c * 512 + tj * P: c * 512 + (tj + 1) * P],
                        wv_sb[:, c * 512:(c + 1) * 512],
                        start=(c == 0), stop=(c == NCH - 1))
                return i_
            tvm = ch.step("tensor", vmm, deps=[last, prev.get("psab")])
            xfm_readers.append(tvm)
            tvc = ch.step("vector", lambda nc: nc.vector.tensor_copy(
                bass.AP(vstash, 0, [[KVH * 65, P], [65, KVH], [1, 64]]),
                ps_c[:].rearrange("p (k d) -> p k d", d=64)),
                deps=[tvm, prev.get("vstash"), tvones])
            prev["psab"] = tvc
            prev["vstash"] = ch.dma(
                "gpsimd", v_in[sub * 512 + tj * P: sub * 512 + (tj + 1) * P, :],
                vstash[:], deps=[tvc], key="v_in")
        prev["xfm_free"] = xfm_readers[-1]
    bufs.release(f"{tag}_wrow", f"{tag}_wv", f"{tag}_wkr", f"{tag}_wk",
                 f"{tag}_wqr", f"{tag}_wq")
    ofm = bufs.sbuf(f"{tag}_ofm", [P, NCH * TOK], BF16)

    tok_k = prev["qbf"]
    tok_v = prev["vstash"]
    tcc1 = ch.step("gpsimd", lambda nc: nc.gpsimd.collective_compute(
        "AllGather", ALU.bypass, replica_groups=CC_RG,
        ins=[k_in.opt()], outs=[k_full.opt()]), deps=[tok_k])
    tcc2 = ch.step("gpsimd", lambda nc: nc.gpsimd.collective_compute(
        "AllGather", ALU.bypass, replica_groups=CC_RG,
        ins=[v_in.opt()], outs=[v_full.opt()]), deps=[tok_v, tcc1])

    for rb in range(2):
        for seg in range(8):
            tkm = None
            for c4 in range(4):
                dl = ch.dma("sync", kdup[:, :512],
                            bass.AP(k_full.tensor,
                                    (rb * 512 + c4 * P) * TOK + seg * 512,
                                    [[TOK, P], [1, 512]]),
                            deps=[tcc1, prev.get("kdup")], key="kdup")
                tsq = ch.step("scalar", lambda nc: nc.scalar.activation(
                    out=sqb[:], in_=kdup[:, :512], func=AF.Square),
                    deps=[dl, prev.get("sqb")])
                prev["kdup"] = tsq
                tkm = ch.step("tensor", lambda nc, c4=c4: nc.tensor.matmul(
                    ps_o[:KVH, :], consts["hvk"][:, c4 * KVH:(c4 + 1) * KVH],
                    sqb[:], start=(c4 == 0), stop=(c4 == 3)),
                    deps=[tsq, prev.get("pson")])
                prev["sqb"] = tkm
            tkr = ch.step("vector", lambda nc, rb=rb, seg=seg:
                          nc.vector.tensor_reduce(
                              nrmk[:, rb * 8 + seg: rb * 8 + seg + 1],
                              ps_o[:KVH, :], AXX, ALU.max),
                          deps=[tkm, prev.get("nrmk")])
            prev["pson"] = tkr
            prev["nrmk"] = tkr
    t1 = ch.step("vector", lambda nc: nc.vector.tensor_reduce(
        nq[:], nrm[:], AXX, ALU.max), deps=[prev["nrm"]])
    t2 = ch.step("vector", lambda nc: nc.vector.tensor_reduce(
        nk[:], nrmk[:], AXX, ALU.max), deps=[prev["nrmk"], t1])
    t3 = ch.step("scalar", lambda nc: nc.scalar.activation(
        out=nq[:], in_=nq[:], func=AF.Sqrt), deps=[t1])
    t4 = ch.step("scalar", lambda nc: nc.scalar.activation(
        out=nk[:], in_=nk[:], func=AF.Sqrt), deps=[t2, t3])
    t5 = ch.step("vector", lambda nc: nc.vector.tensor_copy(
        nkb[:], nk[:]), deps=[t4])
    t6 = ch.step("tensor", lambda nc: nc.tensor.matmul(
        ps_o[:NH, 0:1], consts["sel16"][:], nkb[:], start=True, stop=True),
        deps=[t5, prev.get("pson")])
    t7 = ch.step("vector", lambda nc: nc.vector.tensor_copy(
        nk16[:], ps_o[:NH, 0:1]), deps=[t6])
    t8 = ch.step("vector", lambda nc: nc.vector.tensor_tensor(
        mrow[:], nq[:], nk16[:], ALU.mult), deps=[t7, t3])
    t9 = ch.step("vector", lambda nc: nc.vector.tensor_single_scalar(
        mrow[:], mrow[:], -1.0, ALU.mult), deps=[t8])
    t10 = ch.step("tensor", lambda nc: nc.tensor.transpose(
        ps_o[:1, :NH], mrow[:], identf[:NH, :NH]), deps=[t9, t7])
    t11 = ch.step("vector", lambda nc: nc.vector.tensor_copy(
        mrowT[:], ps_o[:1, :NH]), deps=[t10])
    t12 = ch.step("tensor", lambda nc: nc.tensor.matmul(
        ps_o[:, :NH], consts["ones1p"][:], mrowT[:], start=True, stop=True),
        deps=[t11])
    t13 = ch.step("vector", lambda nc: nc.vector.tensor_copy(
        mbias[:], ps_o[:, :NH]), deps=[t12])
    prev["pson"] = t13
    prev["kdup"] = t13

    NKP = 32
    for kv in range(KVH):
        kts = []
        for pi, pofs in enumerate((0, 64)):
            for rb in range(2):
                dst = kdup[pofs:pofs + 64, rb * 2048:(rb + 1) * 2048]
                kts.append(ch.dma(
                    "sync", dst.rearrange("p (j i) -> p j i", i=P),
                    bass.AP(k_full.tensor,
                            (rb * 512 + kv * HD) * TOK + pi * P,
                            [[TOK, 64], [256, 16], [1, P]]),
                    deps=[tcc1, prev.get("kdup")], key="kdup"))
        vl = ch.dma("sync", vsb[:].rearrange("p (c j) -> p c j", j=65),
                    bass.AP(v_full.tensor, kv * 65,
                            [[KVH * 65, P], [P * KVH * 65, 64], [1, 65]]),
                    deps=[tcc2, prev.get("vsb")], key="vsb")
        for qi in range(2):
            qh = 2 * kv + qi
            qts = [ch.dma("sync", qdup[0:64, :],
                          bass.AP(q_d.tensor, qh * HD * TOK,
                                  [[TOK, 64], [1, TOK]]),
                          deps=[prev.get("qdup")], key="qdup"),
                   ch.dma("sync", qdup[64:128, :],
                          bass.AP(q_d.tensor, qh * HD * TOK,
                                  [[TOK, 64], [1, TOK]]),
                          deps=[prev.get("qdup")], key="qdup")]
            for qb in range(NSUB):
                av_last = None
                exp_toks = [None, None]
                mm_toks = [None, None]
                for kp in range(NKP):
                    pb = ps_s[kp % 2]

                    def smm(nc, kp=kp, qb=qb, pb=pb):
                        nc.tensor.matmul(
                            pb[:, 0:512], kdup[0:64, kp * P:(kp + 1) * P],
                            qdup[0:64, qb * 512:(qb + 1) * 512],
                            start=True, stop=True)
                        return nc.tensor.matmul(
                            pb[:, 512:1024], kdup[64:128, kp * P:(kp + 1) * P],
                            qdup[64:128, qb * 512:(qb + 1) * 512],
                            start=True, stop=True)
                    tsm = ch.step("tensor", smm,
                                  deps=kts + qts + [exp_toks[kp % 2]])
                    mm_toks[kp % 2] = tsm
                    texp = ch.step("scalar", lambda nc, kp=kp, qh=qh, pb=pb:
                                   nc.scalar.activation(
                                       out=exps[kp % 2][:], in_=pb[:],
                                       func=AF.Exp,
                                       bias=mbias[:, qh:qh + 1]),
                                   deps=[tsm, av_last])
                    exp_toks[kp % 2] = texp

                    def avmm(nc, kp=kp, pb=pb):
                        nc.tensor.matmul(
                            ps_o[:65, :], vsb[:, (2 * kp) * 65:
                                              (2 * kp) * 65 + 65],
                            exps[kp % 2][:, 0:512],
                            start=(kp == 0), stop=False)
                        return nc.tensor.matmul(
                            ps_o[:65, :], vsb[:, (2 * kp + 1) * 65:
                                              (2 * kp + 1) * 65 + 65],
                            exps[kp % 2][:, 512:1024],
                            start=False, stop=(kp == NKP - 1))
                    av_last = ch.step("tensor", avmm,
                                      deps=[texp, vl, prev.get("ps_o_free")])
                tr1 = ch.step("vector", lambda nc: nc.vector.tensor_copy(
                    rec[:], ps_o[64:65, :]), deps=[av_last, prev.get("rec")])
                trec = emit_recip(ch, recr, rec, rtmp, 512, [tr1])
                prev["rec"] = trec
                tbc = ch.step("tensor", lambda nc: nc.tensor.matmul(
                    ps_c[:, :512], consts["ones1p"][:], recr[:],
                    start=True, stop=True), deps=[trec, prev.get("psc4")])
                tec = ch.step("vector", lambda nc: nc.vector.tensor_copy(
                    obc[:], ps_c[:, :512]), deps=[tbc, prev.get("obc")])
                prev["psc4"] = tec
                prev["recr"] = tec
                tof = ch.step("vector", lambda nc, qh=qh, qb=qb:
                              nc.vector.tensor_tensor(
                                  ofm[(qh % 2) * 64:(qh % 2) * 64 + 64,
                                      (qh // 2) * TOK + qb * 512:
                                      (qh // 2) * TOK + (qb + 1) * 512],
                                  ps_o[0:64, :], obc[0:64, :], ALU.mult),
                              deps=[tec, av_last,
                                    prev.get(("ofm", qh // 2))])
                prev["obc"] = tof
                prev["ps_o_free"] = tof
                prev[("ofm", qh // 2)] = tof
            prev["qdup"] = tsm
        prev["kdup"] = tsm
        prev["vsb"] = av_last

    ofm_toks = [prev[("ofm", c)] for c in range(NCH)]
    for qb in range(NSUB):
        ytoks = []
        for n in range(NCH):
            def omm(nc, n=n, qb=qb):
                i_ = None
                for c in range(NCH):
                    i_ = nc.tensor.matmul(
                        ps_c[:, :512],
                        wo_sb[:, c * DM + n * P: c * DM + (n + 1) * P],
                        ofm[:, c * TOK + qb * 512: c * TOK + (qb + 1) * 512],
                        start=(c == 0), stop=(c == NCH - 1))
                return i_
            tcm = ch.step("tensor", omm,
                          deps=ofm_toks + [prev.get("psc4")])
            ty = ch.step("vector", lambda nc, n=n: nc.vector.tensor_scalar(
                out=ybf[:, n * 512:(n + 1) * 512], in0=ps_c[:, :512],
                scalar1=consts["bo8"][:, n:n + 1], scalar2=None, op0=ALU.add),
                deps=[tcm, prev.get(("ybf", n))])
            prev["psc4"] = ty
            ytoks.append(ty)
        for sub in range(4):
            r0 = qb * 512 + sub * P
            dxr = ch.dma("sync", xrt[:], x_in[r0:r0 + P, :],
                         deps=[prev.get("xrt_free")], key="xrt")
            tlast = None
            for n in range(NCH):
                tt = ch.step("tensor", lambda nc, n=n, sub=sub:
                             nc.tensor.transpose(
                                 pst[:], ybf[:, n * 512 + sub * P:
                                             n * 512 + (sub + 1) * P],
                                 ident[:]), deps=[ytoks[n], prev.get("pst")])
                ta = ch.step("vector", lambda nc, n=n: nc.vector.tensor_tensor(
                    bass.AP(xrt, n * P, [[DM, P], [1, P]]),
                    bass.AP(xrt, n * P, [[DM, P], [1, P]]),
                    pst[:], ALU.add), deps=[tt, dxr, tlast])
                prev["pst"] = ta
                tlast = ta
                if sub == 3:
                    prev[("ybf", n)] = tt
            prev["xrt_free"] = ch.dma(
                "gpsimd", x_out[r0:r0 + P, :], xrt[:],
                deps=[tlast], key="xout")


B_AP_NAMES = [
    ("c_ident", [P, P], BF16), ("c_identf", [P, P], F32),
    ("w_disp", [DM, 256], F32), ("w_val", [DM, DM], F32),
    ("w_cproj", [DM, DM], F32),
    ("rel_pos_emb", [DELTA, BD], F32), ("b_strain16", [BD, 1], F32),
    ("b_damage_out", [1, 1], F32), ("ones_row", [1, P], F32),
    ("b_disp2", [P, 2], F32), ("b_val_bcast", [P, DM], F32),
    ("b_cproj8", [P, NCH], F32), ("b_state_t", [P, 1], F32),
    ("b_damage_t", [P, 1], F32), ("cmask_t0", [P, NH * DELTA], F32),
    ("invcnt_t0", [P, 640], F32), ("halo_mask", [P, P], F32),
    ("rep8", [BD, P], F32),
    ("w_state", [BD, BD], F32), ("w_strain", [BD, BD], F32),
    ("w_state_i", [BD, BD], F32), ("w_state_j", [BD, BD], F32),
    ("w_damage", [BD, BD], F32), ("w_pos", [BD, BD], F32),
    ("w_bond_out", [BD, 1], F32), ("w_damage_out16", [BD, 1], F32),
    ("bbo_t", [P, 1], F32),
]


def host_b_consts(inp, half):
    f = np.float32
    out = dict(
        c_ident=np.eye(P, dtype=ml_dtypes.bfloat16),
        c_identf=np.eye(P, dtype=f),
        w_disp=np.asarray(inp["L0_attn_w_disp"], f),
        w_val=np.asarray(inp["L0_attn_w_val"], f),
        w_cproj=np.asarray(inp["L0_attn_w_cproj"], f),
        rel_pos_emb=np.asarray(inp["L0_attn_rel_pos_emb"], f),
        b_strain16=np.asarray(inp["L0_attn_b_strain"], f).reshape(BD, 1),
        b_damage_out=np.asarray(inp["L0_attn_b_damage_out"], f).reshape(1, 1),
        ones_row=np.ones((1, P), f),
        b_disp2=np.ascontiguousarray(
            np.asarray(inp["L0_attn_b_disp"], f).reshape(2, P).T),
        b_val_bcast=np.ascontiguousarray(np.broadcast_to(
            np.asarray(inp["L0_attn_b_val"], f), (P, DM))),
        b_cproj8=np.ascontiguousarray(
            np.asarray(inp["L0_attn_b_cproj"], f).reshape(NCH, P).T),
        b_state_t=np.tile(np.asarray(inp["L0_attn_b_state"], f), NCH)
        .reshape(P, 1),
        b_damage_t=np.tile(np.asarray(inp["L0_attn_b_damage"], f), NCH)
        .reshape(P, 1),
        rep8=np.ascontiguousarray(
            np.tile(np.eye(BD, dtype=f), (1, NCH))),
        bbo_t=np.broadcast_to(
            np.asarray(inp["L0_attn_b_bond_out"], f).reshape(1, 1),
            (P, 1)).copy(),
        w_state=np.asarray(inp["L0_attn_w_state"], f),
        w_strain=np.asarray(inp["L0_attn_w_strain"], f),
        w_state_i=np.asarray(inp["L0_attn_w_state_i"], f),
        w_state_j=np.asarray(inp["L0_attn_w_state_j"], f),
        w_damage=np.asarray(inp["L0_attn_w_damage"], f),
        w_pos=np.asarray(inp["L0_attn_w_pos"], f),
        w_bond_out=np.asarray(inp["L0_attn_w_bond_out"], f).reshape(BD, 1),
        w_damage_out16=np.asarray(inp["L0_attn_w_damage_out"], f)
        .reshape(BD, 1),
    )
    if half == 0:
        cm = np.zeros((P, NH, DELTA), f)
        t = np.arange(P)[:, None]
        d = np.arange(DELTA)[None, :]
        cm[:, :, :] = np.where((t - 31 + d) < 0, -1e9, 0.0)[:, None, :]
        out["cmask_t0"] = cm.reshape(P, NH * DELTA)
        g = np.arange(640) - 128
        cnt = np.where(g < 0, 32, np.minimum(g + 1, 32)).astype(f)
        out["invcnt_t0"] = np.broadcast_to(1.0 / cnt, (P, 640)).copy()
        out["halo_mask"] = np.zeros((P, P), f)
    else:
        out["cmask_t0"] = np.zeros((P, NH * DELTA), f)
        out["invcnt_t0"] = np.full((P, 640), 1.0 / 32.0, f)
        out["halo_mask"] = np.ones((P, P), f)
    return out


def build_phaseB():
    nc = bass.Bass("TRN2", target_bir_lowering=False,
                   detect_race_conditions=False)
    x_in = nc.dram_tensor("x_in", [RFULL, DM], F32, kind="ExternalInput").ap()
    x_out = nc.dram_tensor("x_out", [TOK, DM], F32, kind="ExternalOutput").ap()
    aps = {}
    for nm, shp, dt in B_AP_NAMES:
        aps[nm] = nc.dram_tensor(nm, shp, dt, kind="ExternalInput").ap()
    with ExitStack() as ctx:
        bufs = Bufs(nc, ctx)
        ch = Chain(nc, ctx, "B")
        block = ctx.enter_context(nc.Block())
        aps2 = dict(aps)
        aps2["w_damage_out"] = aps["w_damage_out16"]
        pr = emit_l0attn_prep(ch, bufs, aps2)
        ch.fence()
        emit_l0attn_phase(ch, bufs, x_in, x_out, pr)
        finish(ch)
        ch.flush(block)
    return nc


def emit_l1ffn_prep(ch, bufs, aps, tag="Dp"):
    nc = ch.nc
    scr = {}
    for nm, nr, ncol in (("w1_w", L1H, DM), ("w2_w", L1H, DM),
                         ("h1_w", L1H, L1H), ("h2_w", L1H, L1H),
                         ("w3_w", DM, L1H)):
        scr[nm] = nc.dram_tensor(f"{tag}_{nm}T", [ncol, nr], BF16).ap()
    toks = []
    for nm, nr, ncol in (("w1_w", L1H, DM), ("w2_w", L1H, DM),
                         ("h1_w", L1H, L1H), ("h2_w", L1H, L1H),
                         ("w3_w", DM, L1H)):
        toks.append(emit_weight_prep(ch, bufs, aps[nm], scr[nm], nr, ncol,
                                     f"{tag}_{nm}", quant=False,
                                     transpose=True, deps=[]))
    sc = {}
    for nm, w_ in (("s1", 32), ("s2", 32), ("sh1", 32), ("sh2", 32)):
        sc[nm] = bufs.sbuf(f"{tag}_{nm}", [P, w_], F32)
        toks.append(ch.dma("sync", sc[nm][:], aps[nm]))
    s3c = bufs.sbuf(f"{tag}_s3c", [P, NCH], F32)
    sc["s3h"] = bufs.sbuf(f"{tag}_s3h", [P, NCH], F32)
    t3 = ch.dma("sync", s3c[:], aps["s3"])
    toks.append(ch.step("vector", lambda nc: nc.vector.tensor_single_scalar(
        sc["s3h"][:], s3c[:], 0.5, ALU.mult), deps=[t3]))
    return scr, sc, toks


def build_phaseD():
    nc = bass.Bass("TRN2", target_bir_lowering=False,
                   detect_race_conditions=False)
    x_in = nc.dram_tensor("x_in", [TOK, DM], F32, kind="ExternalInput").ap()
    x_out = nc.dram_tensor("x_out", [TOK, DM], F32, kind="ExternalOutput").ap()
    aps = {}
    for nm, shp in (("w1_w", [L1H, DM]), ("w2_w", [L1H, DM]),
                    ("h1_w", [L1H, L1H]), ("h2_w", [L1H, L1H]),
                    ("w3_w", [DM, L1H]), ("s1", [P, 32]), ("s2", [P, 32]),
                    ("sh1", [P, 32]), ("sh2", [P, 32]), ("s3", [P, NCH])):
        aps[nm] = nc.dram_tensor(nm, shp, F32, kind="ExternalInput").ap()
    c_ident = nc.dram_tensor("c_ident", [P, P], BF16, kind="ExternalInput").ap()
    h0_d = nc.dram_tensor("h0_scr", [L1H, TOK], BF16).ap()
    hA_d = nc.dram_tensor("hA_scr", [L1H, TOK], BF16).ap()
    hB_d = nc.dram_tensor("hB_scr", [L1H, TOK], BF16).ap()
    with ExitStack() as ctx:
        bufs = Bufs(nc, ctx)
        ch = Chain(nc, ctx, "D")
        ident = bufs.sbuf("sb_ident", [P, P], BF16)
        block = ctx.enter_context(nc.Block())
        ti = ch.dma("sync", ident[:], c_ident)
        scr, sc, toks = emit_l1ffn_prep(ch, bufs, aps)
        ch.wait_only("tensor", toks + [ti])
        ch.wait_only("sync", toks + [ti])
        emit_l1ffn_phase(ch, bufs, x_in, x_out, scr["w1_w"], scr["w2_w"],
                         scr["h1_w"], scr["h2_w"], scr["w3_w"], sc,
                         h0_d, hA_d, hB_d)
        finish(ch)
        ch.flush(block)
    return nc


def host_d_consts(inp):
    f = np.float32
    rs = lambda v, n: np.ascontiguousarray(
        np.asarray(v, f).reshape(n, P).T)
    return dict(
        w1_w=np.asarray(inp["L1_ffn_w1_w"], f),
        w2_w=np.asarray(inp["L1_ffn_w2_w"], f),
        h1_w=np.asarray(inp["L1_ffn_h1_w"], f),
        h2_w=np.asarray(inp["L1_ffn_h2_w"], f),
        w3_w=np.asarray(inp["L1_ffn_w3_w"], f),
        s1=rs(inp["L1_ffn_w1_s"], 32), s2=rs(inp["L1_ffn_w2_s"], 32),
        sh1=rs(inp["L1_ffn_h1_s"], 32), sh2=rs(inp["L1_ffn_h2_s"], 32),
        s3=rs(inp["L1_ffn_w3_s"], NCH),
        c_ident=np.eye(P, dtype=ml_dtypes.bfloat16),
    )


def build_phaseE():
    nc = bass.Bass("TRN2", target_bir_lowering=False,
                   detect_race_conditions=False)
    x_in = nc.dram_tensor("x_in", [TOK, DM], F32, kind="ExternalInput").ap()
    x_out = nc.dram_tensor("x_out", [TOK, DM], F32, kind="ExternalOutput").ap()
    aps = {}
    for nm, shp, dt in E_AP_NAMES:
        aps[nm] = nc.dram_tensor(nm, shp, dt, kind="ExternalInput").ap()
    c_ident = nc.dram_tensor("c_ident", [P, P], BF16, kind="ExternalInput").ap()
    c_identf = nc.dram_tensor("c_identf", [P, P], F32,
                              kind="ExternalInput").ap()
    with ExitStack() as ctx:
        bufs = Bufs(nc, ctx)
        ch = Chain(nc, ctx, "E")
        ident = bufs.sbuf("sb_ident", [P, P], BF16)
        identf = bufs.sbuf("sb_identf", [P, P], F32)
        block = ctx.enter_context(nc.Block())
        t1 = ch.dma("sync", ident[:], c_ident)
        t2 = ch.dma("sync", identf[:], c_identf)
        ch.wait_only("tensor", [t1, t2])
        ch.wait_only("vector", [t1, t2])
        emit_l1attn_phase(ch, bufs, x_in, x_out, aps)
        finish(ch)
        ch.flush(block)
    return nc


def build_phaseA_full(R):
    nc = bass.Bass("TRN2", target_bir_lowering=False,
                   detect_race_conditions=False)
    x_in = nc.dram_tensor("x_in", [R, DM], F32, kind="ExternalInput").ap()
    wg = nc.dram_tensor("wg", [L0I, DM], F32, kind="ExternalInput").ap()
    wu = nc.dram_tensor("wu", [L0I, DM], F32, kind="ExternalInput").ap()
    wd = nc.dram_tensor("wd", [DM, L0I], F32, kind="ExternalInput").ap()
    c_ident = nc.dram_tensor("c_ident", [P, P], BF16, kind="ExternalInput").ap()
    x_out = nc.dram_tensor("x_out", [R, DM], F32, kind="ExternalOutput").ap()
    wgT = nc.dram_tensor("wgT", [DM, L0I], BF16, kind="ExternalOutput").ap()
    wuT = nc.dram_tensor("wuT", [DM, L0I], BF16, kind="ExternalOutput").ap()
    wdT = nc.dram_tensor("wdT", [L0I, DM], BF16, kind="ExternalOutput").ap()
    xq_d = nc.dram_tensor("xq_scr", [DM, R], BF16).ap()
    hq_d = nc.dram_tensor("hq_scr", [L0I, R], BF16).ap()

    with ExitStack() as ctx:
        bufs = Bufs(nc, ctx)
        ch = Chain(nc, ctx, "A")
        ident = bufs.sbuf("sb_ident", [P, P], BF16)
        block = ctx.enter_context(nc.Block())
        ti = ch.dma("sync", ident[:], c_ident)
        t1 = emit_weight_prep(ch, bufs, wg, wgT, L0I, DM, "pg", quant=True,
                              transpose=True, deps=[ti])
        t2 = emit_weight_prep(ch, bufs, wu, wuT, L0I, DM, "pu", quant=True,
                              transpose=True, deps=[t1])
        t3 = emit_weight_prep(ch, bufs, wd, wdT, DM, L0I, "pd", quant=True,
                              transpose=True, deps=[t2])
        for e in ENGINES:
            ch.wait_only(e, [t1, t2, t3])
        emit_l0ffn_phase(ch, bufs, R, x_in, x_out, wgT, wuT, wdT,
                         xq_d, hq_d, "A")
        finish(ch)
        ch.flush(block)
    return nc


def build_fused():
    nc = bass.Bass("TRN2", target_bir_lowering=False,
                   detect_race_conditions=False)
    x_in = nc.dram_tensor("x_in", [RFULL, DM], F32, kind="ExternalInput").ap()
    x_out = nc.dram_tensor("x_out", [TOK, DM], F32, kind="ExternalOutput").ap()
    wg = nc.dram_tensor("wg", [L0I, DM], F32, kind="ExternalInput").ap()
    wu = nc.dram_tensor("wu", [L0I, DM], F32, kind="ExternalInput").ap()
    wd = nc.dram_tensor("wd", [DM, L0I], F32, kind="ExternalInput").ap()
    c_ident = nc.dram_tensor("c_ident", [P, P], BF16, kind="ExternalInput").ap()
    c_identf = nc.dram_tensor("c_identf", [P, P], F32,
                              kind="ExternalInput").ap()
    bap = {}
    for nm, shp, dt in B_AP_NAMES:
        if nm in ("c_ident", "c_identf"):
            continue
        bap[nm] = nc.dram_tensor(nm, shp, dt, kind="ExternalInput").ap()
    bap["c_ident"], bap["c_identf"] = c_ident, c_identf
    dap = {}
    for nm, shp in (("w1_w", [L1H, DM]), ("w2_w", [L1H, DM]),
                    ("h1_w", [L1H, L1H]), ("h2_w", [L1H, L1H]),
                    ("w3_w", [DM, L1H]), ("s1", [P, 32]), ("s2", [P, 32]),
                    ("sh1", [P, 32]), ("sh2", [P, 32]), ("s3", [P, NCH])):
        dap[nm] = nc.dram_tensor(nm, shp, F32, kind="ExternalInput").ap()
    eap = {}
    for nm, shp, dt in E_AP_NAMES:
        eap[nm] = nc.dram_tensor("e_" + nm, shp, dt, kind="ExternalInput").ap()

    wgT = nc.dram_tensor("wgT", [DM, L0I], BF16).ap()
    wuT = nc.dram_tensor("wuT", [DM, L0I], BF16).ap()
    wdT = nc.dram_tensor("wdT", [L0I, DM], BF16).ap()
    xq_a = nc.dram_tensor("xq_a", [DM, RFULL], BF16).ap()
    hq_a = nc.dram_tensor("hq_a", [L0I, RFULL], BF16).ap()
    xq_c = nc.dram_tensor("xq_c", [DM, TOK], BF16).ap()
    hq_c = nc.dram_tensor("hq_c", [L0I, TOK], BF16).ap()
    h0_d = nc.dram_tensor("h0_scr", [L1H, TOK], BF16).ap()
    hA_d = nc.dram_tensor("hA_scr", [L1H, TOK], BF16).ap()
    hB_d = nc.dram_tensor("hB_scr", [L1H, TOK], BF16).ap()
    x1_d = nc.dram_tensor("x1_d", [RFULL, DM], F32).ap()
    x2_d = nc.dram_tensor("x2_d", [TOK, DM], F32).ap()
    x3_d = nc.dram_tensor("x3_d", [TOK, DM], F32).ap()
    x4_d = nc.dram_tensor("x4_d", [TOK, DM], F32).ap()
    x5_d = nc.dram_tensor("x5_d", [TOK, DM], F32).ap()

    with ExitStack() as ctx:
        bufs = Bufs(nc, ctx)
        ch = Chain(nc, ctx, "M")
        ident = bufs.sbuf("sb_ident", [P, P], BF16)
        identf = bufs.sbuf("sb_identf", [P, P], F32)
        block = ctx.enter_context(nc.Block())
        t1 = ch.dma("sync", ident[:], c_ident)
        t2 = ch.dma("sync", identf[:], c_identf)
        for e in ENGINES:
            ch.wait_only(e, [t1, t2])
        bufs.push_scope()
        tp1 = emit_weight_prep(ch, bufs, wg, wgT, L0I, DM, "pg", quant=True,
                               transpose=True, deps=[])
        tp2 = emit_weight_prep(ch, bufs, wu, wuT, L0I, DM, "pu", quant=True,
                               transpose=True, deps=[tp1])
        emit_weight_prep(ch, bufs, wd, wdT, DM, L0I, "pd", quant=True,
                         transpose=True, deps=[tp2])
        bufs.pop_scope()
        ch.fence()
        scr, sc, _ = emit_l1ffn_prep(ch, bufs, dap)
        ch.fence()
        bufs.push_scope()
        emit_l0ffn_phase(ch, bufs, RFULL, x_in, x1_d, wgT, wuT, wdT,
                         xq_a, hq_a, "A")
        bufs.pop_scope()
        ch.fence()
        bufs.push_scope()
        bap2 = dict(bap)
        bap2["w_damage_out"] = bap["w_damage_out16"]
        pr = emit_l0attn_prep(ch, bufs, bap2)
        ch.fence()
        emit_l0attn_phase(ch, bufs, x1_d, x2_d, pr)
        bufs.pop_scope()
        ch.fence()
        bufs.push_scope()
        emit_l0ffn_phase(ch, bufs, TOK, x2_d, x3_d, wgT, wuT, wdT,
                         xq_c, hq_c, "C")
        bufs.pop_scope()
        ch.fence()
        bufs.push_scope()
        emit_l1ffn_phase(ch, bufs, x3_d, x4_d, scr["w1_w"], scr["w2_w"],
                         scr["h1_w"], scr["h2_w"], scr["w3_w"], sc,
                         h0_d, hA_d, hB_d, tag="D")
        bufs.pop_scope()
        ch.fence()
        bufs.push_scope()
        emit_l1attn_phase(ch, bufs, x4_d, x5_d, eap)
        bufs.pop_scope()
        ch.fence()
        bufs.push_scope()
        emit_l1ffn_phase(ch, bufs, x5_d, x_out, scr["w1_w"], scr["w2_w"],
                         scr["h1_w"], scr["h2_w"], scr["w3_w"], sc,
                         h0_d, hA_d, hB_d, tag="F")
        bufs.pop_scope()
        finish(ch)
        ch.flush(block)
    return nc


def kernel_fused(inputs):
    f = np.float32
    x0 = np.asarray(inputs["L0_attn_x"], f)
    ident = np.eye(P, dtype=ml_dtypes.bfloat16)
    identf = np.eye(P, dtype=f)
    cores = list(range(8))
    nc = _get("fused", build_fused)
    dconsts = host_d_consts(inputs)
    dconsts.pop("c_ident")
    ec = [host_e_consts(inputs, h) for h in range(2)]
    maps = []
    for cid in cores:
        b, h = cid // 2, cid % 2
        own = x0[b, h * TOK:(h + 1) * TOK]
        halo = (np.zeros((HALO, DM), f) if h == 0
                else x0[b, h * TOK - HALO:h * TOK])
        m = dict(x_in=np.concatenate([halo, own], 0),
                 wg=np.asarray(inputs["L0_ffn_w_gate"], f),
                 wu=np.asarray(inputs["L0_ffn_w_up"], f),
                 wd=np.asarray(inputs["L0_ffn_w_down"], f),
                 c_ident=ident, c_identf=identf)
        bc = host_b_consts(inputs, h)
        bc.pop("c_ident"); bc.pop("c_identf")
        m.update(bc)
        m.update(dconsts)
        m.update({"e_" + k: v for k, v in ec[h].items()})
        maps.append(m)
    res = run_bass_kernel_spmd(nc, maps, core_ids=cores)
    out = np.empty((4, 2 * TOK, DM), f)
    for cid in cores:
        b, h = cid // 2, cid % 2
        out[b, h * TOK:(h + 1) * TOK] = np.asarray(res.results[cid]["x_out"])
    return out


_BUILD_CACHE = {}


def _get(name, fn):
    if name not in _BUILD_CACHE:
        _BUILD_CACHE[name] = fn()
    return _BUILD_CACHE[name]


def kernel(**inputs):
    return kernel_fused(inputs)


def kernel_unfused(**inputs):
    f = np.float32
    x0 = np.asarray(inputs["L0_attn_x"], f)
    ident = np.eye(P, dtype=ml_dtypes.bfloat16)
    identf = np.eye(P, dtype=f)
    cores = list(range(8))
    wg = np.asarray(inputs["L0_ffn_w_gate"], f)
    wu = np.asarray(inputs["L0_ffn_w_up"], f)
    wd = np.asarray(inputs["L0_ffn_w_down"], f)

    def shard(cid):
        b, h = cid // 2, cid % 2
        own = x0[b, h * TOK:(h + 1) * TOK]
        halo = (np.zeros((HALO, DM), f) if h == 0
                else x0[b, h * TOK - HALO:h * TOK])
        return np.concatenate([halo, own], 0)

    ncA = _get("Afull", lambda: build_phaseA_full(RFULL))
    maps = [dict(x_in=shard(c), wg=wg, wu=wu, wd=wd, c_ident=ident)
            for c in cores]
    resA = run_bass_kernel_spmd(ncA, maps, core_ids=cores)
    x1 = [np.asarray(r["x_out"]) for r in resA.results]
    wT = [(np.asarray(r["wgT"]), np.asarray(r["wuT"]), np.asarray(r["wdT"]))
          for r in resA.results]

    ncB = _get("B", build_phaseB)
    maps = [dict(x_in=x1[c], **host_b_consts(inputs, c % 2)) for c in cores]
    resB = run_bass_kernel_spmd(ncB, maps, core_ids=cores)
    x2 = [np.asarray(r["x_out"]) for r in resB.results]

    ncC = _get("C", lambda: build_phaseA(TOK))
    maps = [dict(x_in=x2[c], wgT=wT[c][0], wuT=wT[c][1], wdT=wT[c][2],
                 c_ident=ident) for c in cores]
    resC = run_bass_kernel_spmd(ncC, maps, core_ids=cores)
    x3 = [np.asarray(r["x_out"]) for r in resC.results]

    ncD = _get("D", build_phaseD)
    dconsts = host_d_consts(inputs)
    maps = [dict(x_in=x3[c], **dconsts) for c in cores]
    resD = run_bass_kernel_spmd(ncD, maps, core_ids=cores)
    x4 = [np.asarray(r["x_out"]) for r in resD.results]

    ncE = _get("E", build_phaseE)
    ec = [host_e_consts(inputs, h) for h in range(2)]
    maps = [dict(x_in=x4[c], **ec[c % 2], c_ident=ident, c_identf=identf)
            for c in cores]
    resE = run_bass_kernel_spmd(ncE, maps, core_ids=cores)
    x5 = [np.asarray(r["x_out"]) for r in resE.results]

    maps = [dict(x_in=x5[c], **dconsts) for c in cores]
    resF = run_bass_kernel_spmd(ncD, maps, core_ids=cores)
    x6 = [np.asarray(r["x_out"]) for r in resF.results]

    out = np.empty((4, 2 * TOK, DM), f)
    for c in cores:
        b, h = c // 2, c % 2
        out[b, h * TOK:(h + 1) * TOK] = x6[c]
    return out
